# version 02
# baseline (speedup 1.0000x reference)
import numpy as np
import ml_dtypes
from contextlib import ExitStack

import concourse.bass as bass
import concourse.mybir as mybir
from concourse.bass_utils import run_bass_kernel_spmd
from concourse.aot_env import allow_ici_all_to_all

F32 = mybir.dt.float32
BF16 = mybir.dt.bfloat16
FP8 = mybir.dt.float8e4
AF = mybir.ActivationFunctionType
ALU = mybir.AluOpType
AX = mybir.AxisListType

N_CORES = 8
B, S, E = 4, 2048, 4096
T_ALL = B * S
T_OWN = T_ALL // N_CORES
NTT_OWN = T_OWN // 128
NEC = E // 128
HA_D = 128
ROT = 64
NUM_REMAIN = 410
NEG = -3.0e8
LN_EPS = 1e-5
RANK = 32
BF = ml_dtypes.bfloat16
_PHASE_LIMIT = None
_DEBUG_DUMPS = []


def _ppf(p):
    p = np.asarray(p, np.float64)
    a = [-3.969683028665376e+01, 2.209460984245205e+02, -2.759285104469687e+02,
         1.383577518672690e+02, -3.066479806614716e+01, 2.506628277459239e+00]
    b = [-5.447609879822406e+01, 1.615858368580409e+02, -1.556989798598866e+02,
         6.680131188771972e+01, -1.328068155288572e+01]
    c = [-7.784894002430293e-03, -3.223964580411365e-01, -2.400758277161838e+00,
         -2.549732539343734e+00, 4.374664141464968e+00, 2.938163982698783e+00]
    d = [7.784695709041462e-03, 3.224671290700398e-01, 2.445134137142996e+00,
         3.754408661907416e+00]
    out = np.empty_like(p)
    lo = p < 0.02425
    hi = p > 1 - 0.02425
    mid = ~(lo | hi)
    q = np.sqrt(-2 * np.log(np.where(lo, p, 0.5)))
    out_lo = (((((c[0] * q + c[1]) * q + c[2]) * q + c[3]) * q + c[4]) * q + c[5]) / \
             ((((d[0] * q + d[1]) * q + d[2]) * q + d[3]) * q + 1)
    q = p - 0.5
    r = q * q
    out_mid = (((((a[0] * r + a[1]) * r + a[2]) * r + a[3]) * r + a[4]) * r + a[5]) * q / \
              (((((b[0] * r + b[1]) * r + b[2]) * r + b[3]) * r + b[4]) * r + 1)
    q = np.sqrt(-2 * np.log1p(-np.where(hi, p, 0.5)))
    out_hi = -(((((c[0] * q + c[1]) * q + c[2]) * q + c[3]) * q + c[4]) * q + c[5]) / \
             ((((d[0] * q + d[1]) * q + d[2]) * q + d[3]) * q + 1)
    out[lo] = out_lo[lo]
    out[mid] = out_mid[mid]
    out[hi] = out_hi[hi]
    return out


def make_constants():
    cst = {}
    cst["ident_bf"] = np.eye(128).astype(BF)
    tri = np.tril(np.ones((128, 128), np.float32))
    cst["causal01"] = tri
    cst["causal_add"] = np.where(tri > 0, 0.0, NEG).astype(np.float32)
    inv = 1.0 / 10000.0 ** (np.arange(0, HA_D, 2, dtype=np.float64) / HA_D)
    t = np.arange(S, dtype=np.float64)
    emb = np.concatenate([np.outer(t, inv)] * 2, -1)
    cst["cosA"] = np.ascontiguousarray(
        np.cos(emb).astype(np.float32).reshape(16, 128, HA_D).transpose(1, 0, 2).reshape(128, 16 * HA_D))
    cst["sinA"] = np.ascontiguousarray(
        np.sin(emb).astype(np.float32).reshape(16, 128, HA_D).transpose(1, 0, 2).reshape(128, 16 * HA_D))
    invb = 1.0 / 10000.0 ** (np.arange(0, ROT, 2, dtype=np.float64) / ROT)
    sinu = np.outer(t, invb)
    cos2 = np.repeat(np.cos(sinu), 2, axis=1).astype(np.float32)
    sin2 = np.repeat(np.sin(sinu), 2, axis=1).astype(np.float32)
    cst["cosB"] = np.ascontiguousarray(cos2.reshape(16, 128, ROT).transpose(1, 0, 2).reshape(128, 16 * ROT))
    cst["sinB"] = np.ascontiguousarray(sin2.reshape(16, 128, ROT).transpose(1, 0, 2).reshape(128, 16 * ROT))
    n = np.arange(S, dtype=np.float64) + 1.0
    z = np.full(S, -25.0)
    big = n > NUM_REMAIN
    z[big] = _ppf(1.0 - NUM_REMAIN / n[big])
    cst["ztab"] = np.ascontiguousarray(z.reshape(16, 128).T.astype(np.float32))
    cst["invn"] = np.ascontiguousarray((1.0 / n).reshape(16, 128).T.astype(np.float32))
    return cst


CST = make_constants()
ENGINES = ["tensor", "vector", "scalar", "gpsimd", "sync"]


class Sem:
    def __init__(self, handle):
        self.h = handle
        self.n = 0


class Prog:
    def __init__(self, nc, stack):
        self.nc = nc
        self.ops = {e: [] for e in ENGINES}
        self.stack = stack
        self.sems = {}

    def sem(self, name):
        if name not in self.sems:
            self.sems[name] = Sem(self.stack.enter_context(self.nc.semaphore(name)))
        return self.sems[name]

    def op(self, eng, fn, inc=None, inc_n=1):
        if inc is not None:
            inc.n += inc_n
            self.ops[eng].append(lambda e, f=fn, s=inc.h, k=inc_n: f(e).then_inc(s, k))
        else:
            self.ops[eng].append(fn)

    def dma(self, eng, dst, src, sem):
        self.op(eng, lambda e, a=dst, b=src: e.dma_start(out=a, in_=b), inc=sem, inc_n=16)

    def wait(self, eng, sem, val=None):
        v = sem.n if val is None else val
        self.ops[eng].append(lambda e, s=sem.h, vv=v: e.wait_ge(s, vv))

    def barrier(self, name, engines=ENGINES):
        bar = self.sem("bar")
        for eng in engines:
            self.op(eng, lambda e: e.nop(), inc=bar)
        tgt = bar.n
        for eng in engines:
            self.wait(eng, bar, tgt)

    def run(self):
        with self.nc.Block() as block:
            for eng in ENGINES:
                def make(eng_name):
                    def body(e):
                        for fn in self.ops[eng_name]:
                            fn(e)
                    return body
                getattr(block, eng)(make(eng))


def build_nc():
    nc = bass.Bass("TRN2", target_bir_lowering=False, debug=False)
    d = {}

    def di(name, shape, dtype=F32):
        d[name] = nc.dram_tensor(name, list(shape), dtype, kind="ExternalInput").ap()
        return d[name]

    di("x_own", (T_OWN, E))
    di("wqkva", (E, 768))
    di("hp1", (128, 4 * 128)); di("hb1", (128, 4))
    di("hp2", (128, 4 * 128)); di("hb2", (128, 4))
    di("woaT", (E, E))
    di("wqkvb", (E, 1536))
    di("wob", (E, E))
    di("fker", (E, E)); di("fbias", (1, E))
    di("gate", (1, 16))
    di("gAT", (128, 16 * NEC * RANK))
    di("gBT", (128, 16 * 1024))
    di("asT", (128, NEC * RANK))
    di("bsT", (RANK, E))
    di("ident_bf", (128, 128), BF16)
    di("causal01", (128, 128))
    di("causal_add", (128, 128))
    di("cosA", (128, 16 * 128)); di("sinA", (128, 16 * 128))
    di("cosB", (128, 16 * ROT)); di("sinB", (128, 16 * ROT))
    di("ztab", (128, 16)); di("invn", (128, 16))
    y = nc.dram_tensor("y", [T_OWN, E], F32, kind="ExternalOutput").ap()

    ag_in = nc.dram_tensor("ag_in", [E, T_OWN], BF16).ap()
    ag_out = nc.dram_tensor("ag_out", [N_CORES * E, T_OWN], BF16).ap()
    a2a_in = nc.dram_tensor("a2a_in", [N_CORES * 512, T_OWN], BF16).ap()
    a2a_out = nc.dram_tensor("a2a_out", [N_CORES * 512, T_OWN], BF16).ap()
    xr = [nc.dram_tensor(f"xr{i}", [T_OWN, E], F32).ap() for i in range(3)]
    qa_d = nc.dram_tensor("qa_d", [512, T_ALL], BF16).ap()
    qha_d = nc.dram_tensor("qha_d", [512, T_ALL], FP8).ap()
    ka_d = nc.dram_tensor("ka_d", [128, T_ALL], BF16).ap()
    kha_d = nc.dram_tensor("kha_d", [512, T_ALL], FP8).ap()
    va_d = nc.dram_tensor("va_d", [T_ALL // 128, 128, 128], BF16).ap()
    qb_d = nc.dram_tensor("qb_d", [512, T_ALL], BF16).ap()
    kb_d = nc.dram_tensor("kb_d", [512, T_ALL], BF16).ap()
    vb_d = nc.dram_tensor("vb_d", [T_ALL // 128, 128, 512], BF16).ap()
    beff_d = nc.dram_tensor("beff_d", [128, RANK * E // 128], F32).ap()
    xlndbg = nc.dram_tensor("xlndbg", [T_OWN, E], BF16).ap()
    wdbg = nc.dram_tensor("wdbg", [128, NEC * 256], BF16).ap()

    stack = ExitStack()
    P = Prog(nc, stack)
    e_ = stack.enter_context

    ident_bf = e_(nc.sbuf_tensor("c_ident", [128, 128], BF16))
    causal01 = e_(nc.sbuf_tensor("c_c01", [128, 128], F32))
    causal_add = e_(nc.sbuf_tensor("c_cadd", [128, 128], F32))
    cosA = e_(nc.sbuf_tensor("c_cosA", [128, 16 * 128], F32))
    sinA = e_(nc.sbuf_tensor("c_sinA", [128, 16 * 128], F32))
    cosB = e_(nc.sbuf_tensor("c_cosB", [128, 16 * ROT], F32))
    sinB = e_(nc.sbuf_tensor("c_sinB", [128, 16 * ROT], F32))
    ztab = e_(nc.sbuf_tensor("c_ztab", [128, 16], F32))
    invn = e_(nc.sbuf_tensor("c_invn", [128, 16], F32))
    ones_bf = e_(nc.sbuf_tensor("c_onesbf", [1, 128], BF16))
    ones_f = e_(nc.sbuf_tensor("c_onesf", [1, 128], F32))
    gate_row = e_(nc.sbuf_tensor("c_gater", [1, 16], F32))
    gbc = e_(nc.sbuf_tensor("c_gbc", [128, 16], F32))
    fbias_sb = e_(nc.sbuf_tensor("c_fbias", [1, E], BF16))
    hb1_sb = e_(nc.sbuf_tensor("c_hb1", [128, 4], F32))
    hb2_sb = e_(nc.sbuf_tensor("c_hb2", [128, 4], F32))
    hp1_sb = e_(nc.sbuf_tensor("c_hp1", [128, 4 * 128], BF16))
    hp2_sb = e_(nc.sbuf_tensor("c_hp2", [128, 4 * 128], BF16))
    abT_sb = e_(nc.sbuf_tensor("c_abT", [128, NEC * 64], BF16))
    bbT_sb = e_(nc.sbuf_tensor("c_bbT", [64, E], BF16))

    pb = [e_(nc.psum_tensor(f"pb{i}", [128, 512], F32)) for i in range(6)]

    sW = P.sem("sW"); sX = P.sem("sX"); sSt = P.sem("sSt")
    sWp = [P.sem("sW0"), P.sem("sW1")]
    sXp = [P.sem("sX0"), P.sem("sX1")]
    sPE = P.sem("sPE"); sDV = P.sem("sDV"); sAC = P.sem("sAC")
    sCC = P.sem("sCC"); sSc = P.sem("sSc")

    P.dma("sync", ident_bf[:], d["ident_bf"][:], sX)
    P.dma("sync", causal01[:], d["causal01"][:], sX)
    P.dma("sync", causal_add[:], d["causal_add"][:], sX)
    P.dma("sync", cosA[:], d["cosA"][:], sX)
    P.dma("sync", sinA[:], d["sinA"][:], sX)
    P.dma("sync", cosB[:], d["cosB"][:], sX)
    P.dma("sync", sinB[:], d["sinB"][:], sX)
    P.dma("sync", ztab[:], d["ztab"][:], sX)
    P.dma("sync", invn[:], d["invn"][:], sX)
    P.dma("sync", gate_row[:], d["gate"][:], sX)
    P.dma("sync", hb1_sb[:, :], d["hb1"][:], sX)
    P.dma("sync", hb2_sb[:, :], d["hb2"][:], sX)
    P.dma("gpsimd", fbias_sb[:], d["fbias"][:], sW)
    P.dma("gpsimd", hp1_sb[:], d["hp1"][:], sW)
    P.dma("gpsimd", hp2_sb[:], d["hp2"][:], sW)
    P.op("gpsimd", lambda e: e.memset(ones_bf[:], 1.0), inc=sW, inc_n=1)
    P.op("gpsimd", lambda e: e.memset(ones_f[:], 1.0), inc=sW, inc_n=1)

    with ExitStack() as ph:
        lA = ph.enter_context(nc.sbuf_tensor("l_A", [128, 16 * 1024], BF16))
        acc = [ph.enter_context(nc.sbuf_tensor(f"l_acc{i}", [128, 1024], F32)) for i in range(2)]
        asb = ph.enter_context(nc.sbuf_tensor("l_as", [128, 1024], BF16))
        P.dma("gpsimd", lA[:], d["gAT"][:], sW)
        P.dma("gpsimd", asb[:], d["asT"][:], sW)
        P.wait("tensor", sX, None)
        P.wait("tensor", sW, None)
        P.op("tensor", lambda e: e.matmul(pb[0][:, 0:16], ones_f[:, :], gate_row[:, :], start=True, stop=True), inc=sPE)
        P.wait("vector", sPE, None)
        P.op("vector", lambda e: e.tensor_copy(gbc[:], pb[0][:, 0:16]), inc=sDV)
        P.wait("vector", sW, None)
        P.op("vector", lambda e: e.tensor_scalar(out=acc[0][:], in0=lA[:, 0:1024], scalar1=gbc[:, 0:1], scalar2=None, op0=ALU.mult))
        for n in range(1, 16):
            P.op("vector", lambda e, n2=n: e.scalar_tensor_tensor(
                out=acc[0][:], in0=lA[:, n2 * 1024:(n2 + 1) * 1024], scalar=gbc[:, n2:n2 + 1],
                in1=acc[0][:], op0=ALU.mult, op1=ALU.add))
        P.op("vector", lambda e: e.tensor_copy(
            abT_sb[:].rearrange("p (c w) -> p c w", w=64)[:, :, 0:32],
            acc[0][:].rearrange("p (c r) -> p c r", r=32)))
        P.op("vector", lambda e: e.tensor_copy(
            abT_sb[:].rearrange("p (c w) -> p c w", w=64)[:, :, 32:64],
            asb[:].rearrange("p (c r) -> p c r", r=32)), inc=sDV)
        P.wait("gpsimd", sDV, None)
        P.dma("gpsimd", lA[:], d["gBT"][:], sW)
        P.wait("vector", sW, None)
        P.op("vector", lambda e: e.tensor_scalar(out=acc[1][:], in0=lA[:, 0:1024], scalar1=gbc[:, 0:1], scalar2=None, op0=ALU.mult))
        for n in range(1, 16):
            P.op("vector", lambda e, n2=n: e.scalar_tensor_tensor(
                out=acc[1][:], in0=lA[:, n2 * 1024:(n2 + 1) * 1024], scalar=gbc[:, n2:n2 + 1],
                in1=acc[1][:], op0=ALU.mult, op1=ALU.add), inc=(sDV if n == 15 else None))
        P.wait("sync", sDV, None)
        P.dma("sync", beff_d[:], acc[1][:], sSt)
        P.wait("gpsimd", sSt, None)
        P.dma("gpsimd", bbT_sb[0:RANK, :], beff_d.rearrange("(r s) w -> r (s w)", s=4), sW)
        P.dma("gpsimd", bbT_sb[RANK:2 * RANK, :], d["bsT"][:], sW)
        P.barrier("prologue")

    def ln_phase(src_dram, name, to_agin, afm=None):
        with ExitStack() as ph:
            xf32 = [ph.enter_context(nc.sbuf_tensor(f"ln{name}_x{i}", [128, E], F32)) for i in range(2)]
            lnst = ph.enter_context(nc.sbuf_tensor("ln_st" + name, [128, 48], F32))
            lnagg = ph.enter_context(nc.sbuf_tensor("ln_agg" + name, [128, 2], F32))
            lnmu = ph.enter_context(nc.sbuf_tensor("ln_mu" + name, [128, 1], F32))
            lnr = ph.enter_context(nc.sbuf_tensor("ln_r" + name, [128, 1], F32))
            lntmp = ph.enter_context(nc.sbuf_tensor("ln_tmp" + name, [128, 1], F32))
            xln = ph.enter_context(nc.sbuf_tensor("ln_xb" + name, [128, E], BF16))
            tevac = [ph.enter_context(nc.sbuf_tensor(f"ln{name}_tv{i}", [128, 512], BF16)) for i in range(2)]
            pt8 = [ph.enter_context(nc.psum_tensor(f"ln{name}_pt{i}", [128, 1024], BF16)) for i in range(2)]
            lnu = P.sem("lnu_" + name)
            for tt in range(NTT_OWN):
                buf = xf32[tt % 2]
                if tt >= 2:
                    P.wait("sync", lnu, tt - 1)
                P.dma("sync", buf[:], src_dram[tt * 128:(tt + 1) * 128, :], sXp[tt % 2])
                P.wait("vector", sXp[tt % 2], None)
                for c8 in range(8):
                    P.op("vector", lambda e, b2=buf, c=c8: e.bn_stats(
                        lnst[:, c * 6:(c + 1) * 6], b2[:, c * 512:(c + 1) * 512]),
                        inc=(sDV if c8 == 7 else None))
                P.wait("vector", sDV, None)
                P.op("vector", lambda e: e.bn_aggr(lnagg[:], lnst[:]), inc=sDV)
                P.wait("vector", sDV, None)
                P.op("vector", lambda e: e.tensor_copy(lnmu[:], lnagg[:, 0:1]))
                P.op("vector", lambda e: e.tensor_scalar(
                    out=lnr[:], in0=lnagg[:, 1:2], scalar1=LN_EPS, scalar2=None, op0=ALU.add), inc=sDV)
                P.wait("vector", sDV, None)
                P.wait("scalar", sDV, None)
                P.op("scalar", lambda e: e.activation(lntmp[:], lnr[:], AF.Ln), inc=sAC)
                P.op("scalar", lambda e: e.activation(lnr[:], lntmp[:], AF.Exp, scale=-0.5), inc=sAC)
                P.wait("vector", sAC, None)
                P.wait("vector", sPE, None)
                P.op("vector", lambda e, b2=buf: e.tensor_scalar(
                    out=xln[:], in0=b2[:], scalar1=lnmu[:], scalar2=lnr[:],
                    op0=ALU.subtract, op1=ALU.mult), inc=sDV)
                P.op("vector", lambda e: e.nop(), inc=lnu)
                P.wait("tensor", sDV, None)
                for g in range(4):
                    bank = pt8[g % 2]
                    P.wait("tensor", sAC, None)
                    for k in range(8):
                        ec = g * 8 + k
                        P.op("tensor", lambda e, bk=bank, c=ec, k2=k: e.transpose(
                            bk[:, k2 * 128:(k2 + 1) * 128], xln[:, c * 128:(c + 1) * 128], ident_bf[:]),
                            inc=(sPE if k == 7 else None))
                    P.wait("scalar", sPE, None)
                    if to_agin:
                        P.wait("scalar", sSt, None)
                        for half in range(2):
                            P.op("scalar", lambda e, bk=bank, h=half: e.copy(
                                tevac[h][:], bk[:, h * 512:(h + 1) * 512]), inc=sAC)
                        P.wait("sync", sAC, None)
                        for half in range(2):
                            ec0 = g * 8 + half * 4
                            P.dma("sync",
                                  ag_in[ec0 * 128:(ec0 + 4) * 128, tt * 128:(tt + 1) * 128].rearrange("(c p) t -> p c t", p=128),
                                  tevac[half][:].rearrange("p (c t) -> p c t", t=128), sSt)
                    else:
                        for half in range(2):
                            ec0 = g * 8 + half * 4
                            dst = afm[:].rearrange("p (c t) -> p c t", t=T_OWN)[:, ec0:ec0 + 4, tt * 128:(tt + 1) * 128]
                            P.op("scalar", lambda e, bk=bank, h=half, dd=dst: e.copy(
                                dd, bk[:, h * 512:(h + 1) * 512]), inc=sAC)
            P.barrier("ln_" + name)

    def allgather():
        P.wait("gpsimd", sSt, None)
        P.op("gpsimd", lambda e: e.collective_compute(
            "AllGather", ALU.bypass, replica_groups=[list(range(N_CORES))],
            ins=[ag_in.opt()], outs=[ag_out.opt()]), inc=sCC)
        for eng in ENGINES:
            P.wait(eng, sCC, None)

    def alltoall():
        P.wait("gpsimd", sSt, None)
        def go(e):
            with allow_ici_all_to_all():
                return e.collective_compute(
                    "AllToAll", ALU.bypass, replica_groups=[list(range(N_CORES))],
                    ins=[a2a_in.opt()], outs=[a2a_out.opt()])
        P.op("gpsimd", go, inc=sCC)
        for eng in ENGINES:
            P.wait(eng, sCC, None)

    def rope_a(sc, dst, psrc, pc, sbase, dbase):
        cA = cosA[:, pc * 128:(pc + 1) * 128]
        sA = sinA[:, pc * 128:(pc + 1) * 128]
        rt = sc["ropet"]
        P.op("vector", lambda e: e.tensor_tensor(rt[:, 0:128], psrc[:, sbase:sbase + 128], cA, ALU.mult))
        P.op("vector", lambda e: e.tensor_tensor(rt[:, 128:192], psrc[:, sbase + 64:sbase + 128], sA[:, 0:64], ALU.mult))
        P.op("vector", lambda e: e.tensor_tensor(rt[:, 192:256], psrc[:, sbase:sbase + 64], sA[:, 64:128], ALU.mult))
        P.op("vector", lambda e: e.tensor_sub(dst[:, dbase:dbase + 64], rt[:, 0:64], rt[:, 128:192]))
        P.op("vector", lambda e: e.tensor_add(dst[:, dbase + 64:dbase + 128], rt[:, 64:128], rt[:, 192:256]))

    def hash_head(sc, h, src_fm, dst):
        hsil, hy, hz = sc["hsil"], sc["hy"], sc["hz"]
        P.op("tensor", lambda e: e.matmul(pb[4][:], hp1_sb[:, h * 128:(h + 1) * 128], src_fm, start=True, stop=True), inc=sPE)
        P.wait("scalar", sPE, None)
        P.op("scalar", lambda e: e.activation(hsil[:], pb[4][:], AF.Silu, bias=hb1_sb[:, h:h + 1]), inc=sAC)
        P.wait("vector", sAC, None)
        P.op("vector", lambda e: e.tensor_tensor(hy[:], hsil[:], src_fm, ALU.add), inc=sDV)
        P.wait("tensor", sDV, None)
        P.op("tensor", lambda e: e.matmul(pb[5][:], hp2_sb[:, h * 128:(h + 1) * 128], hy[:], start=True, stop=True), inc=sPE)
        P.wait("vector", sPE, None)
        P.op("vector", lambda e: e.scalar_tensor_tensor(
            out=hz[:], in0=pb[5][:], scalar=hb2_sb[:, h:h + 1], in1=hy[:],
            op0=ALU.add, op1=ALU.add), inc=sDV)
        P.wait("scalar", sDV, None)
        P.op("scalar", lambda e: e.activation(dst, hz[:], AF.Sign), inc=sAC)

    def proj_a():
        with ExitStack() as ph:
            wsb = ph.enter_context(nc.sbuf_tensor("pa_w", [128, NEC * 768], BF16))
            xsl = [ph.enter_context(nc.sbuf_tensor(f"pa_x{i}", [128, NEC * 512], BF16)) for i in range(2)]
            sc = dict(
                qr=ph.enter_context(nc.sbuf_tensor("pa_qr", [128, 512], BF16)),
                kr=ph.enter_context(nc.sbuf_tensor("pa_kr", [128, 128], BF16)),
                ropet=ph.enter_context(nc.sbuf_tensor("pa_rt", [128, 256], F32)),
                qT=ph.enter_context(nc.sbuf_tensor("pa_qT", [128, 4 * 512], BF16)),
                kT=ph.enter_context(nc.sbuf_tensor("pa_kT", [128, 512], BF16)),
                v=ph.enter_context(nc.sbuf_tensor("pa_v", [128, 128], BF16)),
                hsil=ph.enter_context(nc.sbuf_tensor("pa_hs", [128, 512], F32)),
                hy=ph.enter_context(nc.sbuf_tensor("pa_hy", [128, 512], BF16)),
                hz=ph.enter_context(nc.sbuf_tensor("pa_hz", [128, 512], F32)),
                qh=ph.enter_context(nc.sbuf_tensor("pa_qh", [128, 4 * 512], FP8)),
                kh=ph.enter_context(nc.sbuf_tensor("pa_kh", [128, 4 * 512], FP8)),
            )
            pt8 = [ph.enter_context(nc.psum_tensor(f"pa_pt{i}", [128, 1024], BF16)) for i in range(2)]
            for ec in range(NEC):
                P.dma("gpsimd", wsb[:, ec * 768:(ec + 1) * 768], d["wqkva"][ec * 128:(ec + 1) * 128, :], sW)
            evt = P.sem("pa_evt")
            for hs in range(16):
                buf = xsl[hs % 2]
                r, half = hs // 2, hs % 2
                if hs >= 2:
                    P.wait("sync", sPE, None)
                for ec in range(NEC):
                    P.dma("sync", buf[:, ec * 512:(ec + 1) * 512],
                          ag_out[r * E + ec * 128:r * E + (ec + 1) * 128, half * 512:(half + 1) * 512], sXp[hs % 2])
                P.wait("tensor", sXp[hs % 2], None)
                P.wait("tensor", sW, None)
                for tt4 in range(4):
                    kglob = hs * 4 + tt4
                    pc = kglob % 16
                    bq = pb[tt4 % 2]
                    bkv = pb[2 + tt4 % 2]
                    if kglob >= 2:
                        P.wait("tensor", evt, kglob - 1)
                    for ec in range(NEC):
                        xt = buf[:, ec * 512 + tt4 * 128: ec * 512 + (tt4 + 1) * 128]
                        P.op("tensor", lambda e, bk=bq, x=xt, ec2=ec: e.matmul(
                            bk[:], x, wsb[:, ec2 * 768:ec2 * 768 + 512],
                            start=(ec2 == 0), stop=(ec2 == NEC - 1)))
                    for ec in range(NEC):
                        xt = buf[:, ec * 512 + tt4 * 128: ec * 512 + (tt4 + 1) * 128]
                        P.op("tensor", lambda e, bk=bkv, x=xt, ec2=ec: e.matmul(
                            bk[:, 0:256], x, wsb[:, ec2 * 768 + 512:(ec2 + 1) * 768],
                            start=(ec2 == 0), stop=(ec2 == NEC - 1)), inc=(sPE if ec == NEC - 1 else None))
                    P.wait("vector", sPE, None)
                    P.wait("vector", sSt, None)
                    for h in range(4):
                        rope_a(sc, sc["qr"], bq, pc, h * 128, h * 128)
                    rope_a(sc, sc["kr"], bkv, pc, 0, 0)
                    P.op("vector", lambda e, bk=bkv: e.tensor_copy(sc["v"][:], bk[:, 128:256]), inc=sDV)
                    P.wait("tensor", sDV, None)
                    bank = pt8[tt4 % 2]
                    for h in range(4):
                        P.op("tensor", lambda e, bk=bank, h2=h: e.transpose(
                            bk[:, h2 * 128:(h2 + 1) * 128], sc["qr"][:, h2 * 128:(h2 + 1) * 128], ident_bf[:]))
                    P.op("tensor", lambda e, bk=bank: e.transpose(bk[:, 512:640], sc["kr"][:, 0:128], ident_bf[:]), inc=sPE)
                    P.wait("scalar", sPE, None)
                    for h in range(4):
                        P.op("scalar", lambda e, bk=bank, h2=h, t4=tt4: e.copy(
                            sc["qT"][:, h2 * 512 + t4 * 128: h2 * 512 + (t4 + 1) * 128], bk[:, h2 * 128:(h2 + 1) * 128]))
                    P.op("scalar", lambda e, bk=bank, t4=tt4: e.copy(
                        sc["kT"][:, t4 * 128:(t4 + 1) * 128], bk[:, 512:640]), inc=sAC)
                    P.wait("sync", sDV, None)
                    P.dma("sync", va_d[kglob, :, :], sc["v"][:], sSt)
                    P.op("scalar", lambda e: e.nop(), inc=evt)
                P.wait("tensor", sAC, None)
                for h in range(4):
                    hash_head(sc, h, sc["qT"][:, h * 512:(h + 1) * 512], sc["qh"][:, h * 512:(h + 1) * 512])
                for h in range(4):
                    hash_head(sc, h, sc["kT"][:, 0:512], sc["kh"][:, h * 512:(h + 1) * 512])
                P.wait("sync", sAC, None)
                t0 = hs * 512
                for h in range(4):
                    P.dma("sync", qa_d[h * 128:(h + 1) * 128, t0:t0 + 512], sc["qT"][:, h * 512:(h + 1) * 512], sSt)
                    P.dma("sync", qha_d[h * 128:(h + 1) * 128, t0:t0 + 512], sc["qh"][:, h * 512:(h + 1) * 512], sSt)
                    P.dma("sync", kha_d[h * 128:(h + 1) * 128, t0:t0 + 512], sc["kh"][:, h * 512:(h + 1) * 512], sSt)
                P.dma("sync", ka_d[:, t0:t0 + 512], sc["kT"][:, 0:512], sSt)
            P.barrier("proj_a")

    def attn_a():
        with ExitStack() as ph:
            bufs = dict(
                q=ph.enter_context(nc.sbuf_tensor("aa_q", [128, 4 * S], BF16)),
                qh=ph.enter_context(nc.sbuf_tensor("aa_qh", [128, 4 * S], FP8)),
                k=ph.enter_context(nc.sbuf_tensor("aa_k", [128, S], BF16)),
                kh=ph.enter_context(nc.sbuf_tensor("aa_kh", [128, 4 * S], FP8)),
                v=ph.enter_context(nc.sbuf_tensor("aa_v", [128, S], BF16)),
                sim=ph.enter_context(nc.sbuf_tensor("aa_sim", [128, S], BF16)),
                ms=ph.enter_context(nc.sbuf_tensor("aa_ms", [128, S], F32)),
                p=ph.enter_context(nc.sbuf_tensor("aa_p", [128, S], BF16)),
                pt=ph.enter_context(nc.sbuf_tensor("aa_pt", [128, 512], BF16)),
                sl=ph.enter_context(nc.sbuf_tensor("aa_sl", [128, 24], F32)),
                th=ph.enter_context(nc.sbuf_tensor("aa_th", [128, 1], F32)),
                rm=ph.enter_context(nc.sbuf_tensor("aa_rm", [128, 1], F32)),
                nrm=ph.enter_context(nc.sbuf_tensor("aa_nrm", [128, 1], F32)),
                ls=ph.enter_context(nc.sbuf_tensor("aa_ls", [128, 1], F32)),
                lr=ph.enter_context(nc.sbuf_tensor("aa_lr", [128, 1], F32)),
                dg=ph.enter_context(nc.sbuf_tensor("aa_dg", [128, 128], BF16)),
                mk=ph.enter_context(nc.sbuf_tensor("aa_mk", [128, 512], BF16)),
                jk=ph.enter_context(nc.sbuf_tensor("aa_jk", [128, 512], BF16)),
                av=ph.enter_context(nc.sbuf_tensor("aa_av", [128, 8192], BF16)),
            )
            bufs["ptb"] = [ph.enter_context(nc.psum_tensor(f"aa_ptb{i}", [128, 512], F32)) for i in range(2)]
            un = P.sem("aa_un")
            for b in range(B):
                t0 = b * S
                P.wait("sync", un, None)
                P.wait("sync", sSt, None)
                P.dma("sync", bufs["k"][:], ka_d[:, t0:t0 + S], sX)
                P.dma("sync", bufs["v"][:].rearrange("p (tt v) -> p tt v", v=128), va_d[b * 16:(b + 1) * 16, :, :].rearrange("tt p v -> p tt v"), sX)
                for h in range(4):
                    P.dma("sync", bufs["kh"][:, h * S:(h + 1) * S], kha_d[h * 128:(h + 1) * 128, t0:t0 + S], sX)
                    P.dma("sync", bufs["q"][:, h * S:(h + 1) * S], qa_d[h * 128:(h + 1) * 128, t0:t0 + S], sX)
                    P.dma("sync", bufs["qh"][:, h * S:(h + 1) * S], qha_d[h * 128:(h + 1) * 128, t0:t0 + S], sX)
                for eng in ("tensor", "vector", "scalar"):
                    P.wait(eng, sX, None)
                P.wait("vector", sSt, None)
                for h in range(4):
                    for qt in range(16):
                        unit_a(b, h, qt, bufs, un)
                P.wait("sync", sDV, None)
                for sh in range(2):
                    shard = b * 2 + sh
                    for h in range(4):
                        P.dma("sync", a2a_in[shard * 512 + h * 128:shard * 512 + (h + 1) * 128, :],
                              bufs["av"][:, h * 2048 + sh * 1024: h * 2048 + (sh + 1) * 1024], sSt)
            P.barrier("attn_a")

    def unit_a(b, h, qt, bufs, un):
        nk = qt + 1
        ng = (nk + 3) // 4
        sim_sb, ms_sb, p_sb, pt_sb = bufs["sim"], bufs["ms"], bufs["p"], bufs["pt"]
        slots, theta = bufs["sl"], bufs["th"]
        qcol = slice(h * S + qt * 128, h * S + (qt + 1) * 128)
        pc = qt
        sim_bank = pb[0 + (qt % 2)]
        sc_bank = pb[2 + (qt % 2)]
        av_bank = pb[4 + (qt % 2)]
        ptb = bufs["ptb"][qt % 2]
        P.wait("tensor", sDV, None); P.wait("tensor", sAC, None)
        P.wait("vector", sPE, None); P.wait("vector", sAC, None)
        P.wait("scalar", sPE, None); P.wait("scalar", sDV, None)
        P.op("vector", lambda e: e.memset(slots[:], 0.0))
        for g in range(ng):
            k0 = g * 512
            w = min(512, nk * 128 - k0)
            if g >= 1:
                P.wait("tensor", sDV, None)
            P.op("tensor", lambda e, k2=k0, w2=w: e.matmul(
                sim_bank[:, 0:w2], bufs["qh"][:, qcol], bufs["kh"][:, h * S + k2: h * S + k2 + w2],
                start=True, stop=True), inc=sPE)
            P.wait("vector", sPE, None)
            if g < ng - 1:
                P.op("vector", lambda e, k2=k0, w2=w, g2=g: e.tensor_scalar(
                    out=sim_sb[:, k2:k2 + w2], in0=sim_bank[:, 0:w2], scalar1=1.0, scalar2=0.0,
                    op0=ALU.mult, op1=ALU.add, accum_out=slots[:, g2:g2 + 1]), inc=sDV)
                P.op("vector", lambda e, k2=k0, w2=w, g2=g: e.scalar_tensor_tensor(
                    out=bufs["jk"][:, 0:w2], in0=sim_sb[:, k2:k2 + w2], scalar=1.0, in1=sim_sb[:, k2:k2 + w2],
                    op0=ALU.mult, op1=ALU.mult, accum_out=slots[:, 8 + g2:9 + g2]), inc=sDV)
            else:
                nd = w - 128
                if nd > 0:
                    P.op("vector", lambda e, k2=k0, w2=nd, g2=g: e.tensor_scalar(
                        out=sim_sb[:, k2:k2 + w2], in0=sim_bank[:, 0:w2], scalar1=1.0, scalar2=0.0,
                        op0=ALU.mult, op1=ALU.add, accum_out=slots[:, g2:g2 + 1]), inc=sDV)
                    P.op("vector", lambda e, k2=k0, w2=nd: e.scalar_tensor_tensor(
                        out=bufs["jk"][:, 0:w2], in0=sim_sb[:, k2:k2 + w2], scalar=1.0, in1=sim_sb[:, k2:k2 + w2],
                        op0=ALU.mult, op1=ALU.mult, accum_out=slots[:, 13:14]), inc=sDV)
                P.op("vector", lambda e, k2=k0 + nd, nd2=nd: e.tensor_tensor(
                    sim_sb[:, k2:k2 + 128], sim_bank[:, nd2:nd2 + 128], causal01[:], ALU.mult), inc=sDV)
                P.op("vector", lambda e, k2=k0 + nd: e.tensor_scalar(
                    out=bufs["jk"][:, 0:128], in0=sim_sb[:, k2:k2 + 128], scalar1=1.0, scalar2=0.0,
                    op0=ALU.mult, op1=ALU.add, accum_out=slots[:, 4:5]), inc=sDV)
                P.op("vector", lambda e, k2=k0 + nd: e.scalar_tensor_tensor(
                    out=bufs["jk"][:, 0:128], in0=sim_sb[:, k2:k2 + 128], scalar=1.0, in1=sim_sb[:, k2:k2 + 128],
                    op0=ALU.mult, op1=ALU.mult, accum_out=slots[:, 12:13]), inc=sDV)
        P.wait("vector", sDV, None)
        P.op("vector", lambda e: e.reduce_sum(slots[:, 14:15], slots[:, 0:6], axis=AX.X))
        P.op("vector", lambda e: e.reduce_sum(slots[:, 15:16], slots[:, 8:14], axis=AX.X), inc=sDV)
        P.wait("vector", sDV, None)
        P.op("vector", lambda e, p2=pc: e.tensor_scalar(
            out=slots[:, 16:17], in0=slots[:, 14:15], scalar1=invn[:, p2:p2 + 1], scalar2=None, op0=ALU.mult))
        P.op("vector", lambda e, p2=pc: e.tensor_scalar(
            out=slots[:, 17:18], in0=slots[:, 15:16], scalar1=invn[:, p2:p2 + 1], scalar2=None, op0=ALU.mult), inc=sDV)
        P.wait("vector", sDV, None)
        P.op("vector", lambda e: e.scalar_tensor_tensor(
            out=slots[:, 18:19], in0=slots[:, 16:17], scalar=-1.0, in1=slots[:, 16:17], op0=ALU.mult, op1=ALU.mult), inc=sDV)
        P.wait("vector", sDV, None)
        P.op("vector", lambda e: e.tensor_tensor(slots[:, 19:20], slots[:, 17:18], slots[:, 18:19], ALU.add), inc=sDV)
        P.wait("vector", sDV, None)
        P.op("vector", lambda e: e.tensor_scalar(
            out=slots[:, 19:20], in0=slots[:, 19:20], scalar1=1e-6, scalar2=None, op0=ALU.max), inc=sDV)
        P.wait("scalar", sDV, None)
        P.op("scalar", lambda e: e.activation(slots[:, 20:21], slots[:, 19:20], AF.Ln), inc=sAC)
        P.wait("scalar", sAC, None)
        P.op("scalar", lambda e: e.activation(slots[:, 21:22], slots[:, 20:21], AF.Exp, scale=0.5), inc=sAC)
        P.wait("vector", sAC, None)
        P.op("vector", lambda e, p2=pc: e.tensor_scalar(
            out=slots[:, 22:23], in0=slots[:, 21:22], scalar1=ztab[:, p2:p2 + 1], scalar2=None, op0=ALU.mult), inc=sDV)
        P.wait("vector", sDV, None)
        P.op("vector", lambda e: e.tensor_tensor(theta[:], slots[:, 16:17], slots[:, 22:23], ALU.add), inc=sDV)
        P.wait("vector", sDV, None)
        P.op("vector", lambda e: e.tensor_scalar(
            out=theta[:], in0=theta[:], scalar1=127.0, scalar2=None, op0=ALU.min), inc=sDV)
        for g in range(ng):
            k0 = g * 512
            w = min(512, nk * 128 - k0)
            if g >= 1:
                P.wait("tensor", sDV, None)
            P.op("tensor", lambda e, k2=k0, w2=w: e.matmul(
                sc_bank[:, 0:w2], bufs["q"][:, qcol], bufs["k"][:, k2:k2 + w2],
                start=True, stop=True), inc=sPE)
            P.wait("vector", sPE, None)
            P.op("vector", lambda e, k2=k0, w2=w: e.tensor_scalar(
                out=bufs["mk"][:, 0:w2], in0=sim_sb[:, k2:k2 + w2], scalar1=theta[:], scalar2=NEG,
                op0=ALU.is_lt, op1=ALU.mult), inc=sDV)
            P.op("vector", lambda e, k2=k0, w2=w: e.scalar_tensor_tensor(
                out=ms_sb[:, k2:k2 + w2], in0=sc_bank[:, 0:w2], scalar=float(1.0 / np.sqrt(HA_D)),
                in1=bufs["mk"][:, 0:w2], op0=ALU.mult, op1=ALU.add), inc=sDV)
            if g == ng - 1:
                kd = nk * 128 - 128
                P.op("vector", lambda e, k2=kd: e.tensor_tensor(
                    ms_sb[:, k2:k2 + 128], ms_sb[:, k2:k2 + 128], causal_add[:], ALU.add), inc=sDV)
            P.op("vector", lambda e, k2=k0, w2=w, g2=g: e.reduce_max(
                slots[:, g2:g2 + 1], ms_sb[:, k2:k2 + w2], axis=AX.X), inc=sDV)
        P.wait("vector", sDV, None)
        P.op("vector", lambda e, g2=ng: e.reduce_max(bufs["rm"][:], slots[:, 0:g2], axis=AX.X), inc=sDV)
        P.wait("vector", sDV, None)
        P.op("vector", lambda e: e.tensor_scalar(
            out=bufs["nrm"][:], in0=bufs["rm"][:], scalar1=-1.0, scalar2=None, op0=ALU.mult), inc=sDV)
        P.wait("scalar", sDV, None)
        for g in range(ng):
            k0 = g * 512
            w = min(512, nk * 128 - k0)
            P.op("scalar", lambda e, k2=k0, w2=w, g2=g: e.activation(
                p_sb[:, k2:k2 + w2], ms_sb[:, k2:k2 + w2], AF.Exp, bias=bufs["nrm"][:],
                accum_out=slots[:, 8 + g2:9 + g2]), inc=sAC)
        P.wait("vector", sAC, None)
        P.op("vector", lambda e, g2=ng: e.reduce_sum(bufs["ls"][:], slots[:, 8:8 + g2], axis=AX.X), inc=sDV)
        P.wait("vector", sDV, None)
        P.op("vector", lambda e: e.reciprocal(bufs["lr"][:], bufs["ls"][:]), inc=sDV)
        P.wait("vector", sDV, None)
        P.op("vector", lambda e: e.tensor_scalar(
            out=bufs["dg"][:], in0=ident_bf[:], scalar1=bufs["lr"][:], scalar2=None, op0=ALU.mult), inc=sDV)
        P.wait("tensor", sDV, None)
        for kc in range(nk):
            ps = kc % 4
            P.op("tensor", lambda e, k2=kc, p2=ps: e.matmul(
                ptb[:, p2 * 128:(p2 + 1) * 128], p_sb[:, k2 * 128:(k2 + 1) * 128], bufs["dg"][:],
                start=True, stop=True), inc=(sPE if (ps == 3 or kc == nk - 1) else None))
            if ps == 3 or kc == nk - 1:
                P.wait("scalar", sPE, None)
                wdt = (ps + 1) * 128
                P.op("scalar", lambda e, w2=wdt: e.copy(pt_sb[:, 0:w2], ptb[:, 0:w2]), inc=sAC)
                P.wait("tensor", sAC, None)
                base = kc - ps
                for j in range(ps + 1):
                    kcj = base + j
                    P.op("tensor", lambda e, k2=kcj, j2=j: e.matmul(
                        av_bank[:, 0:128], bufs["v"][:, k2 * 128:(k2 + 1) * 128], pt_sb[:, j2 * 128:(j2 + 1) * 128],
                        start=(k2 == 0), stop=(k2 == nk - 1)), inc=(sPE if kcj == nk - 1 else None))
        P.wait("vector", sPE, None)
        P.op("vector", lambda e, h2=h, q2=qt: e.tensor_copy(
            bufs["av"][:, h2 * 2048 + q2 * 128: h2 * 2048 + (q2 + 1) * 128], av_bank[:, 0:128]), inc=sDV)
        P.op("vector", lambda e: e.nop(), inc=un)

    def wo_phase(wname, xold_dram, xnew_dram, name):
        with ExitStack() as ph:
            afm = ph.enter_context(nc.sbuf_tensor("wo_afm" + name, [128, NEC * T_OWN], BF16))
            wsl = [ph.enter_context(nc.sbuf_tensor(f"wo{name}_w{i}", [128, NEC * 256], BF16)) for i in range(2)]
            resid = ph.enter_context(nc.sbuf_tensor("wo_res" + name, [128, 512], F32))
            wop = P.sem("wo_" + name)
            for ec in range(NEC):
                P.dma("sync", afm[:, ec * T_OWN:(ec + 1) * T_OWN], a2a_out[ec * 128:(ec + 1) * 128, :], sX)
            for ot in range(16):
                wbuf = wsl[ot % 2]
                if ot >= 2:
                    P.wait("gpsimd", sPE, None)
                for ec in range(NEC):
                    P.dma("gpsimd", wbuf[:, ec * 256:(ec + 1) * 256],
                          d[wname][ec * 128:(ec + 1) * 128, ot * 256:(ot + 1) * 256], sWp[ot % 2])
                P.wait("tensor", sWp[ot % 2], None)
                P.wait("tensor", sX, None)
                for tt in range(NTT_OWN):
                    kg = ot * NTT_OWN + tt
                    bank = pb[kg % 3]
                    if kg >= 3:
                        P.wait("tensor", wop, kg - 2)
                    for ec in range(NEC):
                        P.op("tensor", lambda e, bk=bank, ec2=ec, t2=tt, wb=wbuf: e.matmul(
                            bk[:, 0:256], afm[:, ec2 * T_OWN + t2 * 128: ec2 * T_OWN + (t2 + 1) * 128],
                            wb[:, ec2 * 256:(ec2 + 1) * 256],
                            start=(ec2 == 0), stop=(ec2 == NEC - 1)), inc=(sPE if ec == NEC - 1 else None))
                    P.wait("scalar", sDV, None)
                    P.dma("scalar", resid[:, 0:256], xold_dram[tt * 128:(tt + 1) * 128, ot * 256:(ot + 1) * 256], sSc)
                    P.wait("vector", sSc, None)
                    P.wait("vector", sPE, None)
                    P.wait("vector", sSt, None)
                    P.op("vector", lambda e, bk=bank: e.tensor_tensor(
                        resid[:, 256:512], bk[:, 0:256], resid[:, 0:256], ALU.add), inc=sDV)
                    P.wait("sync", sDV, None)
                    P.dma("sync", xnew_dram[tt * 128:(tt + 1) * 128, ot * 256:(ot + 1) * 256], resid[:, 256:512], sSt)
                    P.op("vector", lambda e: e.nop(), inc=wop)
            P.barrier("wo_" + name)

    def ffn_phase(xold_dram, xnew_dram, name, afm):
        with ExitStack() as ph:
            wsl = [ph.enter_context(nc.sbuf_tensor(f"ff{name}_w{i}", [128, NEC * 256], BF16)) for i in range(2)]
            axT = ph.enter_context(nc.sbuf_tensor("ff_axT" + name, [64, NTT_OWN * 128], BF16))
            ax_tok = ph.enter_context(nc.sbuf_tensor("ff_ax" + name, [128, 64], BF16))
            resid = ph.enter_context(nc.sbuf_tensor("ff_res" + name, [128, 512], F32))
            axp = ph.enter_context(nc.psum_tensor("ff_axp" + name, [128, 128], BF16))
            ffa = P.sem("ffa_" + name)
            ffp = P.sem("ffp_" + name)
            for tt in range(NTT_OWN):
                bank = pb[4]
                if tt >= 1:
                    P.wait("tensor", ffa, 2 * tt)
                for ec in range(NEC):
                    P.op("tensor", lambda e, ec2=ec, t2=tt, bk=bank: e.matmul(
                        bk[:, 0:64], afm[:, ec2 * T_OWN + t2 * 128: ec2 * T_OWN + (t2 + 1) * 128],
                        abT_sb[:, ec2 * 64:(ec2 + 1) * 64],
                        start=(ec2 == 0), stop=(ec2 == NEC - 1)), inc=(sPE if ec == NEC - 1 else None))
                P.wait("vector", sPE, None)
                P.op("vector", lambda e, bk=bank: e.tensor_copy(ax_tok[:, 0:64], bk[:, 0:64]), inc=ffa)
                P.wait("tensor", ffa, None)
                P.op("tensor", lambda e: e.transpose(axp[0:64, 0:128], ax_tok[:, 0:64], ident_bf[:]), inc=sPE)
                P.wait("vector", sPE, None)
                P.op("vector", lambda e, t2=tt: e.tensor_copy(
                    axT[0:64, t2 * 128:(t2 + 1) * 128], axp[0:64, 0:128]), inc=ffa)
            P.barrier("ffn_ax_" + name)
            for ot in range(16):
                wbuf = wsl[ot % 2]
                if ot >= 2:
                    P.wait("gpsimd", sPE, None)
                for ec in range(NEC):
                    P.dma("gpsimd", wbuf[:, ec * 256:(ec + 1) * 256],
                          d["fker"][ec * 128:(ec + 1) * 128, ot * 256:(ot + 1) * 256], sWp[ot % 2])
                P.wait("tensor", sWp[ot % 2], None)
                for tt in range(NTT_OWN):
                    kg = ot * NTT_OWN + tt
                    bank = pb[kg % 3]
                    if kg >= 3:
                        P.wait("tensor", ffp, kg - 2)
                    for ec in range(NEC):
                        P.op("tensor", lambda e, bk=bank, ec2=ec, t2=tt, wb=wbuf: e.matmul(
                            bk[:, 0:256], afm[:, ec2 * T_OWN + t2 * 128: ec2 * T_OWN + (t2 + 1) * 128],
                            wb[:, ec2 * 256:(ec2 + 1) * 256],
                            start=(ec2 == 0), stop=False))
                    P.op("tensor", lambda e, bk=bank, o2=ot: e.matmul(
                        bk[:, 0:256], ones_bf[:, :], fbias_sb[:, o2 * 256:(o2 + 1) * 256],
                        start=False, stop=False))
                    P.op("tensor", lambda e, bk=bank, o2=ot, t2=tt: e.matmul(
                        bk[:, 0:256], axT[0:64, t2 * 128:(t2 + 1) * 128], bbT_sb[0:64, o2 * 256:(o2 + 1) * 256],
                        start=False, stop=True), inc=sPE)
                    P.wait("scalar", sDV, None)
                    P.dma("scalar", resid[:, 0:256], xold_dram[tt * 128:(tt + 1) * 128, ot * 256:(ot + 1) * 256], sSc)
                    P.wait("vector", sSc, None)
                    P.wait("vector", sPE, None)
                    P.wait("vector", sSt, None)
                    P.op("vector", lambda e, bk=bank: e.tensor_tensor(
                        resid[:, 256:512], bk[:, 0:256], resid[:, 0:256], ALU.add), inc=sDV)
                    P.wait("sync", sDV, None)
                    P.dma("sync", xnew_dram[tt * 128:(tt + 1) * 128, ot * 256:(ot + 1) * 256], resid[:, 256:512], sSt)
                    P.op("vector", lambda e: e.nop(), inc=ffp)
            P.barrier("ffn_" + name)

    def proj_b():
        with ExitStack() as ph:
            wsb = ph.enter_context(nc.sbuf_tensor("pb_w", [128, NEC * 1536], BF16))
            xsl = [ph.enter_context(nc.sbuf_tensor(f"pb_x{i}", [128, NEC * 256], BF16)) for i in range(2)]
            sc = dict(
                qrot=ph.enter_context(nc.sbuf_tensor("pb_qr", [128, 512], BF16)),
                krot=ph.enter_context(nc.sbuf_tensor("pb_kr", [128, 512], BF16)),
                rtt=ph.enter_context(nc.sbuf_tensor("pb_rt", [128, 192], F32)),
                v=ph.enter_context(nc.sbuf_tensor("pb_v", [128, 512], BF16)),
                qT=ph.enter_context(nc.sbuf_tensor("pb_qT", [128, 512], BF16)),
                kT=ph.enter_context(nc.sbuf_tensor("pb_kT", [128, 512], BF16)),
            )
            pt8 = [ph.enter_context(nc.psum_tensor(f"pb_pt{i}", [128, 1024], BF16)) for i in range(2)]
            for ec in range(NEC):
                P.dma("gpsimd", wsb[:, ec * 1536:(ec + 1) * 1536], d["wqkvb"][ec * 128:(ec + 1) * 128, :], sW)
            evt = P.sem("pb_evt")
            for qs in range(32):
                buf = xsl[qs % 2]
                r, quar = qs // 4, qs % 4
                if qs >= 2:
                    P.wait("sync", sPE, None)
                for ecg in range(4):
                    P.dma("sync",
                          buf[:].rearrange("p (c t) -> p c t", t=256)[:, ecg * 8:(ecg + 1) * 8, :],
                          ag_out[r * E + ecg * 1024:r * E + (ecg + 1) * 1024, quar * 256:(quar + 1) * 256].rearrange("(c p) t -> p c t", p=128),
                          sXp[qs % 2])
                P.wait("tensor", sXp[qs % 2], None)
                P.wait("tensor", sW, None)
                for tt2 in range(2):
                    tt = qs * 2 + tt2
                    pc = tt % 16
                    bq = pb[0 + tt2]
                    bk_ = pb[2 + tt2]
                    bv = pb[4 + tt2]
                    if tt >= 2:
                        P.wait("tensor", evt, tt - 1)
                    for ec in range(NEC):
                        xt = buf[:, ec * 256 + tt2 * 128: ec * 256 + (tt2 + 1) * 128]
                        P.op("tensor", lambda e, bk2=bq, x=xt, ec2=ec: e.matmul(
                            bk2[:], x, wsb[:, ec2 * 1536:ec2 * 1536 + 512],
                            start=(ec2 == 0), stop=(ec2 == NEC - 1)))
                    for ec in range(NEC):
                        xt = buf[:, ec * 256 + tt2 * 128: ec * 256 + (tt2 + 1) * 128]
                        P.op("tensor", lambda e, bk2=bk_, x=xt, ec2=ec: e.matmul(
                            bk2[:], x, wsb[:, ec2 * 1536 + 512:ec2 * 1536 + 1024],
                            start=(ec2 == 0), stop=(ec2 == NEC - 1)))
                    for ec in range(NEC):
                        xt = buf[:, ec * 256 + tt2 * 128: ec * 256 + (tt2 + 1) * 128]
                        P.op("tensor", lambda e, bk2=bv, x=xt, ec2=ec: e.matmul(
                            bk2[:], x, wsb[:, ec2 * 1536 + 1024:(ec2 + 1) * 1536],
                            start=(ec2 == 0), stop=(ec2 == NEC - 1)), inc=(sPE if ec == NEC - 1 else None))
                    P.wait("vector", sPE, None)
                    P.wait("vector", sSt, None)
                    rot_b(sc, sc["qrot"], bq, pc)
                    rot_b(sc, sc["krot"], bk_, pc)
                    P.op("vector", lambda e, bk2=bv: e.tensor_copy(sc["v"][:], bk2[:]), inc=sDV)
                    P.wait("tensor", sDV, None)
                    bank = pt8[tt2]
                    for dc in range(4):
                        P.op("tensor", lambda e, bk2=bank, d2=dc: e.transpose(
                            bk2[:, d2 * 128:(d2 + 1) * 128], sc["qrot"][:, d2 * 128:(d2 + 1) * 128], ident_bf[:]))
                    for dc in range(4):
                        P.op("tensor", lambda e, bk2=bank, d2=dc: e.transpose(
                            bk2[:, 512 + d2 * 128:512 + (d2 + 1) * 128], sc["krot"][:, d2 * 128:(d2 + 1) * 128], ident_bf[:]),
                            inc=(sPE if dc == 3 else None))
                    P.wait("scalar", sPE, None)
                    P.wait("scalar", sSt, None)
                    P.op("scalar", lambda e, bk2=bank: e.copy(sc["qT"][:], bk2[:, 0:512]))
                    P.op("scalar", lambda e, bk2=bank: e.copy(sc["kT"][:], bk2[:, 512:1024]), inc=sAC)
                    P.wait("sync", sAC, None)
                    P.wait("sync", sDV, None)
                    t0 = tt * 128
                    for dc in range(4):
                        P.dma("sync", qb_d[dc * 128:(dc + 1) * 128, t0:t0 + 128], sc["qT"][:, dc * 128:(dc + 1) * 128], sSt)
                        P.dma("sync", kb_d[dc * 128:(dc + 1) * 128, t0:t0 + 128], sc["kT"][:, dc * 128:(dc + 1) * 128], sSt)
                    P.dma("sync", vb_d[tt, :, :], sc["v"][:], sSt)
                    P.op("sync", lambda e: e.nop(), inc=evt)
            P.barrier("proj_b")

    def rot_b(sc, dst, srcb, pc):
        cB = cosB[:, pc * ROT:(pc + 1) * ROT]
        sB = sinB[:, pc * ROT:(pc + 1) * ROT]
        rtt = sc["rtt"]
        for hh in range(2):
            base = hh * 256
            P.op("vector", lambda e, b2=base: e.tensor_copy(rtt[:, 0:63:2], srcb[:, b2 + 1:b2 + 64:2]))
            P.op("vector", lambda e, b2=base: e.tensor_copy(rtt[:, 1:64:2], srcb[:, b2:b2 + 63:2]))
            P.op("vector", lambda e, b2=base, c=cB: e.tensor_tensor(rtt[:, 64:128], srcb[:, b2:b2 + 64], c, ALU.mult))
            P.op("vector", lambda e, s2=sB: e.tensor_tensor(rtt[:, 128:192], rtt[:, 0:64], s2, ALU.mult))
            P.op("vector", lambda e, b2=base: e.tensor_add(dst[:, b2:b2 + 64], rtt[:, 64:128], rtt[:, 128:192]))
            P.op("vector", lambda e, b2=base: e.tensor_copy(dst[:, b2 + 64:b2 + 256], srcb[:, b2 + 64:b2 + 256]))

    def attn_b():
        with ExitStack() as ph:
            bufs = dict(
                q=ph.enter_context(nc.sbuf_tensor("ab_q", [128, 4 * S], BF16)),
                k=ph.enter_context(nc.sbuf_tensor("ab_k", [128, 4 * S], BF16)),
                v=ph.enter_context(nc.sbuf_tensor("ab_v", [128, 4 * S], BF16)),
                ms=ph.enter_context(nc.sbuf_tensor("ab_ms", [128, S], F32)),
                p=ph.enter_context(nc.sbuf_tensor("ab_p", [128, S], BF16)),
                pt=ph.enter_context(nc.sbuf_tensor("ab_pt", [128, 512], BF16)),
                sl=ph.enter_context(nc.sbuf_tensor("ab_sl", [128, 16], F32)),
                rm=ph.enter_context(nc.sbuf_tensor("ab_rm", [128, 1], F32)),
                nrm=ph.enter_context(nc.sbuf_tensor("ab_nrm", [128, 1], F32)),
                ls=ph.enter_context(nc.sbuf_tensor("ab_ls", [128, 1], F32)),
                lr=ph.enter_context(nc.sbuf_tensor("ab_lr", [128, 1], F32)),
                dg=ph.enter_context(nc.sbuf_tensor("ab_dg", [128, 128], BF16)),
                av=ph.enter_context(nc.sbuf_tensor("ab_av", [128, 8192], BF16)),
            )
            bufs["ptb"] = [ph.enter_context(nc.psum_tensor(f"ab_ptb{i}", [128, 512], F32)) for i in range(2)]
            un = P.sem("ab_un")
            for b in range(B):
                t0 = b * S
                P.wait("sync", un, None)
                P.wait("sync", sSt, None)
                P.dma("sync", bufs["q"][:].rearrange("p (dc t) -> p dc t", t=S), qb_d[:, t0:t0 + S].rearrange("(dc p) t -> p dc t", p=128), sX)
                P.dma("sync", bufs["k"][:].rearrange("p (dc t) -> p dc t", t=S), kb_d[:, t0:t0 + S].rearrange("(dc p) t -> p dc t", p=128), sX)
                P.dma("sync", bufs["v"][:].rearrange("p (tt v) -> p tt v", v=512), vb_d[b * 16:(b + 1) * 16, :, :].rearrange("tt p v -> p tt v"), sX)
                for eng in ("tensor", "vector", "scalar"):
                    P.wait(eng, sX, None)
                P.wait("vector", sSt, None)
                for h in range(2):
                    for qt in range(16):
                        unit_b(b, h, qt, bufs, un)
                P.wait("sync", sDV, None)
                for sh in range(2):
                    shard = b * 2 + sh
                    for r2 in range(4):
                        P.dma("sync", a2a_in[shard * 512 + r2 * 128:shard * 512 + (r2 + 1) * 128, :],
                              bufs["av"][:, r2 * 2048 + sh * 1024: r2 * 2048 + (sh + 1) * 1024], sSt)
            P.barrier("attn_b")

    def unit_b(b, h, qt, bufs, un):
        nk = qt + 1
        ng = (nk + 3) // 4
        ms_sb, p_sb, pt_sb, slots = bufs["ms"], bufs["p"], bufs["pt"], bufs["sl"]
        sc_bank = pb[2 + (qt % 2)]
        av_bank = pb[4 + (qt % 2)]
        ptb = bufs["ptb"][qt % 2]
        P.wait("tensor", sDV, None); P.wait("tensor", sAC, None)
        P.wait("vector", sPE, None); P.wait("vector", sAC, None)
        P.wait("scalar", sPE, None); P.wait("scalar", sDV, None)
        for g in range(ng):
            k0 = g * 512
            w = min(512, nk * 128 - k0)
            if g >= 1:
                P.wait("tensor", sDV, None)
            for dc in range(2):
                dcg = h * 2 + dc
                P.op("tensor", lambda e, k2=k0, w2=w, d2=dcg, d3=dc, q2=qt: e.matmul(
                    sc_bank[:, 0:w2], bufs["q"][:, d2 * S + q2 * 128:d2 * S + (q2 + 1) * 128],
                    bufs["k"][:, d2 * S + k2:d2 * S + k2 + w2],
                    start=(d3 == 0), stop=(d3 == 1)), inc=(sPE if dc == 1 else None))
            P.wait("vector", sPE, None)
            P.op("vector", lambda e, k2=k0, w2=w: e.tensor_scalar(
                out=ms_sb[:, k2:k2 + w2], in0=sc_bank[:, 0:w2], scalar1=1.0 / 16.0, scalar2=None,
                op0=ALU.mult), inc=sDV)
            if g == ng - 1:
                kd = nk * 128 - 128
                P.op("vector", lambda e, k2=kd: e.tensor_tensor(
                    ms_sb[:, k2:k2 + 128], ms_sb[:, k2:k2 + 128], causal_add[:], ALU.add), inc=sDV)
            P.op("vector", lambda e, k2=k0, w2=w, g2=g: e.reduce_max(
                slots[:, g2:g2 + 1], ms_sb[:, k2:k2 + w2], axis=AX.X), inc=sDV)
        P.wait("vector", sDV, None)
        P.op("vector", lambda e, g2=ng: e.reduce_max(bufs["rm"][:], slots[:, 0:g2], axis=AX.X), inc=sDV)
        P.wait("vector", sDV, None)
        P.op("vector", lambda e: e.tensor_scalar(
            out=bufs["nrm"][:], in0=bufs["rm"][:], scalar1=-1.0, scalar2=None, op0=ALU.mult), inc=sDV)
        P.wait("scalar", sDV, None)
        for g in range(ng):
            k0 = g * 512
            w = min(512, nk * 128 - k0)
            P.op("scalar", lambda e, k2=k0, w2=w, g2=g: e.activation(
                p_sb[:, k2:k2 + w2], ms_sb[:, k2:k2 + w2], AF.Exp, bias=bufs["nrm"][:],
                accum_out=slots[:, 8 + g2:9 + g2]), inc=sAC)
        P.wait("vector", sAC, None)
        P.op("vector", lambda e, g2=ng: e.reduce_sum(bufs["ls"][:], slots[:, 8:8 + g2], axis=AX.X), inc=sDV)
        P.wait("vector", sDV, None)
        P.op("vector", lambda e: e.reciprocal(bufs["lr"][:], bufs["ls"][:]), inc=sDV)
        P.wait("vector", sDV, None)
        P.op("vector", lambda e: e.tensor_scalar(
            out=bufs["dg"][:], in0=ident_bf[:], scalar1=bufs["lr"][:], scalar2=None, op0=ALU.mult), inc=sDV)
        P.wait("tensor", sDV, None)
        for kc in range(nk):
            ps = kc % 4
            P.op("tensor", lambda e, k2=kc, p2=ps: e.matmul(
                ptb[:, p2 * 128:(p2 + 1) * 128], p_sb[:, k2 * 128:(k2 + 1) * 128], bufs["dg"][:],
                start=True, stop=True), inc=(sPE if (ps == 3 or kc == nk - 1) else None))
            if ps == 3 or kc == nk - 1:
                P.wait("scalar", sPE, None)
                wdt = (ps + 1) * 128
                P.op("scalar", lambda e, w2=wdt: e.copy(pt_sb[:, 0:w2], ptb[:, 0:w2]), inc=sAC)
                P.wait("tensor", sAC, None)
                base = kc - ps
                for j in range(ps + 1):
                    kcj = base + j
                    for vc in range(2):
                        vcg = h * 2 + vc
                        P.op("tensor", lambda e, k2=kcj, j2=j, v2=vcg, v3=vc: e.matmul(
                            av_bank[:, v3 * 128:(v3 + 1) * 128],
                            bufs["v"][:, k2 * 512 + v2 * 128:k2 * 512 + (v2 + 1) * 128],
                            pt_sb[:, j2 * 128:(j2 + 1) * 128],
                            start=(k2 == 0 and v3 == 0), stop=(k2 == nk - 1)), inc=(sPE if (kcj == nk - 1 and vc == 1) else None))
        P.wait("vector", sPE, None)
        for vc in range(2):
            r2 = h * 2 + vc
            P.op("vector", lambda e, r3=r2, q2=qt, v3=vc: e.tensor_copy(
                bufs["av"][:, r3 * 2048 + q2 * 128: r3 * 2048 + (q2 + 1) * 128],
                av_bank[:, v3 * 128:(v3 + 1) * 128]), inc=(sDV if vc == 1 else None))
        P.op("vector", lambda e: e.nop(), inc=un)

    def _ffn_block(xold, xnew, name):
        with ExitStack() as es:
            afm = es.enter_context(nc.sbuf_tensor("afm" + name, [128, NEC * T_OWN], BF16))
            ln_phase(xold, "s" + name, to_agin=False, afm=afm)
            ffn_phase(xold, xnew, name, afm)

    seq = [
        ("ln1", lambda: ln_phase(d["x_own"], "s1", to_agin=True)),
        ("ag1", allgather),
        ("proj_a", proj_a),
        ("attn_a", attn_a),
        ("a2a1", alltoall),
        ("wo_a", lambda: wo_phase("woaT", d["x_own"], xr[0], "a")),
        ("ffn1", lambda: _ffn_block(xr[0], xr[1], "f1")),
        ("ln3", lambda: ln_phase(xr[1], "s3", to_agin=True)),
        ("ag2", allgather),
        ("proj_b", proj_b),
        ("attn_b", attn_b),
        ("a2a2", alltoall),
        ("wo_b", lambda: wo_phase("wob", xr[1], xr[2], "b")),
        ("ffn2", lambda: _ffn_block(xr[2], y, "f2")),
    ]
    limit = len(seq) if _PHASE_LIMIT is None else _PHASE_LIMIT
    for nm, th in seq[:limit]:
        th()

    dumpable = dict(wdbg=wdbg, xlndbg=xlndbg, ag_in=ag_in, ag_out=ag_out, a2a_in=a2a_in, a2a_out=a2a_out,
                    xr0=xr[0], xr1=xr[1], xr2=xr[2], qa_d=qa_d, qha_d=qha_d,
                    ka_d=ka_d, kha_d=kha_d, va_d=va_d, qb_d=qb_d, kb_d=kb_d,
                    vb_d=vb_d, beff_d=beff_d)
    for nm in _DEBUG_DUMPS:
        ap = dumpable[nm]
        dt_ = ap.tensor.dtype
        od = nc.dram_tensor("dump_" + nm, list(ap.tensor.shape), dt_, kind="ExternalOutput").ap()
        P.dma("sync", od[...], ap[...], sSt)

    P.wait("sync", sSt, None)
    P.run()
    stack.close()
    return nc


def make_core_inputs(c, inputs, shared):
    ii = {}
    f32 = np.float32
    g = inputs
    x = np.asarray(g["L0_attn_a_hidden_states"], f32).reshape(T_ALL, E)
    ii["x_own"] = np.ascontiguousarray(x[c * T_OWN:(c + 1) * T_OWN])
    wq = np.asarray(g["L0_attn_a_Wq"], f32)
    wk = np.asarray(g["L0_attn_a_Wk"], f32)
    wv = np.asarray(g["L0_attn_a_Wv"], f32)
    ii["wqkva"] = np.ascontiguousarray(np.concatenate(
        [wq[c * 512:(c + 1) * 512, :].T, wk[c * 128:(c + 1) * 128, :].T, wv[c * 128:(c + 1) * 128, :].T], axis=1))
    ii["hp1"] = np.ascontiguousarray(np.asarray(g["L0_attn_a_hash_proj1"], f32)[4 * c:4 * c + 4].transpose(1, 0, 2).reshape(128, 512))
    ii["hb1"] = np.ascontiguousarray(np.asarray(g["L0_attn_a_hash_bias1"], f32)[4 * c:4 * c + 4].reshape(4, 128).T)
    ii["hp2"] = np.ascontiguousarray(np.asarray(g["L0_attn_a_hash_proj2"], f32)[4 * c:4 * c + 4].transpose(1, 0, 2).reshape(128, 512))
    ii["hb2"] = np.ascontiguousarray(np.asarray(g["L0_attn_a_hash_bias2"], f32)[4 * c:4 * c + 4].reshape(4, 128).T)
    ii["woaT"] = shared["woaT"]
    wqb = np.asarray(g["L0_attn_b_Wq"], f32)[:, c * 512:(c + 1) * 512]
    wkb = np.asarray(g["L0_attn_b_Wk"], f32)[:, c * 512:(c + 1) * 512]
    wvb = np.asarray(g["L0_attn_b_Wv"], f32)[:, c * 512:(c + 1) * 512]
    ii["wqkvb"] = np.ascontiguousarray(np.concatenate([wqb, wkb, wvb], axis=1))
    ii["wob"] = shared["wob"]
    ii["fker"] = shared["fker"]
    ii["fbias"] = shared["fbias"]
    ii["gate"] = np.ascontiguousarray(np.asarray(g["L0_ffn_gate"], f32)[c].reshape(1, 16))
    ii["gAT"] = shared["gAT"]
    ii["gBT"] = shared["gBT"]
    ii["asT"] = shared["asT"]
    ii["bsT"] = shared["bsT"]
    for k, v in CST.items():
        ii[k] = v
    return ii


def kernel(**inputs):
    f32 = np.float32
    la = np.asarray(inputs["L0_ffn_lora_A"], f32)
    lb = np.asarray(inputs["L0_ffn_lora_B"], f32)
    gAT = la.transpose(0, 2, 1).reshape(16, NEC, 128, RANK).transpose(2, 0, 1, 3).reshape(128, 16 * NEC * RANK)
    gBT = lb.transpose(0, 2, 1).reshape(16, 128, 1024).transpose(1, 0, 2).reshape(128, 16 * 1024)
    asv = np.asarray(inputs["L0_ffn_lora_A_shared"], f32)
    asT = asv.T.reshape(NEC, 128, RANK).transpose(1, 0, 2).reshape(128, NEC * RANK)
    shared = {
        "woaT": np.ascontiguousarray(np.asarray(inputs["L0_attn_a_Wo"], f32).T),
        "wob": np.ascontiguousarray(np.asarray(inputs["L0_attn_b_Wo"], f32)),
        "fker": np.ascontiguousarray(np.asarray(inputs["L0_ffn_kernel"], f32)),
        "fbias": np.ascontiguousarray(np.asarray(inputs["L0_ffn_bias"], f32).reshape(1, E)),
        "gAT": np.ascontiguousarray(gAT),
        "gBT": np.ascontiguousarray(gBT),
        "asT": np.ascontiguousarray(asT),
        "bsT": np.ascontiguousarray(np.asarray(inputs["L0_ffn_lora_B_shared"], f32).T),
    }
    nc = build_nc()
    in_maps = [make_core_inputs(c, inputs, shared) for c in range(N_CORES)]
    res = run_bass_kernel_spmd(nc, in_maps, core_ids=list(range(N_CORES)))
    out = np.concatenate([res.results[c]["y"] for c in range(N_CORES)], axis=0)
    return out.reshape(B, S, E)


# version 03
# speedup vs baseline: 1.0789x; 1.0789x over previous
import numpy as np
import ml_dtypes
from contextlib import ExitStack

import concourse.bass as bass
import concourse.mybir as mybir
from concourse.bass_utils import run_bass_kernel_spmd
from concourse.aot_env import allow_ici_all_to_all

F32 = mybir.dt.float32
BF16 = mybir.dt.bfloat16
FP8 = mybir.dt.float8e4
AF = mybir.ActivationFunctionType
ALU = mybir.AluOpType
AX = mybir.AxisListType

N_CORES = 8
B, S, E = 4, 2048, 4096
T_ALL = B * S
T_OWN = T_ALL // N_CORES
NTT_OWN = T_OWN // 128
NEC = E // 128
HA_D = 128
ROT = 64
NUM_REMAIN = 410
NEG = -3.0e8
LN_EPS = 1e-5
RANK = 32
BF = ml_dtypes.bfloat16
_PHASE_LIMIT = None
_DEBUG_DUMPS = []


def _ppf(p):
    p = np.asarray(p, np.float64)
    a = [-3.969683028665376e+01, 2.209460984245205e+02, -2.759285104469687e+02,
         1.383577518672690e+02, -3.066479806614716e+01, 2.506628277459239e+00]
    b = [-5.447609879822406e+01, 1.615858368580409e+02, -1.556989798598866e+02,
         6.680131188771972e+01, -1.328068155288572e+01]
    c = [-7.784894002430293e-03, -3.223964580411365e-01, -2.400758277161838e+00,
         -2.549732539343734e+00, 4.374664141464968e+00, 2.938163982698783e+00]
    d = [7.784695709041462e-03, 3.224671290700398e-01, 2.445134137142996e+00,
         3.754408661907416e+00]
    out = np.empty_like(p)
    lo = p < 0.02425
    hi = p > 1 - 0.02425
    mid = ~(lo | hi)
    q = np.sqrt(-2 * np.log(np.where(lo, p, 0.5)))
    out_lo = (((((c[0] * q + c[1]) * q + c[2]) * q + c[3]) * q + c[4]) * q + c[5]) / \
             ((((d[0] * q + d[1]) * q + d[2]) * q + d[3]) * q + 1)
    q = p - 0.5
    r = q * q
    out_mid = (((((a[0] * r + a[1]) * r + a[2]) * r + a[3]) * r + a[4]) * r + a[5]) * q / \
              (((((b[0] * r + b[1]) * r + b[2]) * r + b[3]) * r + b[4]) * r + 1)
    q = np.sqrt(-2 * np.log1p(-np.where(hi, p, 0.5)))
    out_hi = -(((((c[0] * q + c[1]) * q + c[2]) * q + c[3]) * q + c[4]) * q + c[5]) / \
             ((((d[0] * q + d[1]) * q + d[2]) * q + d[3]) * q + 1)
    out[lo] = out_lo[lo]
    out[mid] = out_mid[mid]
    out[hi] = out_hi[hi]
    return out


def make_constants():
    cst = {}
    cst["ident_bf"] = np.eye(128).astype(BF)
    tri = np.tril(np.ones((128, 128), np.float32))
    cst["causal01"] = tri
    cst["causal_add"] = np.where(tri > 0, 0.0, NEG).astype(np.float32)
    inv = 1.0 / 10000.0 ** (np.arange(0, HA_D, 2, dtype=np.float64) / HA_D)
    t = np.arange(S, dtype=np.float64)
    emb = np.concatenate([np.outer(t, inv)] * 2, -1)
    cst["cosA"] = np.ascontiguousarray(
        np.cos(emb).astype(np.float32).reshape(16, 128, HA_D).transpose(1, 0, 2).reshape(128, 16 * HA_D))
    cst["sinA"] = np.ascontiguousarray(
        np.sin(emb).astype(np.float32).reshape(16, 128, HA_D).transpose(1, 0, 2).reshape(128, 16 * HA_D))
    invb = 1.0 / 10000.0 ** (np.arange(0, ROT, 2, dtype=np.float64) / ROT)
    sinu = np.outer(t, invb)
    cos2 = np.repeat(np.cos(sinu), 2, axis=1).astype(np.float32)
    sin2 = np.repeat(np.sin(sinu), 2, axis=1).astype(np.float32)
    cst["cosB"] = np.ascontiguousarray(cos2.reshape(16, 128, ROT).transpose(1, 0, 2).reshape(128, 16 * ROT))
    cst["sinB"] = np.ascontiguousarray(sin2.reshape(16, 128, ROT).transpose(1, 0, 2).reshape(128, 16 * ROT))
    n = np.arange(S, dtype=np.float64) + 1.0
    z = np.full(S, -25.0)
    big = n > NUM_REMAIN
    z[big] = _ppf(1.0 - NUM_REMAIN / n[big])
    cst["ztab"] = np.ascontiguousarray(z.reshape(16, 128).T.astype(np.float32))
    cst["invn"] = np.ascontiguousarray((1.0 / n).reshape(16, 128).T.astype(np.float32))
    return cst


CST = make_constants()
ENGINES = ["tensor", "vector", "scalar", "gpsimd", "sync"]


class Sem:
    def __init__(self, handle):
        self.h = handle
        self.n = 0


class Prog:
    def __init__(self, nc, stack):
        self.nc = nc
        self.ops = {e: [] for e in ENGINES}
        self.stack = stack
        self.sems = {}

    def sem(self, name):
        if name not in self.sems:
            self.sems[name] = Sem(self.stack.enter_context(self.nc.semaphore(name)))
        return self.sems[name]

    def op(self, eng, fn, inc=None, inc_n=1):
        if inc is not None:
            inc.n += inc_n
            self.ops[eng].append(lambda e, f=fn, s=inc.h, k=inc_n: f(e).then_inc(s, k))
        else:
            self.ops[eng].append(fn)

    def dma(self, eng, dst, src, sem):
        self.op(eng, lambda e, a=dst, b=src: e.dma_start(out=a, in_=b), inc=sem, inc_n=16)

    def wait(self, eng, sem, val=None):
        v = sem.n if val is None else val
        self.ops[eng].append(lambda e, s=sem.h, vv=v: e.wait_ge(s, vv))

    def barrier(self, name, engines=ENGINES):
        bar = self.sem("bar")
        for eng in engines:
            self.op(eng, lambda e: e.nop(), inc=bar)
        tgt = bar.n
        for eng in engines:
            self.wait(eng, bar, tgt)

    def run(self):
        with self.nc.Block() as block:
            for eng in ENGINES:
                def make(eng_name):
                    def body(e):
                        for fn in self.ops[eng_name]:
                            fn(e)
                    return body
                getattr(block, eng)(make(eng))


def build_nc():
    nc = bass.Bass("TRN2", target_bir_lowering=False, debug=False)
    d = {}

    def di(name, shape, dtype=F32):
        d[name] = nc.dram_tensor(name, list(shape), dtype, kind="ExternalInput").ap()
        return d[name]

    di("x_own", (T_OWN, E))
    di("wqkva", (E, 768))
    di("hp1", (128, 4 * 128)); di("hb1", (128, 4))
    di("hp2", (128, 4 * 128)); di("hb2", (128, 4))
    di("woaT", (E, E))
    di("wqkvb", (E, 1536))
    di("wob", (E, E))
    di("fker", (E, E)); di("fbias", (1, E))
    di("gate", (1, 16))
    di("gAT", (128, 16 * NEC * RANK))
    di("gBT", (128, 16 * 1024))
    di("asT", (128, NEC * RANK))
    di("bsT", (RANK, E))
    di("ident_bf", (128, 128), BF16)
    di("causal01", (128, 128))
    di("causal_add", (128, 128))
    di("cosA", (128, 16 * 128)); di("sinA", (128, 16 * 128))
    di("cosB", (128, 16 * ROT)); di("sinB", (128, 16 * ROT))
    di("ztab", (128, 16)); di("invn", (128, 16))
    y = nc.dram_tensor("y", [T_OWN, E], F32, kind="ExternalOutput").ap()

    ag_in = nc.dram_tensor("ag_in", [E, T_OWN], BF16).ap()
    ag_out = nc.dram_tensor("ag_out", [N_CORES * E, T_OWN], BF16).ap()
    a2a_in = nc.dram_tensor("a2a_in", [N_CORES * 512, T_OWN], BF16).ap()
    a2a_out = nc.dram_tensor("a2a_out", [N_CORES * 512, T_OWN], BF16).ap()
    xr = [nc.dram_tensor(f"xr{i}", [T_OWN, E], F32).ap() for i in range(3)]
    qa_d = nc.dram_tensor("qa_d", [512, T_ALL], BF16).ap()
    qha_d = nc.dram_tensor("qha_d", [512, T_ALL], FP8).ap()
    ka_d = nc.dram_tensor("ka_d", [128, T_ALL], BF16).ap()
    kha_d = nc.dram_tensor("kha_d", [512, T_ALL], FP8).ap()
    va_d = nc.dram_tensor("va_d", [T_ALL // 128, 128, 128], BF16).ap()
    qb_d = nc.dram_tensor("qb_d", [512, T_ALL], BF16).ap()
    kb_d = nc.dram_tensor("kb_d", [512, T_ALL], BF16).ap()
    vb_d = nc.dram_tensor("vb_d", [T_ALL // 128, 128, 512], BF16).ap()
    beff_d = nc.dram_tensor("beff_d", [128, RANK * E // 128], F32).ap()
    xlndbg = nc.dram_tensor("xlndbg", [T_OWN, E], BF16).ap()
    wdbg = nc.dram_tensor("wdbg", [128, NEC * 256], BF16).ap()

    stack = ExitStack()
    P = Prog(nc, stack)
    e_ = stack.enter_context

    ident_bf = e_(nc.sbuf_tensor("c_ident", [128, 128], BF16))
    causal01 = e_(nc.sbuf_tensor("c_c01", [128, 128], F32))
    causal_add = e_(nc.sbuf_tensor("c_cadd", [128, 128], F32))
    cosA = e_(nc.sbuf_tensor("c_cosA", [128, 16 * 128], F32))
    sinA = e_(nc.sbuf_tensor("c_sinA", [128, 16 * 128], F32))
    cosB = e_(nc.sbuf_tensor("c_cosB", [128, 16 * ROT], F32))
    sinB = e_(nc.sbuf_tensor("c_sinB", [128, 16 * ROT], F32))
    ztab = e_(nc.sbuf_tensor("c_ztab", [128, 16], F32))
    invn = e_(nc.sbuf_tensor("c_invn", [128, 16], F32))
    ones_bf = e_(nc.sbuf_tensor("c_onesbf", [1, 128], BF16))
    ones_f = e_(nc.sbuf_tensor("c_onesf", [1, 128], F32))
    gate_row = e_(nc.sbuf_tensor("c_gater", [1, 16], F32))
    gbc = e_(nc.sbuf_tensor("c_gbc", [128, 16], F32))
    fbias_sb = e_(nc.sbuf_tensor("c_fbias", [1, E], BF16))
    hb1_sb = e_(nc.sbuf_tensor("c_hb1", [128, 4], F32))
    hb2_sb = e_(nc.sbuf_tensor("c_hb2", [128, 4], F32))
    hp1_sb = e_(nc.sbuf_tensor("c_hp1", [128, 4 * 128], BF16))
    hp2_sb = e_(nc.sbuf_tensor("c_hp2", [128, 4 * 128], BF16))
    abT_sb = e_(nc.sbuf_tensor("c_abT", [128, NEC * 64], BF16))
    bbT_sb = e_(nc.sbuf_tensor("c_bbT", [64, E], BF16))

    pb = [e_(nc.psum_tensor(f"pb{i}", [128, 512], F32)) for i in range(6)]

    sW = P.sem("sW"); sX = P.sem("sX"); sSt = P.sem("sSt")
    sWp = [P.sem("sW0"), P.sem("sW1")]
    sXp = [P.sem("sX0"), P.sem("sX1")]
    sPE = P.sem("sPE"); sDV = P.sem("sDV"); sAC = P.sem("sAC")
    sCC = P.sem("sCC"); sSc = P.sem("sSc")

    P.dma("sync", ident_bf[:], d["ident_bf"][:], sX)
    P.dma("sync", causal01[:], d["causal01"][:], sX)
    P.dma("sync", causal_add[:], d["causal_add"][:], sX)
    P.dma("sync", cosA[:], d["cosA"][:], sX)
    P.dma("sync", sinA[:], d["sinA"][:], sX)
    P.dma("sync", cosB[:], d["cosB"][:], sX)
    P.dma("sync", sinB[:], d["sinB"][:], sX)
    P.dma("sync", ztab[:], d["ztab"][:], sX)
    P.dma("sync", invn[:], d["invn"][:], sX)
    P.dma("sync", gate_row[:], d["gate"][:], sX)
    P.dma("sync", hb1_sb[:, :], d["hb1"][:], sX)
    P.dma("sync", hb2_sb[:, :], d["hb2"][:], sX)
    P.dma("gpsimd", fbias_sb[:], d["fbias"][:], sW)
    P.dma("gpsimd", hp1_sb[:], d["hp1"][:], sW)
    P.dma("gpsimd", hp2_sb[:], d["hp2"][:], sW)
    P.op("gpsimd", lambda e: e.memset(ones_bf[:], 1.0), inc=sW, inc_n=1)
    P.op("gpsimd", lambda e: e.memset(ones_f[:], 1.0), inc=sW, inc_n=1)

    with ExitStack() as ph:
        lA = ph.enter_context(nc.sbuf_tensor("l_A", [128, 16 * 1024], BF16))
        acc = [ph.enter_context(nc.sbuf_tensor(f"l_acc{i}", [128, 1024], F32)) for i in range(2)]
        asb = ph.enter_context(nc.sbuf_tensor("l_as", [128, 1024], BF16))
        P.dma("gpsimd", lA[:], d["gAT"][:], sW)
        P.dma("gpsimd", asb[:], d["asT"][:], sW)
        P.wait("tensor", sX, None)
        P.wait("tensor", sW, None)
        P.op("tensor", lambda e: e.matmul(pb[0][:, 0:16], ones_f[:, :], gate_row[:, :], start=True, stop=True), inc=sPE)
        P.wait("vector", sPE, None)
        P.op("vector", lambda e: e.tensor_copy(gbc[:], pb[0][:, 0:16]), inc=sDV)
        P.wait("vector", sW, None)
        P.op("vector", lambda e: e.tensor_scalar(out=acc[0][:], in0=lA[:, 0:1024], scalar1=gbc[:, 0:1], scalar2=None, op0=ALU.mult))
        for n in range(1, 16):
            P.op("vector", lambda e, n2=n: e.scalar_tensor_tensor(
                out=acc[0][:], in0=lA[:, n2 * 1024:(n2 + 1) * 1024], scalar=gbc[:, n2:n2 + 1],
                in1=acc[0][:], op0=ALU.mult, op1=ALU.add))
        P.op("vector", lambda e: e.tensor_copy(
            abT_sb[:].rearrange("p (c w) -> p c w", w=64)[:, :, 0:32],
            acc[0][:].rearrange("p (c r) -> p c r", r=32)))
        P.op("vector", lambda e: e.tensor_copy(
            abT_sb[:].rearrange("p (c w) -> p c w", w=64)[:, :, 32:64],
            asb[:].rearrange("p (c r) -> p c r", r=32)), inc=sDV)
        P.wait("gpsimd", sDV, None)
        P.dma("gpsimd", lA[:], d["gBT"][:], sW)
        P.wait("vector", sW, None)
        P.op("vector", lambda e: e.tensor_scalar(out=acc[1][:], in0=lA[:, 0:1024], scalar1=gbc[:, 0:1], scalar2=None, op0=ALU.mult))
        for n in range(1, 16):
            P.op("vector", lambda e, n2=n: e.scalar_tensor_tensor(
                out=acc[1][:], in0=lA[:, n2 * 1024:(n2 + 1) * 1024], scalar=gbc[:, n2:n2 + 1],
                in1=acc[1][:], op0=ALU.mult, op1=ALU.add), inc=(sDV if n == 15 else None))
        P.wait("sync", sDV, None)
        P.dma("sync", beff_d[:], acc[1][:], sSt)
        P.wait("gpsimd", sSt, None)
        P.dma("gpsimd", bbT_sb[0:RANK, :], beff_d.rearrange("(r s) w -> r (s w)", s=4), sW)
        P.dma("gpsimd", bbT_sb[RANK:2 * RANK, :], d["bsT"][:], sW)
        P.barrier("prologue")

    def ln_phase(src_dram, name, to_agin, afm=None):
        with ExitStack() as ph:
            xf32 = [ph.enter_context(nc.sbuf_tensor(f"ln{name}_x{i}", [128, E], F32)) for i in range(2)]
            lnst = ph.enter_context(nc.sbuf_tensor("ln_st" + name, [128, 48], F32))
            lnagg = ph.enter_context(nc.sbuf_tensor("ln_agg" + name, [128, 2], F32))
            lnmu = ph.enter_context(nc.sbuf_tensor("ln_mu" + name, [128, 1], F32))
            lnr = ph.enter_context(nc.sbuf_tensor("ln_r" + name, [128, 1], F32))
            lntmp = ph.enter_context(nc.sbuf_tensor("ln_tmp" + name, [128, 1], F32))
            xln = ph.enter_context(nc.sbuf_tensor("ln_xb" + name, [128, E], BF16))
            tevac = [ph.enter_context(nc.sbuf_tensor(f"ln{name}_tv{i}", [128, 512], BF16)) for i in range(2)]
            pt8 = [ph.enter_context(nc.psum_tensor(f"ln{name}_pt{i}", [128, 1024], BF16)) for i in range(2)]
            lnu = P.sem("lnu_" + name)
            for tt in range(NTT_OWN):
                buf = xf32[tt % 2]
                if tt >= 2:
                    P.wait("sync", lnu, tt - 1)
                P.dma("sync", buf[:], src_dram[tt * 128:(tt + 1) * 128, :], sXp[tt % 2])
                P.wait("vector", sXp[tt % 2], None)
                for c8 in range(8):
                    P.op("vector", lambda e, b2=buf, c=c8: e.bn_stats(
                        lnst[:, c * 6:(c + 1) * 6], b2[:, c * 512:(c + 1) * 512]),
                        inc=(sDV if c8 == 7 else None))
                P.wait("vector", sDV, None)
                P.op("vector", lambda e: e.bn_aggr(lnagg[:], lnst[:]), inc=sDV)
                P.wait("vector", sDV, None)
                P.op("vector", lambda e: e.tensor_copy(lnmu[:], lnagg[:, 0:1]))
                P.op("vector", lambda e: e.tensor_scalar(
                    out=lnr[:], in0=lnagg[:, 1:2], scalar1=LN_EPS, scalar2=None, op0=ALU.add), inc=sDV)
                P.wait("vector", sDV, None)
                P.wait("scalar", sDV, None)
                P.op("scalar", lambda e: e.activation(lntmp[:], lnr[:], AF.Ln), inc=sAC)
                P.op("scalar", lambda e: e.activation(lnr[:], lntmp[:], AF.Exp, scale=-0.5), inc=sAC)
                P.wait("vector", sAC, None)
                P.wait("vector", sPE, None)
                P.op("vector", lambda e, b2=buf: e.tensor_scalar(
                    out=xln[:], in0=b2[:], scalar1=lnmu[:], scalar2=lnr[:],
                    op0=ALU.subtract, op1=ALU.mult), inc=sDV)
                P.op("vector", lambda e: e.nop(), inc=lnu)
                P.wait("tensor", sDV, None)
                for g in range(4):
                    bank = pt8[g % 2]
                    P.wait("tensor", sAC, None)
                    for k in range(8):
                        ec = g * 8 + k
                        P.op("tensor", lambda e, bk=bank, c=ec, k2=k: e.transpose(
                            bk[:, k2 * 128:(k2 + 1) * 128], xln[:, c * 128:(c + 1) * 128], ident_bf[:]),
                            inc=(sPE if k == 7 else None))
                    P.wait("scalar", sPE, None)
                    if to_agin:
                        P.wait("scalar", sSt, None)
                        for half in range(2):
                            P.op("scalar", lambda e, bk=bank, h=half: e.copy(
                                tevac[h][:], bk[:, h * 512:(h + 1) * 512]), inc=sAC)
                        P.wait("sync", sAC, None)
                        for half in range(2):
                            ec0 = g * 8 + half * 4
                            P.dma("sync",
                                  ag_in[ec0 * 128:(ec0 + 4) * 128, tt * 128:(tt + 1) * 128].rearrange("(c p) t -> p c t", p=128),
                                  tevac[half][:].rearrange("p (c t) -> p c t", t=128), sSt)
                    else:
                        for half in range(2):
                            ec0 = g * 8 + half * 4
                            dst = afm[:].rearrange("p (c t) -> p c t", t=T_OWN)[:, ec0:ec0 + 4, tt * 128:(tt + 1) * 128]
                            P.op("scalar", lambda e, bk=bank, h=half, dd=dst: e.copy(
                                dd, bk[:, h * 512:(h + 1) * 512]), inc=sAC)
            P.barrier("ln_" + name)

    def allgather():
        P.wait("gpsimd", sSt, None)
        P.op("gpsimd", lambda e: e.collective_compute(
            "AllGather", ALU.bypass, replica_groups=[list(range(N_CORES))],
            ins=[ag_in.opt()], outs=[ag_out.opt()]), inc=sCC)
        for eng in ENGINES:
            P.wait(eng, sCC, None)

    def alltoall():
        P.wait("gpsimd", sSt, None)
        def go(e):
            with allow_ici_all_to_all():
                return e.collective_compute(
                    "AllToAll", ALU.bypass, replica_groups=[list(range(N_CORES))],
                    ins=[a2a_in.opt()], outs=[a2a_out.opt()])
        P.op("gpsimd", go, inc=sCC)
        for eng in ENGINES:
            P.wait(eng, sCC, None)

    def rope_a(sc, dst, psrc, pc, sbase, dbase):
        cA = cosA[:, pc * 128:(pc + 1) * 128]
        sA = sinA[:, pc * 128:(pc + 1) * 128]
        rt = sc["ropet"]
        P.op("vector", lambda e: e.tensor_tensor(rt[:, 0:128], psrc[:, sbase:sbase + 128], cA, ALU.mult))
        P.op("vector", lambda e: e.tensor_tensor(rt[:, 128:192], psrc[:, sbase + 64:sbase + 128], sA[:, 0:64], ALU.mult))
        P.op("vector", lambda e: e.tensor_tensor(rt[:, 192:256], psrc[:, sbase:sbase + 64], sA[:, 64:128], ALU.mult))
        P.op("vector", lambda e: e.tensor_sub(dst[:, dbase:dbase + 64], rt[:, 0:64], rt[:, 128:192]))
        P.op("vector", lambda e: e.tensor_add(dst[:, dbase + 64:dbase + 128], rt[:, 64:128], rt[:, 192:256]))

    def hash_head(sc, h, src_fm, dst):
        hsil, hy, hz = sc["hsil"], sc["hy"], sc["hz"]
        P.op("tensor", lambda e: e.matmul(pb[4][:], hp1_sb[:, h * 128:(h + 1) * 128], src_fm, start=True, stop=True), inc=sPE)
        P.wait("scalar", sPE, None)
        P.op("scalar", lambda e: e.activation(hsil[:], pb[4][:], AF.Silu, bias=hb1_sb[:, h:h + 1]), inc=sAC)
        P.wait("vector", sAC, None)
        P.op("vector", lambda e: e.tensor_tensor(hy[:], hsil[:], src_fm, ALU.add), inc=sDV)
        P.wait("tensor", sDV, None)
        P.op("tensor", lambda e: e.matmul(pb[5][:], hp2_sb[:, h * 128:(h + 1) * 128], hy[:], start=True, stop=True), inc=sPE)
        P.wait("vector", sPE, None)
        P.op("vector", lambda e: e.scalar_tensor_tensor(
            out=hz[:], in0=pb[5][:], scalar=hb2_sb[:, h:h + 1], in1=hy[:],
            op0=ALU.add, op1=ALU.add), inc=sDV)
        P.wait("scalar", sDV, None)
        P.op("scalar", lambda e: e.activation(dst, hz[:], AF.Sign), inc=sAC)

    def proj_a():
        with ExitStack() as ph:
            wsb = ph.enter_context(nc.sbuf_tensor("pa_w", [128, NEC * 768], BF16))
            xsl = [ph.enter_context(nc.sbuf_tensor(f"pa_x{i}", [128, NEC * 512], BF16)) for i in range(2)]
            sc = dict(
                qr=ph.enter_context(nc.sbuf_tensor("pa_qr", [128, 512], BF16)),
                kr=ph.enter_context(nc.sbuf_tensor("pa_kr", [128, 128], BF16)),
                ropet=ph.enter_context(nc.sbuf_tensor("pa_rt", [128, 256], F32)),
                qT=ph.enter_context(nc.sbuf_tensor("pa_qT", [128, 4 * 512], BF16)),
                kT=ph.enter_context(nc.sbuf_tensor("pa_kT", [128, 512], BF16)),
                v=ph.enter_context(nc.sbuf_tensor("pa_v", [128, 128], BF16)),
                hsil=ph.enter_context(nc.sbuf_tensor("pa_hs", [128, 512], F32)),
                hy=ph.enter_context(nc.sbuf_tensor("pa_hy", [128, 512], BF16)),
                hz=ph.enter_context(nc.sbuf_tensor("pa_hz", [128, 512], F32)),
                qh=ph.enter_context(nc.sbuf_tensor("pa_qh", [128, 4 * 512], FP8)),
                kh=ph.enter_context(nc.sbuf_tensor("pa_kh", [128, 4 * 512], FP8)),
            )
            pt8 = [ph.enter_context(nc.psum_tensor(f"pa_pt{i}", [128, 1024], BF16)) for i in range(2)]
            for ec in range(NEC):
                P.dma("gpsimd", wsb[:, ec * 768:(ec + 1) * 768], d["wqkva"][ec * 128:(ec + 1) * 128, :], sW)
            evt = P.sem("pa_evt")
            for hs in range(16):
                buf = xsl[hs % 2]
                r, half = hs // 2, hs % 2
                if hs >= 2:
                    P.wait("sync", sPE, None)
                for ec in range(NEC):
                    P.dma("sync", buf[:, ec * 512:(ec + 1) * 512],
                          ag_out[r * E + ec * 128:r * E + (ec + 1) * 128, half * 512:(half + 1) * 512], sXp[hs % 2])
                P.wait("tensor", sXp[hs % 2], None)
                P.wait("tensor", sW, None)
                for tt4 in range(4):
                    kglob = hs * 4 + tt4
                    pc = kglob % 16
                    bq = pb[tt4 % 2]
                    bkv = pb[2 + tt4 % 2]
                    if kglob >= 2:
                        P.wait("tensor", evt, kglob - 1)
                    for ec in range(NEC):
                        xt = buf[:, ec * 512 + tt4 * 128: ec * 512 + (tt4 + 1) * 128]
                        P.op("tensor", lambda e, bk=bq, x=xt, ec2=ec: e.matmul(
                            bk[:], x, wsb[:, ec2 * 768:ec2 * 768 + 512],
                            start=(ec2 == 0), stop=(ec2 == NEC - 1)))
                    for ec in range(NEC):
                        xt = buf[:, ec * 512 + tt4 * 128: ec * 512 + (tt4 + 1) * 128]
                        P.op("tensor", lambda e, bk=bkv, x=xt, ec2=ec: e.matmul(
                            bk[:, 0:256], x, wsb[:, ec2 * 768 + 512:(ec2 + 1) * 768],
                            start=(ec2 == 0), stop=(ec2 == NEC - 1)), inc=(sPE if ec == NEC - 1 else None))
                    P.wait("vector", sPE, None)
                    P.wait("vector", sSt, None)
                    for h in range(4):
                        rope_a(sc, sc["qr"], bq, pc, h * 128, h * 128)
                    rope_a(sc, sc["kr"], bkv, pc, 0, 0)
                    P.op("vector", lambda e, bk=bkv: e.tensor_copy(sc["v"][:], bk[:, 128:256]), inc=sDV)
                    P.wait("tensor", sDV, None)
                    bank = pt8[tt4 % 2]
                    for h in range(4):
                        P.op("tensor", lambda e, bk=bank, h2=h: e.transpose(
                            bk[:, h2 * 128:(h2 + 1) * 128], sc["qr"][:, h2 * 128:(h2 + 1) * 128], ident_bf[:]))
                    P.op("tensor", lambda e, bk=bank: e.transpose(bk[:, 512:640], sc["kr"][:, 0:128], ident_bf[:]), inc=sPE)
                    P.wait("scalar", sPE, None)
                    for h in range(4):
                        P.op("scalar", lambda e, bk=bank, h2=h, t4=tt4: e.copy(
                            sc["qT"][:, h2 * 512 + t4 * 128: h2 * 512 + (t4 + 1) * 128], bk[:, h2 * 128:(h2 + 1) * 128]))
                    P.op("scalar", lambda e, bk=bank, t4=tt4: e.copy(
                        sc["kT"][:, t4 * 128:(t4 + 1) * 128], bk[:, 512:640]), inc=sAC)
                    P.wait("sync", sDV, None)
                    P.dma("sync", va_d[kglob, :, :], sc["v"][:], sSt)
                    P.op("scalar", lambda e: e.nop(), inc=evt)
                P.wait("tensor", sAC, None)
                for h in range(4):
                    hash_head(sc, h, sc["qT"][:, h * 512:(h + 1) * 512], sc["qh"][:, h * 512:(h + 1) * 512])
                for h in range(4):
                    hash_head(sc, h, sc["kT"][:, 0:512], sc["kh"][:, h * 512:(h + 1) * 512])
                P.wait("sync", sAC, None)
                t0 = hs * 512
                for h in range(4):
                    P.dma("sync", qa_d[h * 128:(h + 1) * 128, t0:t0 + 512], sc["qT"][:, h * 512:(h + 1) * 512], sSt)
                    P.dma("sync", qha_d[h * 128:(h + 1) * 128, t0:t0 + 512], sc["qh"][:, h * 512:(h + 1) * 512], sSt)
                    P.dma("sync", kha_d[h * 128:(h + 1) * 128, t0:t0 + 512], sc["kh"][:, h * 512:(h + 1) * 512], sSt)
                P.dma("sync", ka_d[:, t0:t0 + 512], sc["kT"][:, 0:512], sSt)
            P.barrier("proj_a")

    def attn_a():
        with ExitStack() as ph:
            bufs = dict(
                q=ph.enter_context(nc.sbuf_tensor("aa_q", [128, 4 * S], BF16)),
                qh=ph.enter_context(nc.sbuf_tensor("aa_qh", [128, 4 * S], FP8)),
                k=ph.enter_context(nc.sbuf_tensor("aa_k", [128, S], BF16)),
                kh=ph.enter_context(nc.sbuf_tensor("aa_kh", [128, 4 * S], FP8)),
                v=ph.enter_context(nc.sbuf_tensor("aa_v", [128, S], BF16)),
                sim=ph.enter_context(nc.sbuf_tensor("aa_sim", [128, S], BF16)),
                ms=ph.enter_context(nc.sbuf_tensor("aa_ms", [128, S], F32)),
                p=ph.enter_context(nc.sbuf_tensor("aa_p", [128, S], BF16)),
                pt=ph.enter_context(nc.sbuf_tensor("aa_pt", [128, 512], BF16)),
                sl=ph.enter_context(nc.sbuf_tensor("aa_sl", [128, 24], F32)),
                th=ph.enter_context(nc.sbuf_tensor("aa_th", [128, 1], F32)),
                rm=ph.enter_context(nc.sbuf_tensor("aa_rm", [128, 1], F32)),
                nrm=ph.enter_context(nc.sbuf_tensor("aa_nrm", [128, 1], F32)),
                ls=ph.enter_context(nc.sbuf_tensor("aa_ls", [128, 1], F32)),
                lr=ph.enter_context(nc.sbuf_tensor("aa_lr", [128, 1], F32)),
                dg=ph.enter_context(nc.sbuf_tensor("aa_dg", [128, 128], BF16)),
                mk=ph.enter_context(nc.sbuf_tensor("aa_mk", [128, 512], BF16)),
                jk=ph.enter_context(nc.sbuf_tensor("aa_jk", [128, 512], BF16)),
                av=ph.enter_context(nc.sbuf_tensor("aa_av", [128, 8192], BF16)),
            )
            bufs["ptb"] = [ph.enter_context(nc.psum_tensor(f"aa_ptb{i}", [128, 512], F32)) for i in range(2)]
            un = P.sem("aa_un")
            for b in range(B):
                t0 = b * S
                P.wait("sync", un, None)
                P.wait("sync", sSt, None)
                P.dma("sync", bufs["k"][:], ka_d[:, t0:t0 + S], sX)
                P.dma("sync", bufs["v"][:].rearrange("p (tt v) -> p tt v", v=128), va_d[b * 16:(b + 1) * 16, :, :].rearrange("tt p v -> p tt v"), sX)
                for h in range(4):
                    P.dma("sync", bufs["kh"][:, h * S:(h + 1) * S], kha_d[h * 128:(h + 1) * 128, t0:t0 + S], sX)
                    P.dma("sync", bufs["q"][:, h * S:(h + 1) * S], qa_d[h * 128:(h + 1) * 128, t0:t0 + S], sX)
                    P.dma("sync", bufs["qh"][:, h * S:(h + 1) * S], qha_d[h * 128:(h + 1) * 128, t0:t0 + S], sX)
                for eng in ("tensor", "vector", "scalar"):
                    P.wait(eng, sX, None)
                P.wait("vector", sSt, None)
                for h in range(4):
                    for qt in range(16):
                        unit_a(b, h, qt, bufs, un)
                P.wait("sync", sDV, None)
                for sh in range(2):
                    shard = b * 2 + sh
                    for h in range(4):
                        P.dma("sync", a2a_in[shard * 512 + h * 128:shard * 512 + (h + 1) * 128, :],
                              bufs["av"][:, h * 2048 + sh * 1024: h * 2048 + (sh + 1) * 1024], sSt)
            P.barrier("attn_a")

    def unit_a(b, h, qt, bufs, un):
        nk = qt + 1
        ng = (nk + 3) // 4
        sim_sb, ms_sb, p_sb, pt_sb = bufs["sim"], bufs["ms"], bufs["p"], bufs["pt"]
        slots, theta = bufs["sl"], bufs["th"]
        qcol = slice(h * S + qt * 128, h * S + (qt + 1) * 128)
        pc = qt
        sim_bank = pb[0 + (qt % 2)]
        sc_bank = pb[2 + (qt % 2)]
        av_bank = pb[4 + (qt % 2)]
        ptb = bufs["ptb"][qt % 2]
        P.wait("tensor", sDV, None); P.wait("tensor", sAC, None)
        P.wait("vector", sPE, None); P.wait("vector", sAC, None)
        P.wait("scalar", sPE, None); P.wait("scalar", sDV, None)
        P.op("vector", lambda e: e.memset(slots[:], 0.0))
        for g in range(ng):
            k0 = g * 512
            w = min(512, nk * 128 - k0)
            if g >= 1:
                P.wait("tensor", sDV, None)
            P.op("tensor", lambda e, k2=k0, w2=w: e.matmul(
                sim_bank[:, 0:w2], bufs["qh"][:, qcol], bufs["kh"][:, h * S + k2: h * S + k2 + w2],
                start=True, stop=True), inc=sPE)
            P.wait("vector", sPE, None)
            if g < ng - 1:
                P.op("vector", lambda e, k2=k0, w2=w, g2=g: e.tensor_scalar(
                    out=sim_sb[:, k2:k2 + w2], in0=sim_bank[:, 0:w2], scalar1=1.0, scalar2=0.0,
                    op0=ALU.mult, op1=ALU.add, accum_out=slots[:, g2:g2 + 1]), inc=sDV)
                P.op("vector", lambda e, k2=k0, w2=w, g2=g: e.scalar_tensor_tensor(
                    out=bufs["jk"][:, 0:w2], in0=sim_sb[:, k2:k2 + w2], scalar=1.0, in1=sim_sb[:, k2:k2 + w2],
                    op0=ALU.mult, op1=ALU.mult, accum_out=slots[:, 8 + g2:9 + g2]), inc=sDV)
            else:
                nd = w - 128
                if nd > 0:
                    P.op("vector", lambda e, k2=k0, w2=nd, g2=g: e.tensor_scalar(
                        out=sim_sb[:, k2:k2 + w2], in0=sim_bank[:, 0:w2], scalar1=1.0, scalar2=0.0,
                        op0=ALU.mult, op1=ALU.add, accum_out=slots[:, g2:g2 + 1]), inc=sDV)
                    P.op("vector", lambda e, k2=k0, w2=nd: e.scalar_tensor_tensor(
                        out=bufs["jk"][:, 0:w2], in0=sim_sb[:, k2:k2 + w2], scalar=1.0, in1=sim_sb[:, k2:k2 + w2],
                        op0=ALU.mult, op1=ALU.mult, accum_out=slots[:, 13:14]), inc=sDV)
                P.op("vector", lambda e, k2=k0 + nd, nd2=nd: e.tensor_tensor(
                    sim_sb[:, k2:k2 + 128], sim_bank[:, nd2:nd2 + 128], causal01[:], ALU.mult), inc=sDV)
                P.op("vector", lambda e, k2=k0 + nd: e.tensor_scalar(
                    out=bufs["jk"][:, 0:128], in0=sim_sb[:, k2:k2 + 128], scalar1=1.0, scalar2=0.0,
                    op0=ALU.mult, op1=ALU.add, accum_out=slots[:, 4:5]), inc=sDV)
                P.op("vector", lambda e, k2=k0 + nd: e.scalar_tensor_tensor(
                    out=bufs["jk"][:, 0:128], in0=sim_sb[:, k2:k2 + 128], scalar=1.0, in1=sim_sb[:, k2:k2 + 128],
                    op0=ALU.mult, op1=ALU.mult, accum_out=slots[:, 12:13]), inc=sDV)
        P.wait("vector", sDV, None)
        P.op("vector", lambda e: e.reduce_sum(slots[:, 14:15], slots[:, 0:6], axis=AX.X))
        P.op("vector", lambda e: e.reduce_sum(slots[:, 15:16], slots[:, 8:14], axis=AX.X), inc=sDV)
        P.wait("vector", sDV, None)
        P.op("vector", lambda e, p2=pc: e.tensor_scalar(
            out=slots[:, 16:17], in0=slots[:, 14:15], scalar1=invn[:, p2:p2 + 1], scalar2=None, op0=ALU.mult))
        P.op("vector", lambda e, p2=pc: e.tensor_scalar(
            out=slots[:, 17:18], in0=slots[:, 15:16], scalar1=invn[:, p2:p2 + 1], scalar2=None, op0=ALU.mult), inc=sDV)
        P.wait("vector", sDV, None)
        P.op("vector", lambda e: e.scalar_tensor_tensor(
            out=slots[:, 18:19], in0=slots[:, 16:17], scalar=-1.0, in1=slots[:, 16:17], op0=ALU.mult, op1=ALU.mult), inc=sDV)
        P.wait("vector", sDV, None)
        P.op("vector", lambda e: e.tensor_tensor(slots[:, 19:20], slots[:, 17:18], slots[:, 18:19], ALU.add), inc=sDV)
        P.wait("vector", sDV, None)
        P.op("vector", lambda e: e.tensor_scalar(
            out=slots[:, 19:20], in0=slots[:, 19:20], scalar1=1e-6, scalar2=None, op0=ALU.max), inc=sDV)
        P.wait("scalar", sDV, None)
        P.op("scalar", lambda e: e.activation(slots[:, 20:21], slots[:, 19:20], AF.Ln), inc=sAC)
        P.wait("scalar", sAC, None)
        P.op("scalar", lambda e: e.activation(slots[:, 21:22], slots[:, 20:21], AF.Exp, scale=0.5), inc=sAC)
        P.wait("vector", sAC, None)
        P.op("vector", lambda e, p2=pc: e.tensor_scalar(
            out=slots[:, 22:23], in0=slots[:, 21:22], scalar1=ztab[:, p2:p2 + 1], scalar2=None, op0=ALU.mult), inc=sDV)
        P.wait("vector", sDV, None)
        P.op("vector", lambda e: e.tensor_tensor(theta[:], slots[:, 16:17], slots[:, 22:23], ALU.add), inc=sDV)
        P.wait("vector", sDV, None)
        P.op("vector", lambda e: e.tensor_scalar(
            out=theta[:], in0=theta[:], scalar1=127.0, scalar2=None, op0=ALU.min), inc=sDV)
        for g in range(ng):
            k0 = g * 512
            w = min(512, nk * 128 - k0)
            if g >= 1:
                P.wait("tensor", sDV, None)
            P.op("tensor", lambda e, k2=k0, w2=w: e.matmul(
                sc_bank[:, 0:w2], bufs["q"][:, qcol], bufs["k"][:, k2:k2 + w2],
                start=True, stop=True), inc=sPE)
            P.wait("vector", sPE, None)
            P.op("vector", lambda e, k2=k0, w2=w: e.tensor_scalar(
                out=bufs["mk"][:, 0:w2], in0=sim_sb[:, k2:k2 + w2], scalar1=theta[:], scalar2=NEG,
                op0=ALU.is_lt, op1=ALU.mult), inc=sDV)
            P.op("vector", lambda e, k2=k0, w2=w: e.scalar_tensor_tensor(
                out=ms_sb[:, k2:k2 + w2], in0=sc_bank[:, 0:w2], scalar=float(1.0 / np.sqrt(HA_D)),
                in1=bufs["mk"][:, 0:w2], op0=ALU.mult, op1=ALU.add), inc=sDV)
            if g == ng - 1:
                kd = nk * 128 - 128
                P.op("vector", lambda e, k2=kd: e.tensor_tensor(
                    ms_sb[:, k2:k2 + 128], ms_sb[:, k2:k2 + 128], causal_add[:], ALU.add), inc=sDV)
            P.op("vector", lambda e, k2=k0, w2=w, g2=g: e.reduce_max(
                slots[:, g2:g2 + 1], ms_sb[:, k2:k2 + w2], axis=AX.X), inc=sDV)
        P.wait("vector", sDV, None)
        P.op("vector", lambda e, g2=ng: e.reduce_max(bufs["rm"][:], slots[:, 0:g2], axis=AX.X), inc=sDV)
        P.wait("vector", sDV, None)
        P.op("vector", lambda e: e.tensor_scalar(
            out=bufs["nrm"][:], in0=bufs["rm"][:], scalar1=-1.0, scalar2=None, op0=ALU.mult), inc=sDV)
        P.wait("scalar", sDV, None)
        for g in range(ng):
            k0 = g * 512
            w = min(512, nk * 128 - k0)
            P.op("scalar", lambda e, k2=k0, w2=w, g2=g: e.activation(
                p_sb[:, k2:k2 + w2], ms_sb[:, k2:k2 + w2], AF.Exp, bias=bufs["nrm"][:],
                accum_out=slots[:, 8 + g2:9 + g2]), inc=sAC)
        P.wait("vector", sAC, None)
        P.op("vector", lambda e, g2=ng: e.reduce_sum(bufs["ls"][:], slots[:, 8:8 + g2], axis=AX.X), inc=sDV)
        P.wait("vector", sDV, None)
        P.op("vector", lambda e: e.reciprocal(bufs["lr"][:], bufs["ls"][:]), inc=sDV)
        P.wait("vector", sDV, None)
        P.op("vector", lambda e: e.tensor_scalar(
            out=bufs["dg"][:], in0=ident_bf[:], scalar1=bufs["lr"][:], scalar2=None, op0=ALU.mult), inc=sDV)
        P.wait("tensor", sDV, None)
        for kc in range(nk):
            ps = kc % 4
            P.op("tensor", lambda e, k2=kc, p2=ps: e.matmul(
                ptb[:, p2 * 128:(p2 + 1) * 128], p_sb[:, k2 * 128:(k2 + 1) * 128], bufs["dg"][:],
                start=True, stop=True), inc=(sPE if (ps == 3 or kc == nk - 1) else None))
            if ps == 3 or kc == nk - 1:
                P.wait("scalar", sPE, None)
                wdt = (ps + 1) * 128
                P.op("scalar", lambda e, w2=wdt: e.copy(pt_sb[:, 0:w2], ptb[:, 0:w2]), inc=sAC)
                P.wait("tensor", sAC, None)
                base = kc - ps
                for j in range(ps + 1):
                    kcj = base + j
                    P.op("tensor", lambda e, k2=kcj, j2=j: e.matmul(
                        av_bank[:, 0:128], bufs["v"][:, k2 * 128:(k2 + 1) * 128], pt_sb[:, j2 * 128:(j2 + 1) * 128],
                        start=(k2 == 0), stop=(k2 == nk - 1)), inc=(sPE if kcj == nk - 1 else None))
        P.wait("vector", sPE, None)
        P.op("vector", lambda e, h2=h, q2=qt: e.tensor_copy(
            bufs["av"][:, h2 * 2048 + q2 * 128: h2 * 2048 + (q2 + 1) * 128], av_bank[:, 0:128]), inc=sDV)
        P.op("vector", lambda e: e.nop(), inc=un)

    def wo_phase(wname, xold_dram, xnew_dram, name):
        with ExitStack() as ph:
            afm = ph.enter_context(nc.sbuf_tensor("wo_afm" + name, [128, NEC * T_OWN], BF16))
            wsl = [ph.enter_context(nc.sbuf_tensor(f"wo{name}_w{i}", [128, NEC * 512], BF16)) for i in range(2)]
            rin = [ph.enter_context(nc.sbuf_tensor(f"wo{name}_ri{i}", [128, 512], F32)) for i in range(2)]
            rout = [ph.enter_context(nc.sbuf_tensor(f"wo{name}_ro{i}", [128, 512], F32)) for i in range(2)]
            addc = P.sem("woadd_" + name)
            stc = P.sem("wost_" + name)
            ldc = P.sem("wold_" + name)
            addc0 = addc.n; stc0 = stc.n; ldc0 = ldc.n
            pe0 = sPE.n
            for ec in range(NEC):
                P.dma("sync", afm[:, ec * T_OWN:(ec + 1) * T_OWN], a2a_out[ec * 128:(ec + 1) * 128, :], sX)
            for ot in range(8):
                wbuf = wsl[ot % 2]
                if ot >= 2:
                    P.wait("gpsimd", addc, addc0 + (ot - 1) * 8)
                for ec in range(NEC):
                    P.dma("gpsimd", wbuf[:, ec * 512:(ec + 1) * 512],
                          d[wname][ec * 128:(ec + 1) * 128, ot * 512:(ot + 1) * 512], sWp[ot % 2])
                for tt in range(NTT_OWN):
                    kg = ot * NTT_OWN + tt
                    bank = pb[kg % 3]
                    if kg >= 2:
                        P.wait("scalar", addc, addc0 + kg - 1)
                    P.dma("scalar", rin[kg % 2][:], xold_dram[tt * 128:(tt + 1) * 128, ot * 512:(ot + 1) * 512], sSc)
                    if tt == 0:
                        P.wait("tensor", sWp[ot % 2], None)
                        if ot == 0:
                            P.wait("tensor", sX, None)
                    if kg >= 3:
                        P.wait("tensor", addc, addc0 + kg - 2)
                    for ec in range(NEC):
                        P.op("tensor", lambda e, bk=bank, ec2=ec, t2=tt, wb=wbuf: e.matmul(
                            bk[:], afm[:, ec2 * T_OWN + t2 * 128: ec2 * T_OWN + (t2 + 1) * 128],
                            wb[:, ec2 * 512:(ec2 + 1) * 512],
                            start=(ec2 == 0), stop=(ec2 == NEC - 1)), inc=(sPE if ec == NEC - 1 else None))
                    P.wait("vector", sSc, None)
                    P.wait("vector", sPE, pe0 + kg + 1)
                    if kg >= 2:
                        P.wait("vector", stc, stc0 + kg - 1)
                    P.op("vector", lambda e, bk=bank, k2=kg: e.tensor_tensor(
                        rout[k2 % 2][:], bk[:], rin[k2 % 2][:], ALU.add), inc=addc)
                    P.wait("sync", addc, addc0 + kg + 1)
                    P.dma("sync", xnew_dram[tt * 128:(tt + 1) * 128, ot * 512:(ot + 1) * 512], rout[kg % 2][:], sSt)
                    P.op("sync", lambda e: e.nop(), inc=stc)
            P.barrier("wo_" + name)

    def ffn_phase(xold_dram, xnew_dram, name, afm):
        with ExitStack() as ph:
            wsl = [ph.enter_context(nc.sbuf_tensor(f"ff{name}_w{i}", [128, NEC * 512], BF16)) for i in range(2)]
            axT = ph.enter_context(nc.sbuf_tensor("ff_axT" + name, [64, NTT_OWN * 128], BF16))
            ax_tok = ph.enter_context(nc.sbuf_tensor("ff_ax" + name, [128, 64], BF16))
            rin = [ph.enter_context(nc.sbuf_tensor(f"ff{name}_ri{i}", [128, 512], F32)) for i in range(2)]
            rout = [ph.enter_context(nc.sbuf_tensor(f"ff{name}_ro{i}", [128, 512], F32)) for i in range(2)]
            axp = ph.enter_context(nc.psum_tensor("ff_axp" + name, [128, 128], BF16))
            ffa = P.sem("ffa_" + name)
            addc = P.sem("ffadd_" + name)
            stc = P.sem("ffst_" + name)
            addc0 = addc.n; stc0 = stc.n
            pe0 = sPE.n
            for tt in range(NTT_OWN):
                bank = pb[4]
                if tt >= 1:
                    P.wait("tensor", ffa, 2 * tt)
                for ec in range(NEC):
                    P.op("tensor", lambda e, ec2=ec, t2=tt, bk=bank: e.matmul(
                        bk[:, 0:64], afm[:, ec2 * T_OWN + t2 * 128: ec2 * T_OWN + (t2 + 1) * 128],
                        abT_sb[:, ec2 * 64:(ec2 + 1) * 64],
                        start=(ec2 == 0), stop=(ec2 == NEC - 1)), inc=(sPE if ec == NEC - 1 else None))
                P.wait("vector", sPE, None)
                P.op("vector", lambda e, bk=bank: e.tensor_copy(ax_tok[:, 0:64], bk[:, 0:64]), inc=ffa)
                P.wait("tensor", ffa, None)
                P.op("tensor", lambda e: e.transpose(axp[0:64, 0:128], ax_tok[:, 0:64], ident_bf[:]), inc=sPE)
                P.wait("vector", sPE, None)
                P.op("vector", lambda e, t2=tt: e.tensor_copy(
                    axT[0:64, t2 * 128:(t2 + 1) * 128], axp[0:64, 0:128]), inc=ffa)
            P.barrier("ffn_ax_" + name)
            pe0 = sPE.n
            for ot in range(8):
                wbuf = wsl[ot % 2]
                if ot >= 2:
                    P.wait("gpsimd", addc, addc0 + (ot - 1) * 8)
                for ec in range(NEC):
                    P.dma("gpsimd", wbuf[:, ec * 512:(ec + 1) * 512],
                          d["fker"][ec * 128:(ec + 1) * 128, ot * 512:(ot + 1) * 512], sWp[ot % 2])
                for tt in range(NTT_OWN):
                    kg = ot * NTT_OWN + tt
                    bank = pb[kg % 3]
                    if kg >= 2:
                        P.wait("scalar", addc, addc0 + kg - 1)
                    P.dma("scalar", rin[kg % 2][:], xold_dram[tt * 128:(tt + 1) * 128, ot * 512:(ot + 1) * 512], sSc)
                    if tt == 0:
                        P.wait("tensor", sWp[ot % 2], None)
                    if kg >= 3:
                        P.wait("tensor", addc, addc0 + kg - 2)
                    for ec in range(NEC):
                        P.op("tensor", lambda e, bk=bank, ec2=ec, t2=tt, wb=wbuf: e.matmul(
                            bk[:], afm[:, ec2 * T_OWN + t2 * 128: ec2 * T_OWN + (t2 + 1) * 128],
                            wb[:, ec2 * 512:(ec2 + 1) * 512],
                            start=(ec2 == 0), stop=False))
                    P.op("tensor", lambda e, bk=bank, o2=ot: e.matmul(
                        bk[:], ones_bf[:, :], fbias_sb[:, o2 * 512:(o2 + 1) * 512],
                        start=False, stop=False))
                    P.op("tensor", lambda e, bk=bank, o2=ot, t2=tt: e.matmul(
                        bk[:], axT[0:64, t2 * 128:(t2 + 1) * 128], bbT_sb[0:64, o2 * 512:(o2 + 1) * 512],
                        start=False, stop=True), inc=sPE)
                    P.wait("vector", sSc, None)
                    P.wait("vector", sPE, pe0 + kg + 1)
                    if kg >= 2:
                        P.wait("vector", stc, stc0 + kg - 1)
                    P.op("vector", lambda e, bk=bank, k2=kg: e.tensor_tensor(
                        rout[k2 % 2][:], bk[:], rin[k2 % 2][:], ALU.add), inc=addc)
                    P.wait("sync", addc, addc0 + kg + 1)
                    P.dma("sync", xnew_dram[tt * 128:(tt + 1) * 128, ot * 512:(ot + 1) * 512], rout[kg % 2][:], sSt)
                    P.op("sync", lambda e: e.nop(), inc=stc)
            P.barrier("ffn_" + name)

    def proj_b():
        with ExitStack() as ph:
            wsb = ph.enter_context(nc.sbuf_tensor("pb_w", [128, NEC * 1536], BF16))
            xsl = [ph.enter_context(nc.sbuf_tensor(f"pb_x{i}", [128, NEC * 256], BF16)) for i in range(2)]
            sc = dict(
                qrot=ph.enter_context(nc.sbuf_tensor("pb_qr", [128, 512], BF16)),
                krot=ph.enter_context(nc.sbuf_tensor("pb_kr", [128, 512], BF16)),
                rtt=ph.enter_context(nc.sbuf_tensor("pb_rt", [128, 192], F32)),
                v=ph.enter_context(nc.sbuf_tensor("pb_v", [128, 512], BF16)),
                qT=ph.enter_context(nc.sbuf_tensor("pb_qT", [128, 512], BF16)),
                kT=ph.enter_context(nc.sbuf_tensor("pb_kT", [128, 512], BF16)),
            )
            pt8 = [ph.enter_context(nc.psum_tensor(f"pb_pt{i}", [128, 1024], BF16)) for i in range(2)]
            for ec in range(NEC):
                P.dma("gpsimd", wsb[:, ec * 1536:(ec + 1) * 1536], d["wqkvb"][ec * 128:(ec + 1) * 128, :], sW)
            evt = P.sem("pb_evt")
            for qs in range(32):
                buf = xsl[qs % 2]
                r, quar = qs // 4, qs % 4
                if qs >= 2:
                    P.wait("sync", sPE, None)
                for ecg in range(4):
                    P.dma("sync",
                          buf[:].rearrange("p (c t) -> p c t", t=256)[:, ecg * 8:(ecg + 1) * 8, :],
                          ag_out[r * E + ecg * 1024:r * E + (ecg + 1) * 1024, quar * 256:(quar + 1) * 256].rearrange("(c p) t -> p c t", p=128),
                          sXp[qs % 2])
                P.wait("tensor", sXp[qs % 2], None)
                P.wait("tensor", sW, None)
                for tt2 in range(2):
                    tt = qs * 2 + tt2
                    pc = tt % 16
                    bq = pb[0 + tt2]
                    bk_ = pb[2 + tt2]
                    bv = pb[4 + tt2]
                    if tt >= 2:
                        P.wait("tensor", evt, tt - 1)
                    for ec in range(NEC):
                        xt = buf[:, ec * 256 + tt2 * 128: ec * 256 + (tt2 + 1) * 128]
                        P.op("tensor", lambda e, bk2=bq, x=xt, ec2=ec: e.matmul(
                            bk2[:], x, wsb[:, ec2 * 1536:ec2 * 1536 + 512],
                            start=(ec2 == 0), stop=(ec2 == NEC - 1)))
                    for ec in range(NEC):
                        xt = buf[:, ec * 256 + tt2 * 128: ec * 256 + (tt2 + 1) * 128]
                        P.op("tensor", lambda e, bk2=bk_, x=xt, ec2=ec: e.matmul(
                            bk2[:], x, wsb[:, ec2 * 1536 + 512:ec2 * 1536 + 1024],
                            start=(ec2 == 0), stop=(ec2 == NEC - 1)))
                    for ec in range(NEC):
                        xt = buf[:, ec * 256 + tt2 * 128: ec * 256 + (tt2 + 1) * 128]
                        P.op("tensor", lambda e, bk2=bv, x=xt, ec2=ec: e.matmul(
                            bk2[:], x, wsb[:, ec2 * 1536 + 1024:(ec2 + 1) * 1536],
                            start=(ec2 == 0), stop=(ec2 == NEC - 1)), inc=(sPE if ec == NEC - 1 else None))
                    P.wait("vector", sPE, None)
                    P.wait("vector", sSt, None)
                    rot_b(sc, sc["qrot"], bq, pc)
                    rot_b(sc, sc["krot"], bk_, pc)
                    P.op("vector", lambda e, bk2=bv: e.tensor_copy(sc["v"][:], bk2[:]), inc=sDV)
                    P.wait("tensor", sDV, None)
                    bank = pt8[tt2]
                    for dc in range(4):
                        P.op("tensor", lambda e, bk2=bank, d2=dc: e.transpose(
                            bk2[:, d2 * 128:(d2 + 1) * 128], sc["qrot"][:, d2 * 128:(d2 + 1) * 128], ident_bf[:]))
                    for dc in range(4):
                        P.op("tensor", lambda e, bk2=bank, d2=dc: e.transpose(
                            bk2[:, 512 + d2 * 128:512 + (d2 + 1) * 128], sc["krot"][:, d2 * 128:(d2 + 1) * 128], ident_bf[:]),
                            inc=(sPE if dc == 3 else None))
                    P.wait("scalar", sPE, None)
                    P.wait("scalar", sSt, None)
                    P.op("scalar", lambda e, bk2=bank: e.copy(sc["qT"][:], bk2[:, 0:512]))
                    P.op("scalar", lambda e, bk2=bank: e.copy(sc["kT"][:], bk2[:, 512:1024]), inc=sAC)
                    P.wait("sync", sAC, None)
                    P.wait("sync", sDV, None)
                    t0 = tt * 128
                    for dc in range(4):
                        P.dma("sync", qb_d[dc * 128:(dc + 1) * 128, t0:t0 + 128], sc["qT"][:, dc * 128:(dc + 1) * 128], sSt)
                        P.dma("sync", kb_d[dc * 128:(dc + 1) * 128, t0:t0 + 128], sc["kT"][:, dc * 128:(dc + 1) * 128], sSt)
                    P.dma("sync", vb_d[tt, :, :], sc["v"][:], sSt)
                    P.op("sync", lambda e: e.nop(), inc=evt)
            P.barrier("proj_b")

    def rot_b(sc, dst, srcb, pc):
        cB = cosB[:, pc * ROT:(pc + 1) * ROT]
        sB = sinB[:, pc * ROT:(pc + 1) * ROT]
        rtt = sc["rtt"]
        for hh in range(2):
            base = hh * 256
            P.op("vector", lambda e, b2=base: e.tensor_copy(rtt[:, 0:63:2], srcb[:, b2 + 1:b2 + 64:2]))
            P.op("vector", lambda e, b2=base: e.tensor_copy(rtt[:, 1:64:2], srcb[:, b2:b2 + 63:2]))
            P.op("vector", lambda e, b2=base, c=cB: e.tensor_tensor(rtt[:, 64:128], srcb[:, b2:b2 + 64], c, ALU.mult))
            P.op("vector", lambda e, s2=sB: e.tensor_tensor(rtt[:, 128:192], rtt[:, 0:64], s2, ALU.mult))
            P.op("vector", lambda e, b2=base: e.tensor_add(dst[:, b2:b2 + 64], rtt[:, 64:128], rtt[:, 128:192]))
            P.op("vector", lambda e, b2=base: e.tensor_copy(dst[:, b2 + 64:b2 + 256], srcb[:, b2 + 64:b2 + 256]))

    def attn_b():
        with ExitStack() as ph:
            bufs = dict(
                q=ph.enter_context(nc.sbuf_tensor("ab_q", [128, 4 * S], BF16)),
                k=ph.enter_context(nc.sbuf_tensor("ab_k", [128, 4 * S], BF16)),
                v=ph.enter_context(nc.sbuf_tensor("ab_v", [128, 4 * S], BF16)),
                ms=ph.enter_context(nc.sbuf_tensor("ab_ms", [128, S], F32)),
                p=ph.enter_context(nc.sbuf_tensor("ab_p", [128, S], BF16)),
                pt=ph.enter_context(nc.sbuf_tensor("ab_pt", [128, 512], BF16)),
                sl=ph.enter_context(nc.sbuf_tensor("ab_sl", [128, 16], F32)),
                rm=ph.enter_context(nc.sbuf_tensor("ab_rm", [128, 1], F32)),
                nrm=ph.enter_context(nc.sbuf_tensor("ab_nrm", [128, 1], F32)),
                ls=ph.enter_context(nc.sbuf_tensor("ab_ls", [128, 1], F32)),
                lr=ph.enter_context(nc.sbuf_tensor("ab_lr", [128, 1], F32)),
                dg=ph.enter_context(nc.sbuf_tensor("ab_dg", [128, 128], BF16)),
                av=ph.enter_context(nc.sbuf_tensor("ab_av", [128, 8192], BF16)),
            )
            bufs["ptb"] = [ph.enter_context(nc.psum_tensor(f"ab_ptb{i}", [128, 512], F32)) for i in range(2)]
            un = P.sem("ab_un")
            for b in range(B):
                t0 = b * S
                P.wait("sync", un, None)
                P.wait("sync", sSt, None)
                P.dma("sync", bufs["q"][:].rearrange("p (dc t) -> p dc t", t=S), qb_d[:, t0:t0 + S].rearrange("(dc p) t -> p dc t", p=128), sX)
                P.dma("sync", bufs["k"][:].rearrange("p (dc t) -> p dc t", t=S), kb_d[:, t0:t0 + S].rearrange("(dc p) t -> p dc t", p=128), sX)
                P.dma("sync", bufs["v"][:].rearrange("p (tt v) -> p tt v", v=512), vb_d[b * 16:(b + 1) * 16, :, :].rearrange("tt p v -> p tt v"), sX)
                for eng in ("tensor", "vector", "scalar"):
                    P.wait(eng, sX, None)
                P.wait("vector", sSt, None)
                for h in range(2):
                    for qt in range(16):
                        unit_b(b, h, qt, bufs, un)
                P.wait("sync", sDV, None)
                for sh in range(2):
                    shard = b * 2 + sh
                    for r2 in range(4):
                        P.dma("sync", a2a_in[shard * 512 + r2 * 128:shard * 512 + (r2 + 1) * 128, :],
                              bufs["av"][:, r2 * 2048 + sh * 1024: r2 * 2048 + (sh + 1) * 1024], sSt)
            P.barrier("attn_b")

    def unit_b(b, h, qt, bufs, un):
        nk = qt + 1
        ng = (nk + 3) // 4
        ms_sb, p_sb, pt_sb, slots = bufs["ms"], bufs["p"], bufs["pt"], bufs["sl"]
        sc_bank = pb[2 + (qt % 2)]
        av_bank = pb[4 + (qt % 2)]
        ptb = bufs["ptb"][qt % 2]
        P.wait("tensor", sDV, None); P.wait("tensor", sAC, None)
        P.wait("vector", sPE, None); P.wait("vector", sAC, None)
        P.wait("scalar", sPE, None); P.wait("scalar", sDV, None)
        for g in range(ng):
            k0 = g * 512
            w = min(512, nk * 128 - k0)
            if g >= 1:
                P.wait("tensor", sDV, None)
            for dc in range(2):
                dcg = h * 2 + dc
                P.op("tensor", lambda e, k2=k0, w2=w, d2=dcg, d3=dc, q2=qt: e.matmul(
                    sc_bank[:, 0:w2], bufs["q"][:, d2 * S + q2 * 128:d2 * S + (q2 + 1) * 128],
                    bufs["k"][:, d2 * S + k2:d2 * S + k2 + w2],
                    start=(d3 == 0), stop=(d3 == 1)), inc=(sPE if dc == 1 else None))
            P.wait("vector", sPE, None)
            P.op("vector", lambda e, k2=k0, w2=w: e.tensor_scalar(
                out=ms_sb[:, k2:k2 + w2], in0=sc_bank[:, 0:w2], scalar1=1.0 / 16.0, scalar2=None,
                op0=ALU.mult), inc=sDV)
            if g == ng - 1:
                kd = nk * 128 - 128
                P.op("vector", lambda e, k2=kd: e.tensor_tensor(
                    ms_sb[:, k2:k2 + 128], ms_sb[:, k2:k2 + 128], causal_add[:], ALU.add), inc=sDV)
            P.op("vector", lambda e, k2=k0, w2=w, g2=g: e.reduce_max(
                slots[:, g2:g2 + 1], ms_sb[:, k2:k2 + w2], axis=AX.X), inc=sDV)
        P.wait("vector", sDV, None)
        P.op("vector", lambda e, g2=ng: e.reduce_max(bufs["rm"][:], slots[:, 0:g2], axis=AX.X), inc=sDV)
        P.wait("vector", sDV, None)
        P.op("vector", lambda e: e.tensor_scalar(
            out=bufs["nrm"][:], in0=bufs["rm"][:], scalar1=-1.0, scalar2=None, op0=ALU.mult), inc=sDV)
        P.wait("scalar", sDV, None)
        for g in range(ng):
            k0 = g * 512
            w = min(512, nk * 128 - k0)
            P.op("scalar", lambda e, k2=k0, w2=w, g2=g: e.activation(
                p_sb[:, k2:k2 + w2], ms_sb[:, k2:k2 + w2], AF.Exp, bias=bufs["nrm"][:],
                accum_out=slots[:, 8 + g2:9 + g2]), inc=sAC)
        P.wait("vector", sAC, None)
        P.op("vector", lambda e, g2=ng: e.reduce_sum(bufs["ls"][:], slots[:, 8:8 + g2], axis=AX.X), inc=sDV)
        P.wait("vector", sDV, None)
        P.op("vector", lambda e: e.reciprocal(bufs["lr"][:], bufs["ls"][:]), inc=sDV)
        P.wait("vector", sDV, None)
        P.op("vector", lambda e: e.tensor_scalar(
            out=bufs["dg"][:], in0=ident_bf[:], scalar1=bufs["lr"][:], scalar2=None, op0=ALU.mult), inc=sDV)
        P.wait("tensor", sDV, None)
        for kc in range(nk):
            ps = kc % 4
            P.op("tensor", lambda e, k2=kc, p2=ps: e.matmul(
                ptb[:, p2 * 128:(p2 + 1) * 128], p_sb[:, k2 * 128:(k2 + 1) * 128], bufs["dg"][:],
                start=True, stop=True), inc=(sPE if (ps == 3 or kc == nk - 1) else None))
            if ps == 3 or kc == nk - 1:
                P.wait("scalar", sPE, None)
                wdt = (ps + 1) * 128
                P.op("scalar", lambda e, w2=wdt: e.copy(pt_sb[:, 0:w2], ptb[:, 0:w2]), inc=sAC)
                P.wait("tensor", sAC, None)
                base = kc - ps
                for j in range(ps + 1):
                    kcj = base + j
                    for vc in range(2):
                        vcg = h * 2 + vc
                        P.op("tensor", lambda e, k2=kcj, j2=j, v2=vcg, v3=vc: e.matmul(
                            av_bank[:, v3 * 128:(v3 + 1) * 128],
                            bufs["v"][:, k2 * 512 + v2 * 128:k2 * 512 + (v2 + 1) * 128],
                            pt_sb[:, j2 * 128:(j2 + 1) * 128],
                            start=(k2 == 0 and v3 == 0), stop=(k2 == nk - 1)), inc=(sPE if (kcj == nk - 1 and vc == 1) else None))
        P.wait("vector", sPE, None)
        for vc in range(2):
            r2 = h * 2 + vc
            P.op("vector", lambda e, r3=r2, q2=qt, v3=vc: e.tensor_copy(
                bufs["av"][:, r3 * 2048 + q2 * 128: r3 * 2048 + (q2 + 1) * 128],
                av_bank[:, v3 * 128:(v3 + 1) * 128]), inc=(sDV if vc == 1 else None))
        P.op("vector", lambda e: e.nop(), inc=un)

    def _ffn_block(xold, xnew, name):
        with ExitStack() as es:
            afm = es.enter_context(nc.sbuf_tensor("afm" + name, [128, NEC * T_OWN], BF16))
            ln_phase(xold, "s" + name, to_agin=False, afm=afm)
            ffn_phase(xold, xnew, name, afm)

    seq = [
        ("ln1", lambda: ln_phase(d["x_own"], "s1", to_agin=True)),
        ("ag1", allgather),
        ("proj_a", proj_a),
        ("attn_a", attn_a),
        ("a2a1", alltoall),
        ("wo_a", lambda: wo_phase("woaT", d["x_own"], xr[0], "a")),
        ("ffn1", lambda: _ffn_block(xr[0], xr[1], "f1")),
        ("ln3", lambda: ln_phase(xr[1], "s3", to_agin=True)),
        ("ag2", allgather),
        ("proj_b", proj_b),
        ("attn_b", attn_b),
        ("a2a2", alltoall),
        ("wo_b", lambda: wo_phase("wob", xr[1], xr[2], "b")),
        ("ffn2", lambda: _ffn_block(xr[2], y, "f2")),
    ]
    limit = len(seq) if _PHASE_LIMIT is None else _PHASE_LIMIT
    for nm, th in seq[:limit]:
        th()

    dumpable = dict(wdbg=wdbg, xlndbg=xlndbg, ag_in=ag_in, ag_out=ag_out, a2a_in=a2a_in, a2a_out=a2a_out,
                    xr0=xr[0], xr1=xr[1], xr2=xr[2], qa_d=qa_d, qha_d=qha_d,
                    ka_d=ka_d, kha_d=kha_d, va_d=va_d, qb_d=qb_d, kb_d=kb_d,
                    vb_d=vb_d, beff_d=beff_d)
    for nm in _DEBUG_DUMPS:
        ap = dumpable[nm]
        dt_ = ap.tensor.dtype
        od = nc.dram_tensor("dump_" + nm, list(ap.tensor.shape), dt_, kind="ExternalOutput").ap()
        P.dma("sync", od[...], ap[...], sSt)

    P.wait("sync", sSt, None)
    P.run()
    stack.close()
    return nc


def make_core_inputs(c, inputs, shared):
    ii = {}
    f32 = np.float32
    g = inputs
    x = np.asarray(g["L0_attn_a_hidden_states"], f32).reshape(T_ALL, E)
    ii["x_own"] = np.ascontiguousarray(x[c * T_OWN:(c + 1) * T_OWN])
    wq = np.asarray(g["L0_attn_a_Wq"], f32)
    wk = np.asarray(g["L0_attn_a_Wk"], f32)
    wv = np.asarray(g["L0_attn_a_Wv"], f32)
    ii["wqkva"] = np.ascontiguousarray(np.concatenate(
        [wq[c * 512:(c + 1) * 512, :].T, wk[c * 128:(c + 1) * 128, :].T, wv[c * 128:(c + 1) * 128, :].T], axis=1))
    ii["hp1"] = np.ascontiguousarray(np.asarray(g["L0_attn_a_hash_proj1"], f32)[4 * c:4 * c + 4].transpose(1, 0, 2).reshape(128, 512))
    ii["hb1"] = np.ascontiguousarray(np.asarray(g["L0_attn_a_hash_bias1"], f32)[4 * c:4 * c + 4].reshape(4, 128).T)
    ii["hp2"] = np.ascontiguousarray(np.asarray(g["L0_attn_a_hash_proj2"], f32)[4 * c:4 * c + 4].transpose(1, 0, 2).reshape(128, 512))
    ii["hb2"] = np.ascontiguousarray(np.asarray(g["L0_attn_a_hash_bias2"], f32)[4 * c:4 * c + 4].reshape(4, 128).T)
    ii["woaT"] = shared["woaT"]
    wqb = np.asarray(g["L0_attn_b_Wq"], f32)[:, c * 512:(c + 1) * 512]
    wkb = np.asarray(g["L0_attn_b_Wk"], f32)[:, c * 512:(c + 1) * 512]
    wvb = np.asarray(g["L0_attn_b_Wv"], f32)[:, c * 512:(c + 1) * 512]
    ii["wqkvb"] = np.ascontiguousarray(np.concatenate([wqb, wkb, wvb], axis=1))
    ii["wob"] = shared["wob"]
    ii["fker"] = shared["fker"]
    ii["fbias"] = shared["fbias"]
    ii["gate"] = np.ascontiguousarray(np.asarray(g["L0_ffn_gate"], f32)[c].reshape(1, 16))
    ii["gAT"] = shared["gAT"]
    ii["gBT"] = shared["gBT"]
    ii["asT"] = shared["asT"]
    ii["bsT"] = shared["bsT"]
    for k, v in CST.items():
        ii[k] = v
    return ii


def kernel(**inputs):
    f32 = np.float32
    la = np.asarray(inputs["L0_ffn_lora_A"], f32)
    lb = np.asarray(inputs["L0_ffn_lora_B"], f32)
    gAT = la.transpose(0, 2, 1).reshape(16, NEC, 128, RANK).transpose(2, 0, 1, 3).reshape(128, 16 * NEC * RANK)
    gBT = lb.transpose(0, 2, 1).reshape(16, 128, 1024).transpose(1, 0, 2).reshape(128, 16 * 1024)
    asv = np.asarray(inputs["L0_ffn_lora_A_shared"], f32)
    asT = asv.T.reshape(NEC, 128, RANK).transpose(1, 0, 2).reshape(128, NEC * RANK)
    shared = {
        "woaT": np.ascontiguousarray(np.asarray(inputs["L0_attn_a_Wo"], f32).T),
        "wob": np.ascontiguousarray(np.asarray(inputs["L0_attn_b_Wo"], f32)),
        "fker": np.ascontiguousarray(np.asarray(inputs["L0_ffn_kernel"], f32)),
        "fbias": np.ascontiguousarray(np.asarray(inputs["L0_ffn_bias"], f32).reshape(1, E)),
        "gAT": np.ascontiguousarray(gAT),
        "gBT": np.ascontiguousarray(gBT),
        "asT": np.ascontiguousarray(asT),
        "bsT": np.ascontiguousarray(np.asarray(inputs["L0_ffn_lora_B_shared"], f32).T),
    }
    nc = build_nc()
    in_maps = [make_core_inputs(c, inputs, shared) for c in range(N_CORES)]
    res = run_bass_kernel_spmd(nc, in_maps, core_ids=list(range(N_CORES)))
    out = np.concatenate([res.results[c]["y"] for c in range(N_CORES)], axis=0)
    return out.reshape(B, S, E)


# version 05
# speedup vs baseline: 1.1678x; 1.0824x over previous
import numpy as np
import ml_dtypes
from contextlib import ExitStack

import concourse.bass as bass
import concourse.mybir as mybir
from concourse.bass_utils import run_bass_kernel_spmd
from concourse.aot_env import allow_ici_all_to_all

F32 = mybir.dt.float32
BF16 = mybir.dt.bfloat16
FP8 = mybir.dt.float8e4
AF = mybir.ActivationFunctionType
ALU = mybir.AluOpType
AX = mybir.AxisListType

N_CORES = 8
B, S, E = 4, 2048, 4096
T_ALL = B * S
T_OWN = T_ALL // N_CORES
NTT_OWN = T_OWN // 128
NEC = E // 128
HA_D = 128
ROT = 64
NUM_REMAIN = 410
NEG = -3.0e8
LN_EPS = 1e-5
RANK = 32
BF = ml_dtypes.bfloat16
_PHASE_LIMIT = None
_DEBUG_DUMPS = []
_SERIALIZE_UNITS = False
_SER_LN = False
_SER_AX = False


def _ppf(p):
    p = np.asarray(p, np.float64)
    a = [-3.969683028665376e+01, 2.209460984245205e+02, -2.759285104469687e+02,
         1.383577518672690e+02, -3.066479806614716e+01, 2.506628277459239e+00]
    b = [-5.447609879822406e+01, 1.615858368580409e+02, -1.556989798598866e+02,
         6.680131188771972e+01, -1.328068155288572e+01]
    c = [-7.784894002430293e-03, -3.223964580411365e-01, -2.400758277161838e+00,
         -2.549732539343734e+00, 4.374664141464968e+00, 2.938163982698783e+00]
    d = [7.784695709041462e-03, 3.224671290700398e-01, 2.445134137142996e+00,
         3.754408661907416e+00]
    out = np.empty_like(p)
    lo = p < 0.02425
    hi = p > 1 - 0.02425
    mid = ~(lo | hi)
    q = np.sqrt(-2 * np.log(np.where(lo, p, 0.5)))
    out_lo = (((((c[0] * q + c[1]) * q + c[2]) * q + c[3]) * q + c[4]) * q + c[5]) / \
             ((((d[0] * q + d[1]) * q + d[2]) * q + d[3]) * q + 1)
    q = p - 0.5
    r = q * q
    out_mid = (((((a[0] * r + a[1]) * r + a[2]) * r + a[3]) * r + a[4]) * r + a[5]) * q / \
              (((((b[0] * r + b[1]) * r + b[2]) * r + b[3]) * r + b[4]) * r + 1)
    q = np.sqrt(-2 * np.log1p(-np.where(hi, p, 0.5)))
    out_hi = -(((((c[0] * q + c[1]) * q + c[2]) * q + c[3]) * q + c[4]) * q + c[5]) / \
             ((((d[0] * q + d[1]) * q + d[2]) * q + d[3]) * q + 1)
    out[lo] = out_lo[lo]
    out[mid] = out_mid[mid]
    out[hi] = out_hi[hi]
    return out


def make_constants():
    cst = {}
    cst["ident_bf"] = np.eye(128).astype(BF)
    tri = np.tril(np.ones((128, 128), np.float32))
    cst["causal01"] = tri
    cst["causal_add"] = np.where(tri > 0, 0.0, NEG).astype(np.float32)
    inv = 1.0 / 10000.0 ** (np.arange(0, HA_D, 2, dtype=np.float64) / HA_D)
    t = np.arange(S, dtype=np.float64)
    emb = np.concatenate([np.outer(t, inv)] * 2, -1)
    cst["cosA"] = np.ascontiguousarray(
        np.cos(emb).astype(np.float32).reshape(16, 128, HA_D).transpose(1, 0, 2).reshape(128, 16 * HA_D))
    cst["sinA"] = np.ascontiguousarray(
        np.sin(emb).astype(np.float32).reshape(16, 128, HA_D).transpose(1, 0, 2).reshape(128, 16 * HA_D))
    invb = 1.0 / 10000.0 ** (np.arange(0, ROT, 2, dtype=np.float64) / ROT)
    sinu = np.outer(t, invb)
    cos2 = np.repeat(np.cos(sinu), 2, axis=1).astype(np.float32)
    sin2 = np.repeat(np.sin(sinu), 2, axis=1).astype(np.float32)
    cst["cosB"] = np.ascontiguousarray(cos2.reshape(16, 128, ROT).transpose(1, 0, 2).reshape(128, 16 * ROT))
    cst["sinB"] = np.ascontiguousarray(sin2.reshape(16, 128, ROT).transpose(1, 0, 2).reshape(128, 16 * ROT))
    n = np.arange(S, dtype=np.float64) + 1.0
    z = np.full(S, -25.0)
    big = n > NUM_REMAIN
    z[big] = _ppf(1.0 - NUM_REMAIN / n[big])
    cst["ztab"] = np.ascontiguousarray(z.reshape(16, 128).T.astype(np.float32))
    cst["invn"] = np.ascontiguousarray((1.0 / n).reshape(16, 128).T.astype(np.float32))
    return cst


CST = make_constants()
ENGINES = ["tensor", "vector", "scalar", "gpsimd", "sync"]


class Sem:
    def __init__(self, handle):
        self.h = handle
        self.n = 0


class Prog:
    def __init__(self, nc, stack):
        self.nc = nc
        self.ops = {e: [] for e in ENGINES}
        self.stack = stack
        self.sems = {}

    def sem(self, name):
        if name not in self.sems:
            self.sems[name] = Sem(self.stack.enter_context(self.nc.semaphore(name)))
        return self.sems[name]

    def op(self, eng, fn, inc=None, inc_n=1):
        if inc is not None:
            inc.n += inc_n
            self.ops[eng].append(lambda e, f=fn, s=inc.h, k=inc_n: f(e).then_inc(s, k))
        else:
            self.ops[eng].append(fn)

    def dma(self, eng, dst, src, sem):
        self.op(eng, lambda e, a=dst, b=src: e.dma_start(out=a, in_=b), inc=sem, inc_n=16)

    def wait(self, eng, sem, val=None):
        v = sem.n if val is None else val
        self.ops[eng].append(lambda e, s=sem.h, vv=v: e.wait_ge(s, vv))

    def barrier(self, name, engines=ENGINES):
        bar = self.sem("bar")
        for eng in engines:
            self.op(eng, lambda e: e.nop(), inc=bar)
        tgt = bar.n
        for eng in engines:
            self.wait(eng, bar, tgt)

    def run(self):
        with self.nc.Block() as block:
            for eng in ENGINES:
                def make(eng_name):
                    def body(e):
                        for fn in self.ops[eng_name]:
                            fn(e)
                    return body
                getattr(block, eng)(make(eng))


def build_nc():
    nc = bass.Bass("TRN2", target_bir_lowering=False, debug=False)
    d = {}

    def di(name, shape, dtype=F32):
        d[name] = nc.dram_tensor(name, list(shape), dtype, kind="ExternalInput").ap()
        return d[name]

    di("x_own", (T_OWN, E))
    di("wqkva", (E, 768))
    di("hp1", (128, 4 * 128)); di("hb1", (128, 4))
    di("hp2", (128, 4 * 128)); di("hb2", (128, 4))
    di("woaT", (E, E))
    di("wqkvb", (E, 1536))
    di("wob", (E, E))
    di("fker", (E, E)); di("fbias", (1, E))
    di("gate", (1, 16))
    di("gAT", (128, 16 * NEC * RANK))
    di("gBT", (128, 16 * 1024))
    di("asT", (128, NEC * RANK))
    di("bsT", (RANK, E))
    di("ident_bf", (128, 128), BF16)
    di("causal01", (128, 128))
    di("causal_add", (128, 128))
    di("cosA", (128, 16 * 128)); di("sinA", (128, 16 * 128))
    di("cosB", (128, 16 * ROT)); di("sinB", (128, 16 * ROT))
    di("ztab", (128, 16)); di("invn", (128, 16))
    y = nc.dram_tensor("y", [T_OWN, E], F32, kind="ExternalOutput").ap()

    ag_in = nc.dram_tensor("ag_in", [E, T_OWN], BF16).ap()
    ag_out = nc.dram_tensor("ag_out", [N_CORES * E, T_OWN], BF16).ap()
    a2a_in = nc.dram_tensor("a2a_in", [N_CORES * 512, T_OWN], BF16).ap()
    a2a_out = nc.dram_tensor("a2a_out", [N_CORES * 512, T_OWN], BF16).ap()
    xr = [nc.dram_tensor(f"xr{i}", [T_OWN, E], F32).ap() for i in range(3)]
    qa_d = nc.dram_tensor("qa_d", [512, T_ALL], BF16).ap()
    qha_d = nc.dram_tensor("qha_d", [512, T_ALL], FP8).ap()
    ka_d = nc.dram_tensor("ka_d", [128, T_ALL], BF16).ap()
    kha_d = nc.dram_tensor("kha_d", [512, T_ALL], FP8).ap()
    va_d = nc.dram_tensor("va_d", [T_ALL // 128, 128, 128], BF16).ap()
    qb_d = nc.dram_tensor("qb_d", [512, T_ALL], BF16).ap()
    kb_d = nc.dram_tensor("kb_d", [512, T_ALL], BF16).ap()
    vb_d = nc.dram_tensor("vb_d", [T_ALL // 128, 128, 512], BF16).ap()
    beff_d = nc.dram_tensor("beff_d", [128, RANK * E // 128], F32).ap()
    xlndbg = nc.dram_tensor("xlndbg", [T_OWN, E], BF16).ap()
    wdbg = nc.dram_tensor("wdbg", [128, NEC * 256], BF16).ap()

    stack = ExitStack()
    P = Prog(nc, stack)
    e_ = stack.enter_context

    ident_bf = e_(nc.sbuf_tensor("c_ident", [128, 128], BF16))
    causal01 = e_(nc.sbuf_tensor("c_c01", [128, 128], F32))
    causal_add = e_(nc.sbuf_tensor("c_cadd", [128, 128], F32))
    cosA = e_(nc.sbuf_tensor("c_cosA", [128, 16 * 128], F32))
    sinA = e_(nc.sbuf_tensor("c_sinA", [128, 16 * 128], F32))
    cosB = e_(nc.sbuf_tensor("c_cosB", [128, 16 * ROT], F32))
    sinB = e_(nc.sbuf_tensor("c_sinB", [128, 16 * ROT], F32))
    ztab = e_(nc.sbuf_tensor("c_ztab", [128, 16], F32))
    invn = e_(nc.sbuf_tensor("c_invn", [128, 16], F32))
    ones_bf = e_(nc.sbuf_tensor("c_onesbf", [1, 128], BF16))
    ones_f = e_(nc.sbuf_tensor("c_onesf", [1, 128], F32))
    gate_row = e_(nc.sbuf_tensor("c_gater", [1, 16], F32))
    gbc = e_(nc.sbuf_tensor("c_gbc", [128, 16], F32))
    fbias_sb = e_(nc.sbuf_tensor("c_fbias", [1, E], BF16))
    hb1_sb = e_(nc.sbuf_tensor("c_hb1", [128, 4], F32))
    hb2_sb = e_(nc.sbuf_tensor("c_hb2", [128, 4], F32))
    hp1_sb = e_(nc.sbuf_tensor("c_hp1", [128, 4 * 128], BF16))
    hp2_sb = e_(nc.sbuf_tensor("c_hp2", [128, 4 * 128], BF16))
    abT_sb = e_(nc.sbuf_tensor("c_abT", [128, NEC * 64], BF16))
    bbT_sb = e_(nc.sbuf_tensor("c_bbT", [64, E], BF16))

    pb = [e_(nc.psum_tensor(f"pb{i}", [128, 512], F32)) for i in range(6)]

    sW = P.sem("sW"); sX = P.sem("sX"); sSt = P.sem("sSt")
    sWp = [P.sem("sW0"), P.sem("sW1")]
    sXp = [P.sem("sX0"), P.sem("sX1")]
    sPE = P.sem("sPE"); sDV = P.sem("sDV"); sAC = P.sem("sAC")
    sCC = P.sem("sCC"); sSc = P.sem("sSc")

    P.dma("sync", ident_bf[:], d["ident_bf"][:], sX)
    P.dma("sync", causal01[:], d["causal01"][:], sX)
    P.dma("sync", causal_add[:], d["causal_add"][:], sX)
    P.dma("sync", cosA[:], d["cosA"][:], sX)
    P.dma("sync", sinA[:], d["sinA"][:], sX)
    P.dma("sync", cosB[:], d["cosB"][:], sX)
    P.dma("sync", sinB[:], d["sinB"][:], sX)
    P.dma("sync", ztab[:], d["ztab"][:], sX)
    P.dma("sync", invn[:], d["invn"][:], sX)
    P.dma("sync", gate_row[:], d["gate"][:], sX)
    P.dma("sync", hb1_sb[:, :], d["hb1"][:], sX)
    P.dma("sync", hb2_sb[:, :], d["hb2"][:], sX)
    P.dma("gpsimd", fbias_sb[:], d["fbias"][:], sW)
    P.dma("gpsimd", hp1_sb[:], d["hp1"][:], sW)
    P.dma("gpsimd", hp2_sb[:], d["hp2"][:], sW)
    P.op("gpsimd", lambda e: e.memset(ones_bf[:], 1.0), inc=sW, inc_n=1)
    P.op("gpsimd", lambda e: e.memset(ones_f[:], 1.0), inc=sW, inc_n=1)

    with ExitStack() as ph:
        lA = ph.enter_context(nc.sbuf_tensor("l_A", [128, 16 * 1024], BF16))
        acc = [ph.enter_context(nc.sbuf_tensor(f"l_acc{i}", [128, 1024], F32)) for i in range(2)]
        asb = ph.enter_context(nc.sbuf_tensor("l_as", [128, 1024], BF16))
        P.dma("gpsimd", lA[:], d["gAT"][:], sW)
        P.dma("gpsimd", asb[:], d["asT"][:], sW)
        P.wait("tensor", sX, None)
        P.wait("tensor", sW, None)
        P.op("tensor", lambda e: e.matmul(pb[0][:, 0:16], ones_f[:, :], gate_row[:, :], start=True, stop=True), inc=sPE)
        P.wait("vector", sPE, None)
        P.op("vector", lambda e: e.tensor_copy(gbc[:], pb[0][:, 0:16]), inc=sDV)
        P.wait("vector", sW, None)
        P.op("vector", lambda e: e.tensor_scalar(out=acc[0][:], in0=lA[:, 0:1024], scalar1=gbc[:, 0:1], scalar2=None, op0=ALU.mult))
        for n in range(1, 16):
            P.op("vector", lambda e, n2=n: e.scalar_tensor_tensor(
                out=acc[0][:], in0=lA[:, n2 * 1024:(n2 + 1) * 1024], scalar=gbc[:, n2:n2 + 1],
                in1=acc[0][:], op0=ALU.mult, op1=ALU.add))
        P.op("vector", lambda e: e.tensor_copy(
            abT_sb[:].rearrange("p (c w) -> p c w", w=64)[:, :, 0:32],
            acc[0][:].rearrange("p (c r) -> p c r", r=32)))
        P.op("vector", lambda e: e.tensor_copy(
            abT_sb[:].rearrange("p (c w) -> p c w", w=64)[:, :, 32:64],
            asb[:].rearrange("p (c r) -> p c r", r=32)), inc=sDV)
        P.wait("gpsimd", sDV, None)
        P.dma("gpsimd", lA[:], d["gBT"][:], sW)
        P.wait("vector", sW, None)
        P.op("vector", lambda e: e.tensor_scalar(out=acc[1][:], in0=lA[:, 0:1024], scalar1=gbc[:, 0:1], scalar2=None, op0=ALU.mult))
        for n in range(1, 16):
            P.op("vector", lambda e, n2=n: e.scalar_tensor_tensor(
                out=acc[1][:], in0=lA[:, n2 * 1024:(n2 + 1) * 1024], scalar=gbc[:, n2:n2 + 1],
                in1=acc[1][:], op0=ALU.mult, op1=ALU.add), inc=(sDV if n == 15 else None))
        P.wait("sync", sDV, None)
        P.dma("sync", beff_d[:], acc[1][:], sSt)
        P.wait("gpsimd", sSt, None)
        P.dma("gpsimd", bbT_sb[0:RANK, :], beff_d.rearrange("(r s) w -> r (s w)", s=4), sW)
        P.dma("gpsimd", bbT_sb[RANK:2 * RANK, :], d["bsT"][:], sW)
        P.barrier("prologue")

    def wait_stores(eng):
        for nm, s in list(P.sems.items()):
            if nm.startswith(("wostp", "ffstp")) or nm == "sSt":
                P.wait(eng, s, s.n)

    def ln_phase(src_dram, name, to_agin, afm=None):
        with ExitStack() as ph:
            xf32 = [ph.enter_context(nc.sbuf_tensor(f"ln{name}_x{i}", [128, E], F32)) for i in range(2)]
            lnst = ph.enter_context(nc.sbuf_tensor("ln_st" + name, [128, 48], F32))
            lnjk = ph.enter_context(nc.sbuf_tensor("ln_jk" + name, [128, 512], BF16))
            lnagg = ph.enter_context(nc.sbuf_tensor("ln_agg" + name, [128, 2], F32))
            lnsl = ph.enter_context(nc.sbuf_tensor("ln_sl" + name, [128, 8], F32))
            lnr = ph.enter_context(nc.sbuf_tensor("ln_r" + name, [128, 1], F32))
            lntmp = ph.enter_context(nc.sbuf_tensor("ln_tmp" + name, [128, 1], F32))
            xln = ph.enter_context(nc.sbuf_tensor("ln_xb" + name, [128, E], BF16))
            tevac = [ph.enter_context(nc.sbuf_tensor(f"ln{name}_tv{i}", [128, 512], BF16)) for i in range(2)]
            pt8 = [ph.enter_context(nc.psum_tensor(f"ln{name}_pt{i}", [128, 1024], BF16)) for i in range(2)]
            lnu = P.sem("lnu_" + name)
            wait_stores("sync")
            for tt in range(NTT_OWN):
                if _SER_LN:
                    P.barrier("serln")
                buf = xf32[tt % 2]
                if tt >= 2:
                    P.wait("sync", lnu, tt - 1)
                P.dma("sync", buf[:], src_dram[tt * 128:(tt + 1) * 128, :], sXp[tt % 2])
                P.wait("vector", sXp[tt % 2], None)
                for c8 in range(8):
                    P.op("vector", lambda e, b2=buf, c=c8: e.bn_stats(
                        lnst[:, c * 6:(c + 1) * 6], b2[:, c * 512:(c + 1) * 512]))
                P.op("vector", lambda e: e.tensor_copy(lnjk[:, 0:256], xln[:, 0:256]))
                P.op("vector", lambda e: e.tensor_copy(lnjk[:, 256:512], xln[:, 256:512]), inc=sDV)
                P.wait("vector", sDV, None)
                P.op("vector", lambda e: e.bn_aggr(lnagg[:], lnst[:]))
                P.op("vector", lambda e: e.tensor_copy(lnjk[:, 0:256], xln[:, 1024:1280]))
                P.op("vector", lambda e: e.tensor_copy(lnjk[:, 256:512], xln[:, 1280:1536]), inc=sDV)
                P.wait("vector", sDV, None)
                P.op("vector", lambda e: e.tensor_copy(lnsl[:, 2:3], lnagg[:, 0:1]))
                P.op("vector", lambda e: e.tensor_scalar(
                    out=lnsl[:, 5:6], in0=lnagg[:, 1:2], scalar1=LN_EPS, scalar2=None, op0=ALU.add))
                P.op("vector", lambda e: e.tensor_copy(lnjk[:, 0:256], xln[:, 512:768]))
                P.op("vector", lambda e: e.tensor_copy(lnjk[:, 256:512], xln[:, 768:1024]), inc=sDV)
                P.wait("scalar", sDV, None)
                P.op("scalar", lambda e: e.activation(lntmp[:], lnsl[:, 5:6], AF.Ln), inc=sAC)
                P.wait("scalar", sAC, None)
                P.op("scalar", lambda e: e.activation(lnr[:], lntmp[:], AF.Exp, scale=-0.5))
                P.op("scalar", lambda e: e.activation(lnsl[:, 7:8], lnsl[:, 6:7], AF.Ln))
                P.op("scalar", lambda e: e.activation(lnsl[:, 7:8], lnsl[:, 6:7], AF.Ln), inc=sAC)
                P.wait("vector", sAC, None)
                P.wait("vector", sPE, None)
                P.op("vector", lambda e, b2=buf: e.tensor_scalar(
                    out=xln[:], in0=b2[:], scalar1=lnsl[:, 2:3], scalar2=lnr[:],
                    op0=ALU.subtract, op1=ALU.mult))
                P.op("vector", lambda e: e.tensor_copy(lnjk[:, 0:512], xln[:, 0:512]))
                P.op("vector", lambda e: e.tensor_copy(lnjk[:, 0:512], xln[:, 2048:2560]))
                P.op("vector", lambda e: e.tensor_copy(lnjk[:, 0:512], xln[:, E - 512:E]), inc=sDV)
                P.op("vector", lambda e: e.nop(), inc=lnu)
                P.wait("tensor", sDV, None)
                for g in range(4):
                    bank = pt8[g % 2]
                    P.wait("tensor", sAC, None)
                    for k in range(8):
                        ec = g * 8 + k
                        P.op("tensor", lambda e, bk=bank, c=ec, k2=k: e.transpose(
                            bk[:, k2 * 128:(k2 + 1) * 128], xln[:, c * 128:(c + 1) * 128], ident_bf[:]),
                            inc=(sPE if k == 7 else None))
                    P.wait("scalar", sPE, None)
                    if to_agin:
                        P.wait("scalar", sSt, None)
                        for half in range(2):
                            P.op("scalar", lambda e, bk=bank, h=half: e.copy(
                                tevac[h][:], bk[:, h * 512:(h + 1) * 512]), inc=sAC)
                        P.wait("sync", sAC, None)
                        for half in range(2):
                            ec0 = g * 8 + half * 4
                            P.dma("sync",
                                  ag_in[ec0 * 128:(ec0 + 4) * 128, tt * 128:(tt + 1) * 128].rearrange("(c p) t -> p c t", p=128),
                                  tevac[half][:].rearrange("p (c t) -> p c t", t=128), sSt)
                    else:
                        for half in range(2):
                            ec0 = g * 8 + half * 4
                            dst = afm[:].rearrange("p (c t) -> p c t", t=T_OWN)[:, ec0:ec0 + 4, tt * 128:(tt + 1) * 128]
                            P.op("scalar", lambda e, bk=bank, h=half, dd=dst: e.copy(
                                dd, bk[:, h * 512:(h + 1) * 512]), inc=sAC)
            P.barrier("ln_" + name)

    def allgather():
        P.wait("gpsimd", sSt, None)
        P.op("gpsimd", lambda e: e.collective_compute(
            "AllGather", ALU.bypass, replica_groups=[list(range(N_CORES))],
            ins=[ag_in.opt()], outs=[ag_out.opt()]), inc=sCC)
        for eng in ENGINES:
            P.wait(eng, sCC, None)

    def alltoall():
        P.wait("gpsimd", sSt, None)
        def go(e):
            with allow_ici_all_to_all():
                return e.collective_compute(
                    "AllToAll", ALU.bypass, replica_groups=[list(range(N_CORES))],
                    ins=[a2a_in.opt()], outs=[a2a_out.opt()])
        P.op("gpsimd", go, inc=sCC)
        for eng in ENGINES:
            P.wait(eng, sCC, None)

    def rope_a(sc, dst, psrc, pc, sbase, dbase):
        cA = cosA[:, pc * 128:(pc + 1) * 128]
        sA = sinA[:, pc * 128:(pc + 1) * 128]
        rt = sc["ropet"]
        P.op("vector", lambda e: e.tensor_tensor(rt[:, 0:128], psrc[:, sbase:sbase + 128], cA, ALU.mult))
        P.op("vector", lambda e: e.tensor_tensor(rt[:, 128:192], psrc[:, sbase + 64:sbase + 128], sA[:, 0:64], ALU.mult))
        P.op("vector", lambda e: e.tensor_tensor(rt[:, 192:256], psrc[:, sbase:sbase + 64], sA[:, 64:128], ALU.mult))
        P.op("vector", lambda e: e.tensor_sub(dst[:, dbase:dbase + 64], rt[:, 0:64], rt[:, 128:192]))
        P.op("vector", lambda e: e.tensor_add(dst[:, dbase + 64:dbase + 128], rt[:, 64:128], rt[:, 192:256]))

    def hash_head(sc, h, src_fm, dst):
        hsil, hy, hz = sc["hsil"], sc["hy"], sc["hz"]
        P.op("tensor", lambda e: e.matmul(pb[4][:], hp1_sb[:, h * 128:(h + 1) * 128], src_fm, start=True, stop=True), inc=sPE)
        P.wait("scalar", sPE, None)
        P.op("scalar", lambda e: e.activation(hsil[:], pb[4][:], AF.Silu, bias=hb1_sb[:, h:h + 1]), inc=sAC)
        P.wait("vector", sAC, None)
        P.op("vector", lambda e: e.tensor_tensor(hy[:], hsil[:], src_fm, ALU.add), inc=sDV)
        P.wait("tensor", sDV, None)
        P.op("tensor", lambda e: e.matmul(pb[5][:], hp2_sb[:, h * 128:(h + 1) * 128], hy[:], start=True, stop=True), inc=sPE)
        P.wait("vector", sPE, None)
        P.op("vector", lambda e: e.scalar_tensor_tensor(
            out=hz[:], in0=pb[5][:], scalar=hb2_sb[:, h:h + 1], in1=hy[:],
            op0=ALU.add, op1=ALU.add), inc=sDV)
        P.wait("scalar", sDV, None)
        P.op("scalar", lambda e: e.activation(dst, hz[:], AF.Sign), inc=sAC)

    def proj_a():
        with ExitStack() as ph:
            wsb = ph.enter_context(nc.sbuf_tensor("pa_w", [128, NEC * 768], BF16))
            xsl = [ph.enter_context(nc.sbuf_tensor(f"pa_x{i}", [128, NEC * 512], BF16)) for i in range(2)]
            sc = dict(
                qr=ph.enter_context(nc.sbuf_tensor("pa_qr", [128, 512], BF16)),
                kr=ph.enter_context(nc.sbuf_tensor("pa_kr", [128, 128], BF16)),
                ropet=ph.enter_context(nc.sbuf_tensor("pa_rt", [128, 256], F32)),
                qT=ph.enter_context(nc.sbuf_tensor("pa_qT", [128, 4 * 512], BF16)),
                kT=ph.enter_context(nc.sbuf_tensor("pa_kT", [128, 512], BF16)),
                v=ph.enter_context(nc.sbuf_tensor("pa_v", [128, 128], BF16)),
                hsil=ph.enter_context(nc.sbuf_tensor("pa_hs", [128, 512], F32)),
                hy=ph.enter_context(nc.sbuf_tensor("pa_hy", [128, 512], BF16)),
                hz=ph.enter_context(nc.sbuf_tensor("pa_hz", [128, 512], F32)),
                qh=ph.enter_context(nc.sbuf_tensor("pa_qh", [128, 4 * 512], FP8)),
                kh=ph.enter_context(nc.sbuf_tensor("pa_kh", [128, 4 * 512], FP8)),
            )
            pt8 = [ph.enter_context(nc.psum_tensor(f"pa_pt{i}", [128, 1024], BF16)) for i in range(2)]
            for ec in range(NEC):
                P.dma("gpsimd", wsb[:, ec * 768:(ec + 1) * 768], d["wqkva"][ec * 128:(ec + 1) * 128, :], sW)
            evt = P.sem("pa_evt")
            for hs in range(16):
                buf = xsl[hs % 2]
                r, half = hs // 2, hs % 2
                if hs >= 2:
                    P.wait("sync", sPE, None)
                for ec in range(NEC):
                    P.dma("sync", buf[:, ec * 512:(ec + 1) * 512],
                          ag_out[r * E + ec * 128:r * E + (ec + 1) * 128, half * 512:(half + 1) * 512], sXp[hs % 2])
                P.wait("tensor", sXp[hs % 2], None)
                P.wait("tensor", sW, None)
                for tt4 in range(4):
                    kglob = hs * 4 + tt4
                    pc = kglob % 16
                    bq = pb[tt4 % 2]
                    bkv = pb[2 + tt4 % 2]
                    if kglob >= 2:
                        P.wait("tensor", evt, kglob - 1)
                    for ec in range(NEC):
                        xt = buf[:, ec * 512 + tt4 * 128: ec * 512 + (tt4 + 1) * 128]
                        P.op("tensor", lambda e, bk=bq, x=xt, ec2=ec: e.matmul(
                            bk[:], x, wsb[:, ec2 * 768:ec2 * 768 + 512],
                            start=(ec2 == 0), stop=(ec2 == NEC - 1)))
                    for ec in range(NEC):
                        xt = buf[:, ec * 512 + tt4 * 128: ec * 512 + (tt4 + 1) * 128]
                        P.op("tensor", lambda e, bk=bkv, x=xt, ec2=ec: e.matmul(
                            bk[:, 0:256], x, wsb[:, ec2 * 768 + 512:(ec2 + 1) * 768],
                            start=(ec2 == 0), stop=(ec2 == NEC - 1)), inc=(sPE if ec == NEC - 1 else None))
                    P.wait("vector", sPE, None)
                    P.wait("vector", sSt, None)
                    for h in range(4):
                        rope_a(sc, sc["qr"], bq, pc, h * 128, h * 128)
                    rope_a(sc, sc["kr"], bkv, pc, 0, 0)
                    P.op("vector", lambda e, bk=bkv: e.tensor_copy(sc["v"][:], bk[:, 128:256]), inc=sDV)
                    P.wait("tensor", sDV, None)
                    bank = pt8[tt4 % 2]
                    for h in range(4):
                        P.op("tensor", lambda e, bk=bank, h2=h: e.transpose(
                            bk[:, h2 * 128:(h2 + 1) * 128], sc["qr"][:, h2 * 128:(h2 + 1) * 128], ident_bf[:]))
                    P.op("tensor", lambda e, bk=bank: e.transpose(bk[:, 512:640], sc["kr"][:, 0:128], ident_bf[:]), inc=sPE)
                    P.wait("scalar", sPE, None)
                    for h in range(4):
                        P.op("scalar", lambda e, bk=bank, h2=h, t4=tt4: e.copy(
                            sc["qT"][:, h2 * 512 + t4 * 128: h2 * 512 + (t4 + 1) * 128], bk[:, h2 * 128:(h2 + 1) * 128]))
                    P.op("scalar", lambda e, bk=bank, t4=tt4: e.copy(
                        sc["kT"][:, t4 * 128:(t4 + 1) * 128], bk[:, 512:640]), inc=sAC)
                    P.wait("sync", sDV, None)
                    P.dma("sync", va_d[kglob, :, :], sc["v"][:], sSt)
                    P.op("scalar", lambda e: e.nop(), inc=evt)
                P.wait("tensor", sAC, None)
                for h in range(4):
                    hash_head(sc, h, sc["qT"][:, h * 512:(h + 1) * 512], sc["qh"][:, h * 512:(h + 1) * 512])
                for h in range(4):
                    hash_head(sc, h, sc["kT"][:, 0:512], sc["kh"][:, h * 512:(h + 1) * 512])
                P.wait("sync", sAC, None)
                t0 = hs * 512
                for h in range(4):
                    P.dma("sync", qa_d[h * 128:(h + 1) * 128, t0:t0 + 512], sc["qT"][:, h * 512:(h + 1) * 512], sSt)
                    P.dma("sync", qha_d[h * 128:(h + 1) * 128, t0:t0 + 512], sc["qh"][:, h * 512:(h + 1) * 512], sSt)
                    P.dma("sync", kha_d[h * 128:(h + 1) * 128, t0:t0 + 512], sc["kh"][:, h * 512:(h + 1) * 512], sSt)
                P.dma("sync", ka_d[:, t0:t0 + 512], sc["kT"][:, 0:512], sSt)
            P.barrier("proj_a")

    def attn_a():
        with ExitStack() as ph:
            q_sb = ph.enter_context(nc.sbuf_tensor("aa_q", [128, 4 * S], BF16))
            qh_sb = ph.enter_context(nc.sbuf_tensor("aa_qh", [128, 4 * S], FP8))
            k_sb = ph.enter_context(nc.sbuf_tensor("aa_k", [128, S], BF16))
            kh_sb = ph.enter_context(nc.sbuf_tensor("aa_kh", [128, 4 * S], FP8))
            v_sb = ph.enter_context(nc.sbuf_tensor("aa_v", [128, S], BF16))
            sim2 = [ph.enter_context(nc.sbuf_tensor(f"aa_sim{i}", [128, S], BF16)) for i in range(2)]
            ms2 = [ph.enter_context(nc.sbuf_tensor(f"aa_ms{i}", [128, S], F32)) for i in range(2)]
            p2 = [ph.enter_context(nc.sbuf_tensor(f"aa_p{i}", [128, S], BF16)) for i in range(2)]
            pt_sb = ph.enter_context(nc.sbuf_tensor("aa_pt", [128, 1024], BF16))
            sl2 = [ph.enter_context(nc.sbuf_tensor(f"aa_sl{i}", [128, 24], F32)) for i in range(2)]
            th2 = [ph.enter_context(nc.sbuf_tensor(f"aa_th{i}", [128, 4], F32)) for i in range(2)]
            dg2 = [ph.enter_context(nc.sbuf_tensor(f"aa_dg{i}", [128, 128], BF16)) for i in range(2)]
            mk2 = [ph.enter_context(nc.sbuf_tensor(f"aa_mk{i}", [128, 512], BF16)) for i in range(2)]
            jk2 = [ph.enter_context(nc.sbuf_tensor(f"aa_jk{i}", [128, 512], BF16)) for i in range(2)]
            avst = ph.enter_context(nc.sbuf_tensor("aa_av", [128, 8192], BF16))
            ptf = [ph.enter_context(nc.psum_tensor(f"aa_ptb{i}", [128, 512], F32)) for i in range(2)]
            bufs = dict(q=q_sb, qh=qh_sb, k=k_sb, kh=kh_sb, v=v_sb, sim=sim2, ms=ms2,
                        p=p2, pt=pt_sb, sl=sl2, th=th2, dg=dg2, mk=mk2, jk=jk2,
                        av=avst, ptf=ptf, hstride=S, D=128, scale=float(1.0 / np.sqrt(128.0)))
            run_attn(bufs, kind="a")

    def attn_b():
        with ExitStack() as ph:
            q_sb = ph.enter_context(nc.sbuf_tensor("ab_q", [128, 4 * S], BF16))
            k_sb = ph.enter_context(nc.sbuf_tensor("ab_k", [128, 4 * S], BF16))
            v_sb = ph.enter_context(nc.sbuf_tensor("ab_v", [128, 4 * S], BF16))
            ms2 = [ph.enter_context(nc.sbuf_tensor(f"ab_ms{i}", [128, S], F32)) for i in range(2)]
            p2 = [ph.enter_context(nc.sbuf_tensor(f"ab_p{i}", [128, S], BF16)) for i in range(2)]
            pt_sb = ph.enter_context(nc.sbuf_tensor("ab_pt", [128, 1024], BF16))
            sl2 = [ph.enter_context(nc.sbuf_tensor(f"ab_sl{i}", [128, 24], F32)) for i in range(2)]
            dg2 = [ph.enter_context(nc.sbuf_tensor(f"ab_dg{i}", [128, 128], BF16)) for i in range(2)]
            avst = ph.enter_context(nc.sbuf_tensor("ab_av", [128, 8192], BF16))
            ptf = [ph.enter_context(nc.psum_tensor(f"ab_ptb{i}", [128, 512], F32)) for i in range(2)]
            bufs = dict(q=q_sb, k=k_sb, v=v_sb, ms=ms2, p=p2, pt=pt_sb, sl=sl2,
                        dg=dg2, av=avst, ptf=ptf, scale=float(1.0 / 16.0))
            run_attn(bufs, kind="b")

    def run_attn(bufs, kind):
        nheads = 4 if kind == "a" else 2
        pre = "aa" if kind == "a" else "ab"
        esim = P.sem(pre + "_esim"); emsk = P.sem(pre + "_emsk")
        eexp = P.sem(pre + "_eexp"); eptv = P.sem(pre + "_eptv")
        eav = P.sem(pre + "_eav"); esum = P.sem(pre + "_esum")
        ethe = P.sem(pre + "_ethe"); dnr = P.sem(pre + "_dnr")
        tve1 = P.sem(pre + "_tve1"); tac = P.sem(pre + "_tac")
        psim = P.sem(pre + "_psim"); psc = P.sem(pre + "_psc")
        pptg = P.sem(pre + "_pptg"); pavg = P.sem(pre + "_pavg")
        un = P.sem(pre + "_un")
        ctrs = dict(simg=0, scg=0, ptg=0, avg=0)
        rec = {}
        expCnt = {}; ptgCnt = {}; avgCnt = {}

        def front(u, b, h, qt):
            if _SERIALIZE_UNITS:
                for en in ("tensor", "vector", "scalar"):
                    for sm in (esim, emsk, eexp, eptv, eav, esum, ethe, dnr, tac,
                               psim, psc, pptg, pavg, sDV, sAC):
                        P.wait(en, sm, sm.n)
            p = u % 2
            nk = qt + 1
            ng = (nk + 3) // 4
            sim_sb = bufs["sim"][p] if kind == "a" else None
            ms_sb = bufs["ms"][p]
            slots = bufs["sl"][p]
            theta = bufs["th"][p] if kind == "a" else None
            mskt = bufs["mk"][p] if kind == "a" else None
            junk = bufs["jk"][p] if kind == "a" else None
            S_ = S
            if kind == "a":
                qcol = slice(h * S_ + qt * 128, h * S_ + (qt + 1) * 128)
                P.op("vector", lambda e, s2=slots: e.memset(s2[:], 0.0))
                for g in range(ng):
                    k0 = g * 512
                    w = min(512, nk * 128 - k0)
                    sg = ctrs["simg"]; ctrs["simg"] += 1
                    bank = pb[sg % 2]
                    if sg >= 2:
                        P.wait("tensor", esim, sg - 1)
                    P.op("tensor", lambda e, k2=k0, w2=w, bk=bank: e.matmul(
                        bk[:, 0:w2], bufs["qh"][:, qcol], bufs["kh"][:, h * S_ + k2: h * S_ + k2 + w2],
                        start=True, stop=True), inc=psim)
                    P.wait("vector", psim, psim.n)
                    if g < ng - 1:
                        P.op("vector", lambda e, k2=k0, w2=w, g2=g, bk=bank: e.tensor_scalar(
                            out=sim_sb[:, k2:k2 + w2], in0=bk[:, 0:w2], scalar1=1.0, scalar2=0.0,
                            op0=ALU.mult, op1=ALU.add, accum_out=slots[:, g2:g2 + 1]))
                        P.op("vector", lambda e, k2=k0, w2=w, g2=g: e.scalar_tensor_tensor(
                            out=junk[:, 0:w2], in0=sim_sb[:, k2:k2 + w2], scalar=1.0, in1=sim_sb[:, k2:k2 + w2],
                            op0=ALU.mult, op1=ALU.mult, accum_out=slots[:, 8 + g2:9 + g2]), inc=esim)
                    else:
                        nd = w - 128
                        if nd > 0:
                            P.op("vector", lambda e, k2=k0, w2=nd, g2=g, bk=bank: e.tensor_scalar(
                                out=sim_sb[:, k2:k2 + w2], in0=bk[:, 0:w2], scalar1=1.0, scalar2=0.0,
                                op0=ALU.mult, op1=ALU.add, accum_out=slots[:, g2:g2 + 1]))
                            P.op("vector", lambda e, k2=k0, w2=nd: e.scalar_tensor_tensor(
                                out=junk[:, 0:w2], in0=sim_sb[:, k2:k2 + w2], scalar=1.0, in1=sim_sb[:, k2:k2 + w2],
                                op0=ALU.mult, op1=ALU.mult, accum_out=slots[:, 13:14]))
                        P.op("vector", lambda e, k2=k0 + nd, nd2=nd, bk=bank: e.tensor_tensor(
                            sim_sb[:, k2:k2 + 128], bk[:, nd2:nd2 + 128], causal01[:], ALU.mult))
                        P.op("vector", lambda e, k2=k0 + nd: e.tensor_scalar(
                            out=junk[:, 0:128], in0=sim_sb[:, k2:k2 + 128], scalar1=1.0, scalar2=0.0,
                            op0=ALU.mult, op1=ALU.add, accum_out=slots[:, 4:5]))
                        P.op("vector", lambda e, k2=k0 + nd: e.scalar_tensor_tensor(
                            out=junk[:, 0:128], in0=sim_sb[:, k2:k2 + 128], scalar=1.0, in1=sim_sb[:, k2:k2 + 128],
                            op0=ALU.mult, op1=ALU.mult, accum_out=slots[:, 12:13]), inc=esim)
                P.op("vector", lambda e: e.nop(), inc=sDV)
                P.wait("vector", sDV, None)
                P.op("vector", lambda e, s2=slots: e.reduce_sum(s2[:, 14:15], s2[:, 0:6], axis=AX.X))
                P.op("vector", lambda e, s2=slots: e.reduce_sum(s2[:, 15:16], s2[:, 8:14], axis=AX.X), inc=sDV)
                P.wait("vector", sDV, None)
                P.op("vector", lambda e, s2=slots, p2=qt: e.tensor_scalar(
                    out=s2[:, 16:17], in0=s2[:, 14:15], scalar1=invn[:, p2:p2 + 1], scalar2=None, op0=ALU.mult))
                P.op("vector", lambda e, s2=slots, p2=qt: e.tensor_scalar(
                    out=s2[:, 17:18], in0=s2[:, 15:16], scalar1=invn[:, p2:p2 + 1], scalar2=None, op0=ALU.mult), inc=sDV)
                P.wait("vector", sDV, None)
                P.op("vector", lambda e, s2=slots: e.scalar_tensor_tensor(
                    out=s2[:, 18:19], in0=s2[:, 16:17], scalar=-1.0, in1=s2[:, 16:17], op0=ALU.mult, op1=ALU.mult), inc=sDV)
                P.wait("vector", sDV, None)
                P.op("vector", lambda e, s2=slots: e.tensor_tensor(s2[:, 19:20], s2[:, 17:18], s2[:, 18:19], ALU.add), inc=sDV)
                P.wait("vector", sDV, None)
                P.op("vector", lambda e, s2=slots: e.tensor_scalar(
                    out=s2[:, 19:20], in0=s2[:, 19:20], scalar1=1e-6, scalar2=None, op0=ALU.max), inc=tve1)
                P.wait("scalar", tve1, tve1.n)
                P.op("scalar", lambda e, s2=slots: e.activation(s2[:, 20:21], s2[:, 19:20], AF.Ln), inc=sAC)
                P.wait("scalar", sAC, None)
                P.op("scalar", lambda e, s2=slots: e.activation(s2[:, 21:22], s2[:, 20:21], AF.Exp, scale=0.5), inc=tac)
                P.wait("vector", tac, tac.n)
                P.op("vector", lambda e, s2=slots, p2=qt: e.tensor_scalar(
                    out=s2[:, 22:23], in0=s2[:, 21:22], scalar1=ztab[:, p2:p2 + 1], scalar2=None, op0=ALU.mult), inc=sDV)
                P.wait("vector", sDV, None)
                P.op("vector", lambda e, s2=slots, t2=theta: e.tensor_tensor(t2[:, 0:1], s2[:, 16:17], s2[:, 22:23], ALU.add), inc=sDV)
                P.wait("vector", sDV, None)
                P.op("vector", lambda e, t2=theta: e.tensor_scalar(
                    out=t2[:, 0:1], in0=t2[:, 0:1], scalar1=127.0, scalar2=None, op0=ALU.min), inc=ethe)
            for g in range(ng):
                k0 = g * 512
                w = min(512, nk * 128 - k0)
                sg = ctrs["scg"]; ctrs["scg"] += 1
                bank = pb[2 + sg % 2]
                if sg >= 2:
                    P.wait("tensor", emsk, sg - 1)
                if kind == "a":
                    P.op("tensor", lambda e, k2=k0, w2=w, bk=bank: e.matmul(
                        bk[:, 0:w2], bufs["q"][:, qcol], bufs["k"][:, k2:k2 + w2],
                        start=True, stop=True), inc=psc)
                else:
                    for dc in range(2):
                        dcg = h * 2 + dc
                        P.op("tensor", lambda e, k2=k0, w2=w, d2=dcg, d3=dc, q2=qt, bk=bank: e.matmul(
                            bk[:, 0:w2], bufs["q"][:, d2 * S + q2 * 128:d2 * S + (q2 + 1) * 128],
                            bufs["k"][:, d2 * S + k2:d2 * S + k2 + w2],
                            start=(d3 == 0), stop=(d3 == 1)), inc=(psc if dc == 1 else None))
                P.wait("vector", psc, psc.n)
                if u >= 2:
                    P.wait("vector", eexp, expCnt.get(u - 2, 0))
                if kind == "a":
                    P.op("vector", lambda e, k2=k0, w2=w, t2=theta: e.tensor_scalar(
                        out=mskt[:, 0:w2], in0=sim_sb[:, k2:k2 + w2], scalar1=t2[:, 0:1], scalar2=NEG,
                        op0=ALU.is_lt, op1=ALU.mult), inc=sDV)
                    P.wait("vector", sDV, None)
                    P.op("vector", lambda e, k2=k0, w2=w, bk=bank: e.scalar_tensor_tensor(
                        out=ms_sb[:, k2:k2 + w2], in0=bk[:, 0:w2], scalar=bufs["scale"],
                        in1=mskt[:, 0:w2], op0=ALU.mult, op1=ALU.add))
                else:
                    P.op("vector", lambda e, k2=k0, w2=w, bk=bank: e.tensor_scalar(
                        out=ms_sb[:, k2:k2 + w2], in0=bk[:, 0:w2], scalar1=bufs["scale"], scalar2=None,
                        op0=ALU.mult))
                if g == ng - 1:
                    kd = nk * 128 - 128
                    P.op("vector", lambda e, k2=kd: e.tensor_tensor(
                        ms_sb[:, k2:k2 + 128], ms_sb[:, k2:k2 + 128], causal_add[:], ALU.add), inc=sDV)
                    P.wait("vector", sDV, None)
                P.op("vector", lambda e, k2=k0, w2=w, g2=g, s2=slots: e.reduce_max(
                    s2[:, g2:g2 + 1], ms_sb[:, k2:k2 + w2], axis=AX.X), inc=emsk)
            P.wait("vector", emsk, emsk.n)
            P.op("vector", lambda e, g2=ng, s2=slots: e.reduce_max(s2[:, 6:7], s2[:, 0:g2], axis=AX.X), inc=sDV)
            P.wait("vector", sDV, None)
            P.op("vector", lambda e, s2=slots: e.tensor_scalar(
                out=s2[:, 7:8], in0=s2[:, 6:7], scalar1=-1.0, scalar2=None, op0=ALU.mult), inc=dnr)
            P.wait("scalar", dnr, dnr.n)
            for g in range(ng):
                k0 = g * 512
                w = min(512, nk * 128 - k0)
                if u >= 2:
                    P.wait("scalar", pptg, ptgCnt.get(u - 2, 0))
                P.op("scalar", lambda e, k2=k0, w2=w, g2=g, s2=slots, pp=bufs["p"][p]: e.activation(
                    pp[:, k2:k2 + w2], ms_sb[:, k2:k2 + w2], AF.Exp, bias=s2[:, 7:8],
                    accum_out=s2[:, 8 + g2:9 + g2]), inc=eexp)
            expCnt[u] = eexp.n

        def back(u, b, h, qt):
            if _SERIALIZE_UNITS:
                for en in ("tensor", "vector", "scalar"):
                    for sm in (esim, emsk, eexp, eptv, eav, esum, ethe, dnr, tac,
                               psim, psc, pptg, pavg, sDV, sAC):
                        P.wait(en, sm, sm.n)
            p = u % 2
            nk = qt + 1
            ng = (nk + 3) // 4
            slots = bufs["sl"][p]
            diagr = bufs["dg"][p]
            p_sb = bufs["p"][p]
            pt_sb = bufs["pt"]
            ptf = bufs["ptf"]
            P.wait("vector", eexp, expCnt[u])
            P.op("vector", lambda e, g2=ng, s2=slots: e.reduce_sum(s2[:, 16:17], s2[:, 8:8 + g2], axis=AX.X), inc=sDV)
            P.wait("vector", sDV, None)
            P.op("vector", lambda e, s2=slots: e.reciprocal(s2[:, 17:18], s2[:, 16:17]), inc=sDV)
            P.wait("vector", sDV, None)
            if u >= 2:
                P.wait("vector", pptg, ptgCnt[u - 2])
            P.op("vector", lambda e, s2=slots, dg=diagr: e.tensor_scalar(
                out=dg[:], in0=ident_bf[:], scalar1=s2[:, 17:18], scalar2=None, op0=ALU.mult), inc=esum)
            npt = (nk + 3) // 4
            av_bank = pb[4 + (u % 2)]
            ptg0 = ctrs["ptg"]
            first_av = True

            def emit_pt(gi):
                ptg = ctrs["ptg"]; ctrs["ptg"] += 1
                bank = ptf[ptg % 2]
                base = gi * 4
                cnt = min(4, nk - base)
                if gi == 0:
                    P.wait("tensor", esum, esum.n)
                if ptg >= 2:
                    P.wait("tensor", eptv, ptg - 1)
                for j in range(cnt):
                    kc = base + j
                    P.op("tensor", lambda e, k2=kc, j2=j, bk=bank, dg=diagr: e.matmul(
                        bk[:, j2 * 128:(j2 + 1) * 128], p_sb[:, k2 * 128:(k2 + 1) * 128], dg[:],
                        start=True, stop=True), inc=(pptg if j == cnt - 1 else None))
                P.wait("scalar", pptg, pptg.n)
                if ptg >= 2:
                    P.wait("scalar", pavg, ptg - 1)
                P.op("scalar", lambda e, bk=bank, w2=cnt * 128, sl2=(ptg % 2): e.copy(
                    pt_sb[:, sl2 * 512:sl2 * 512 + w2], bk[:, 0:w2]), inc=eptv)

            def emit_av(gi):
                nonlocal first_av
                ptg = ptg0 + gi
                base = gi * 4
                cnt = min(4, nk - base)
                P.wait("tensor", eptv, ptg + 1)
                if first_av:
                    P.wait("tensor", eav, max(0, u - 1))
                    first_av = False
                for j in range(cnt):
                    kc = base + j
                    if bufs.get("k") is not None and "sim" in bufs:
                        P.op("tensor", lambda e, k2=kc, j2=j, bk=av_bank, sl2=(ptg % 2): e.matmul(
                            bk[:, 0:128], bufs["v"][:, k2 * 128:(k2 + 1) * 128],
                            pt_sb[:, sl2 * 512 + j2 * 128:sl2 * 512 + (j2 + 1) * 128],
                            start=(k2 == 0), stop=(k2 == nk - 1)),
                            inc=(pavg if (j == cnt - 1) else None))
                    else:
                        for vc in range(2):
                            vcg = h * 2 + vc
                            P.op("tensor", lambda e, k2=kc, j2=j, v2=vcg, v3=vc, bk=av_bank, sl2=(ptg % 2): e.matmul(
                                bk[:, v3 * 128:(v3 + 1) * 128],
                                bufs["v"][:, k2 * 512 + v2 * 128:k2 * 512 + (v2 + 1) * 128],
                                pt_sb[:, sl2 * 512 + j2 * 128:sl2 * 512 + (j2 + 1) * 128],
                                start=(k2 == 0 and v3 == 0), stop=(k2 == nk - 1)),
                                inc=(pavg if (j == cnt - 1 and vc == 1) else None))

            emit_pt(0)
            if npt > 1:
                emit_pt(1)
            for gi in range(npt):
                if gi + 2 < npt:
                    pass
                emit_av(gi)
                if gi + 2 < npt:
                    emit_pt(gi + 2)
                elif gi + 2 == npt:
                    pass
            ptgCnt[u] = ctrs["ptg"]
            avgCnt[u] = pavg.n
            P.wait("vector", pavg, pavg.n)
            if kind == "a":
                P.op("vector", lambda e, h2=h, q2=qt, bk=av_bank: e.tensor_copy(
                    bufs["av"][:, h2 * 2048 + q2 * 128: h2 * 2048 + (q2 + 1) * 128], bk[:, 0:128]), inc=eav)
            else:
                for vc in range(2):
                    r2 = h * 2 + vc
                    P.op("vector", lambda e, r3=r2, q2=qt, v3=vc, bk=av_bank: e.tensor_copy(
                        bufs["av"][:, r3 * 2048 + q2 * 128: r3 * 2048 + (q2 + 1) * 128],
                        bk[:, v3 * 128:(v3 + 1) * 128]), inc=(eav if vc == 1 else None))
            P.op("vector", lambda e: e.nop(), inc=un)

        uidx = 0
        for b in range(B):
            t0 = b * S
            P.wait("sync", un, None)
            P.wait("sync", sSt, None)
            if kind == "a":
                P.dma("sync", bufs["k"][:], ka_d[:, t0:t0 + S], sX)
                P.dma("sync", bufs["v"][:].rearrange("p (tt v) -> p tt v", v=128),
                      va_d[b * 16:(b + 1) * 16, :, :].rearrange("tt p v -> p tt v"), sX)
                for hh in range(4):
                    P.dma("sync", bufs["kh"][:, hh * S:(hh + 1) * S], kha_d[hh * 128:(hh + 1) * 128, t0:t0 + S], sX)
                    P.dma("sync", bufs["q"][:, hh * S:(hh + 1) * S], qa_d[hh * 128:(hh + 1) * 128, t0:t0 + S], sX)
                    P.dma("sync", bufs["qh"][:, hh * S:(hh + 1) * S], qha_d[hh * 128:(hh + 1) * 128, t0:t0 + S], sX)
            else:
                P.dma("sync", bufs["q"][:].rearrange("p (dc t) -> p dc t", t=S), qb_d[:, t0:t0 + S].rearrange("(dc p) t -> p dc t", p=128), sX)
                P.dma("sync", bufs["k"][:].rearrange("p (dc t) -> p dc t", t=S), kb_d[:, t0:t0 + S].rearrange("(dc p) t -> p dc t", p=128), sX)
                P.dma("sync", bufs["v"][:].rearrange("p (tt v) -> p tt v", v=512), vb_d[b * 16:(b + 1) * 16, :, :].rearrange("tt p v -> p tt v"), sX)
            for eng in ("tensor", "vector", "scalar"):
                P.wait(eng, sX, None)
            P.wait("vector", sSt, None)
            units = [(hh, qt) for hh in range(nheads) for qt in range(16)]
            prev = None
            for (hh, qt) in units:
                front(uidx, b, hh, qt)
                if prev is not None:
                    back(*prev)
                prev = (uidx, b, hh, qt)
                uidx += 1
            back(*prev)
            P.wait("sync", eav, eav.n)
            for sh in range(2):
                shard = b * 2 + sh
                if kind == "a":
                    for hh in range(4):
                        P.dma("sync", a2a_in[shard * 512 + hh * 128:shard * 512 + (hh + 1) * 128, :],
                              bufs["av"][:, hh * 2048 + sh * 1024: hh * 2048 + (sh + 1) * 1024], sSt)
                else:
                    for r2 in range(4):
                        P.dma("sync", a2a_in[shard * 512 + r2 * 128:shard * 512 + (r2 + 1) * 128, :],
                              bufs["av"][:, r2 * 2048 + sh * 1024: r2 * 2048 + (sh + 1) * 1024], sSt)
        P.barrier("attn_" + kind)

    def wo_phase(wname, xold_dram, xnew_dram, name):
        with ExitStack() as ph:
            afm = ph.enter_context(nc.sbuf_tensor("wo_afm" + name, [128, NEC * T_OWN], BF16))
            wsl = [ph.enter_context(nc.sbuf_tensor(f"wo{name}_w{i}", [128, NEC * 512], BF16)) for i in range(2)]
            rin = [ph.enter_context(nc.sbuf_tensor(f"wo{name}_ri{i}", [128, 512], F32)) for i in range(2)]
            rout = [ph.enter_context(nc.sbuf_tensor(f"wo{name}_ro{i}", [128, 512], F32)) for i in range(2)]
            addc = P.sem("woadd_" + name)
            wait_stores("scalar")
            sScp = [P.sem(f"wosc{name}{i}") for i in range(2)]
            sStp = [P.sem(f"wostp{name}{i}") for i in range(2)]
            addc0 = addc.n
            pe0 = sPE.n
            for ec in range(NEC):
                P.dma("sync", afm[:, ec * T_OWN:(ec + 1) * T_OWN], a2a_out[ec * 128:(ec + 1) * 128, :], sX)
            for ot in range(8):
                wbuf = wsl[ot % 2]
                if ot >= 2:
                    P.wait("gpsimd", addc, addc0 + (ot - 1) * 8)
                for ec in range(NEC):
                    P.dma("gpsimd", wbuf[:, ec * 512:(ec + 1) * 512],
                          d[wname][ec * 128:(ec + 1) * 128, ot * 512:(ot + 1) * 512], sWp[ot % 2])
                for tt in range(NTT_OWN):
                    kg = ot * NTT_OWN + tt
                    bank = pb[kg % 3]
                    if kg >= 2:
                        P.wait("scalar", addc, addc0 + kg - 1)
                    P.dma("scalar", rin[kg % 2][:], xold_dram[tt * 128:(tt + 1) * 128, ot * 512:(ot + 1) * 512], sScp[kg % 2])
                    if tt == 0:
                        P.wait("tensor", sWp[ot % 2], None)
                        if ot == 0:
                            P.wait("tensor", sX, None)
                    if kg >= 3:
                        P.wait("tensor", addc, addc0 + kg - 2)
                    for ec in range(NEC):
                        P.op("tensor", lambda e, bk=bank, ec2=ec, t2=tt, wb=wbuf: e.matmul(
                            bk[:], afm[:, ec2 * T_OWN + t2 * 128: ec2 * T_OWN + (t2 + 1) * 128],
                            wb[:, ec2 * 512:(ec2 + 1) * 512],
                            start=(ec2 == 0), stop=(ec2 == NEC - 1)), inc=(sPE if ec == NEC - 1 else None))
                    P.wait("vector", sScp[kg % 2], None)
                    P.wait("vector", sPE, pe0 + kg + 1)
                    if kg >= 2:
                        P.wait("vector", sStp[kg % 2], None)
                    P.op("vector", lambda e, bk=bank, k2=kg: e.tensor_tensor(
                        rout[k2 % 2][:], bk[:], rin[k2 % 2][:], ALU.add), inc=addc)
                    P.wait("sync", addc, addc0 + kg + 1)
                    P.dma("sync", xnew_dram[tt * 128:(tt + 1) * 128, ot * 512:(ot + 1) * 512], rout[kg % 2][:], sStp[kg % 2])
            P.barrier("wo_" + name)

    def ffn_phase(xold_dram, xnew_dram, name, afm):
        with ExitStack() as ph:
            wsl = [ph.enter_context(nc.sbuf_tensor(f"ff{name}_w{i}", [128, NEC * 512], BF16)) for i in range(2)]
            axT = ph.enter_context(nc.sbuf_tensor("ff_axT" + name, [64, NTT_OWN * 128], BF16))
            ax_tok = ph.enter_context(nc.sbuf_tensor("ff_ax" + name, [128, 64], BF16))
            rin = [ph.enter_context(nc.sbuf_tensor(f"ff{name}_ri{i}", [128, 512], F32)) for i in range(2)]
            rout = [ph.enter_context(nc.sbuf_tensor(f"ff{name}_ro{i}", [128, 512], F32)) for i in range(2)]
            axp = ph.enter_context(nc.psum_tensor("ff_axp" + name, [128, 128], BF16))
            ffa = P.sem("ffa_" + name)
            addc = P.sem("ffadd_" + name)
            wait_stores("scalar")
            sScp = [P.sem(f"ffsc{name}{i}") for i in range(2)]
            sStp = [P.sem(f"ffstp{name}{i}") for i in range(2)]
            addc0 = addc.n
            pe0 = sPE.n
            for tt in range(NTT_OWN):
                bank = pb[4]
                if _SER_AX:
                    P.barrier("serax")
                if tt >= 1:
                    P.wait("tensor", ffa, 2 * tt)
                for ec in range(NEC):
                    P.op("tensor", lambda e, ec2=ec, t2=tt, bk=bank: e.matmul(
                        bk[:, 0:64], afm[:, ec2 * T_OWN + t2 * 128: ec2 * T_OWN + (t2 + 1) * 128],
                        abT_sb[:, ec2 * 64:(ec2 + 1) * 64],
                        start=(ec2 == 0), stop=(ec2 == NEC - 1)), inc=(sPE if ec == NEC - 1 else None))
                P.wait("vector", sPE, None)
                P.op("vector", lambda e, bk=bank: e.tensor_copy(ax_tok[:, 0:64], bk[:, 0:64]), inc=ffa)
                P.wait("tensor", ffa, None)
                P.op("tensor", lambda e: e.transpose(axp[0:64, 0:128], ax_tok[:, 0:64], ident_bf[:]), inc=sPE)
                P.wait("vector", sPE, None)
                P.op("vector", lambda e, t2=tt: e.tensor_copy(
                    axT[0:64, t2 * 128:(t2 + 1) * 128], axp[0:64, 0:128]), inc=ffa)
            P.barrier("ffn_ax_" + name)
            pe0 = sPE.n
            for ot in range(8):
                wbuf = wsl[ot % 2]
                if ot >= 2:
                    P.wait("gpsimd", addc, addc0 + (ot - 1) * 8)
                for ec in range(NEC):
                    P.dma("gpsimd", wbuf[:, ec * 512:(ec + 1) * 512],
                          d["fker"][ec * 128:(ec + 1) * 128, ot * 512:(ot + 1) * 512], sWp[ot % 2])
                for tt in range(NTT_OWN):
                    kg = ot * NTT_OWN + tt
                    bank = pb[kg % 3]
                    if kg >= 2:
                        P.wait("scalar", addc, addc0 + kg - 1)
                    P.dma("scalar", rin[kg % 2][:], xold_dram[tt * 128:(tt + 1) * 128, ot * 512:(ot + 1) * 512], sScp[kg % 2])
                    if tt == 0:
                        P.wait("tensor", sWp[ot % 2], None)
                    if kg >= 3:
                        P.wait("tensor", addc, addc0 + kg - 2)
                    for ec in range(NEC):
                        P.op("tensor", lambda e, bk=bank, ec2=ec, t2=tt, wb=wbuf: e.matmul(
                            bk[:], afm[:, ec2 * T_OWN + t2 * 128: ec2 * T_OWN + (t2 + 1) * 128],
                            wb[:, ec2 * 512:(ec2 + 1) * 512],
                            start=(ec2 == 0), stop=False))
                    P.op("tensor", lambda e, bk=bank, o2=ot: e.matmul(
                        bk[:], ones_bf[:, :], fbias_sb[:, o2 * 512:(o2 + 1) * 512],
                        start=False, stop=False))
                    P.op("tensor", lambda e, bk=bank, o2=ot, t2=tt: e.matmul(
                        bk[:], axT[0:64, t2 * 128:(t2 + 1) * 128], bbT_sb[0:64, o2 * 512:(o2 + 1) * 512],
                        start=False, stop=True), inc=sPE)
                    P.wait("vector", sScp[kg % 2], None)
                    P.wait("vector", sPE, pe0 + kg + 1)
                    if kg >= 2:
                        P.wait("vector", sStp[kg % 2], None)
                    P.op("vector", lambda e, bk=bank, k2=kg: e.tensor_tensor(
                        rout[k2 % 2][:], bk[:], rin[k2 % 2][:], ALU.add), inc=addc)
                    P.wait("sync", addc, addc0 + kg + 1)
                    P.dma("sync", xnew_dram[tt * 128:(tt + 1) * 128, ot * 512:(ot + 1) * 512], rout[kg % 2][:], sStp[kg % 2])
            P.barrier("ffn_" + name)

    def proj_b():
        with ExitStack() as ph:
            wsb = ph.enter_context(nc.sbuf_tensor("pb_w", [128, NEC * 1536], BF16))
            xsl = [ph.enter_context(nc.sbuf_tensor(f"pb_x{i}", [128, NEC * 256], BF16)) for i in range(2)]
            sc = dict(
                qrot=ph.enter_context(nc.sbuf_tensor("pb_qr", [128, 512], BF16)),
                krot=ph.enter_context(nc.sbuf_tensor("pb_kr", [128, 512], BF16)),
                rtt=ph.enter_context(nc.sbuf_tensor("pb_rt", [128, 192], F32)),
                v=ph.enter_context(nc.sbuf_tensor("pb_v", [128, 512], BF16)),
                qT=ph.enter_context(nc.sbuf_tensor("pb_qT", [128, 512], BF16)),
                kT=ph.enter_context(nc.sbuf_tensor("pb_kT", [128, 512], BF16)),
            )
            pt8 = [ph.enter_context(nc.psum_tensor(f"pb_pt{i}", [128, 1024], BF16)) for i in range(2)]
            for ec in range(NEC):
                P.dma("gpsimd", wsb[:, ec * 1536:(ec + 1) * 1536], d["wqkvb"][ec * 128:(ec + 1) * 128, :], sW)
            evt = P.sem("pb_evt")
            for qs in range(32):
                buf = xsl[qs % 2]
                r, quar = qs // 4, qs % 4
                if qs >= 2:
                    P.wait("sync", sPE, None)
                for ecg in range(4):
                    P.dma("sync",
                          buf[:].rearrange("p (c t) -> p c t", t=256)[:, ecg * 8:(ecg + 1) * 8, :],
                          ag_out[r * E + ecg * 1024:r * E + (ecg + 1) * 1024, quar * 256:(quar + 1) * 256].rearrange("(c p) t -> p c t", p=128),
                          sXp[qs % 2])
                P.wait("tensor", sXp[qs % 2], None)
                P.wait("tensor", sW, None)
                for tt2 in range(2):
                    tt = qs * 2 + tt2
                    pc = tt % 16
                    bq = pb[0 + tt2]
                    bk_ = pb[2 + tt2]
                    bv = pb[4 + tt2]
                    if tt >= 2:
                        P.wait("tensor", evt, tt - 1)
                    for ec in range(NEC):
                        xt = buf[:, ec * 256 + tt2 * 128: ec * 256 + (tt2 + 1) * 128]
                        P.op("tensor", lambda e, bk2=bq, x=xt, ec2=ec: e.matmul(
                            bk2[:], x, wsb[:, ec2 * 1536:ec2 * 1536 + 512],
                            start=(ec2 == 0), stop=(ec2 == NEC - 1)))
                    for ec in range(NEC):
                        xt = buf[:, ec * 256 + tt2 * 128: ec * 256 + (tt2 + 1) * 128]
                        P.op("tensor", lambda e, bk2=bk_, x=xt, ec2=ec: e.matmul(
                            bk2[:], x, wsb[:, ec2 * 1536 + 512:ec2 * 1536 + 1024],
                            start=(ec2 == 0), stop=(ec2 == NEC - 1)))
                    for ec in range(NEC):
                        xt = buf[:, ec * 256 + tt2 * 128: ec * 256 + (tt2 + 1) * 128]
                        P.op("tensor", lambda e, bk2=bv, x=xt, ec2=ec: e.matmul(
                            bk2[:], x, wsb[:, ec2 * 1536 + 1024:(ec2 + 1) * 1536],
                            start=(ec2 == 0), stop=(ec2 == NEC - 1)), inc=(sPE if ec == NEC - 1 else None))
                    P.wait("vector", sPE, None)
                    P.wait("vector", sSt, None)
                    rot_b(sc, sc["qrot"], bq, pc)
                    rot_b(sc, sc["krot"], bk_, pc)
                    P.op("vector", lambda e, bk2=bv: e.tensor_copy(sc["v"][:], bk2[:]), inc=sDV)
                    P.wait("tensor", sDV, None)
                    bank = pt8[tt2]
                    for dc in range(4):
                        P.op("tensor", lambda e, bk2=bank, d2=dc: e.transpose(
                            bk2[:, d2 * 128:(d2 + 1) * 128], sc["qrot"][:, d2 * 128:(d2 + 1) * 128], ident_bf[:]))
                    for dc in range(4):
                        P.op("tensor", lambda e, bk2=bank, d2=dc: e.transpose(
                            bk2[:, 512 + d2 * 128:512 + (d2 + 1) * 128], sc["krot"][:, d2 * 128:(d2 + 1) * 128], ident_bf[:]),
                            inc=(sPE if dc == 3 else None))
                    P.wait("scalar", sPE, None)
                    P.wait("scalar", sSt, None)
                    P.op("scalar", lambda e, bk2=bank: e.copy(sc["qT"][:], bk2[:, 0:512]))
                    P.op("scalar", lambda e, bk2=bank: e.copy(sc["kT"][:], bk2[:, 512:1024]), inc=sAC)
                    P.wait("sync", sAC, None)
                    P.wait("sync", sDV, None)
                    t0 = tt * 128
                    for dc in range(4):
                        P.dma("sync", qb_d[dc * 128:(dc + 1) * 128, t0:t0 + 128], sc["qT"][:, dc * 128:(dc + 1) * 128], sSt)
                        P.dma("sync", kb_d[dc * 128:(dc + 1) * 128, t0:t0 + 128], sc["kT"][:, dc * 128:(dc + 1) * 128], sSt)
                    P.dma("sync", vb_d[tt, :, :], sc["v"][:], sSt)
                    P.op("sync", lambda e: e.nop(), inc=evt)
            P.barrier("proj_b")

    def rot_b(sc, dst, srcb, pc):
        cB = cosB[:, pc * ROT:(pc + 1) * ROT]
        sB = sinB[:, pc * ROT:(pc + 1) * ROT]
        rtt = sc["rtt"]
        for hh in range(2):
            base = hh * 256
            P.op("vector", lambda e, b2=base: e.tensor_copy(rtt[:, 0:63:2], srcb[:, b2 + 1:b2 + 64:2]))
            P.op("vector", lambda e, b2=base: e.tensor_copy(rtt[:, 1:64:2], srcb[:, b2:b2 + 63:2]))
            P.op("vector", lambda e, b2=base, c=cB: e.tensor_tensor(rtt[:, 64:128], srcb[:, b2:b2 + 64], c, ALU.mult))
            P.op("vector", lambda e, s2=sB: e.tensor_tensor(rtt[:, 128:192], rtt[:, 0:64], s2, ALU.mult))
            P.op("vector", lambda e, b2=base: e.tensor_add(dst[:, b2:b2 + 64], rtt[:, 64:128], rtt[:, 128:192]))
            P.op("vector", lambda e, b2=base: e.tensor_copy(dst[:, b2 + 64:b2 + 256], srcb[:, b2 + 64:b2 + 256]))

    def attn_b():
        with ExitStack() as ph:
            bufs = dict(
                q=ph.enter_context(nc.sbuf_tensor("ab_q", [128, 4 * S], BF16)),
                k=ph.enter_context(nc.sbuf_tensor("ab_k", [128, 4 * S], BF16)),
                v=ph.enter_context(nc.sbuf_tensor("ab_v", [128, 4 * S], BF16)),
                ms=ph.enter_context(nc.sbuf_tensor("ab_ms", [128, S], F32)),
                p=ph.enter_context(nc.sbuf_tensor("ab_p", [128, S], BF16)),
                pt=ph.enter_context(nc.sbuf_tensor("ab_pt", [128, 512], BF16)),
                sl=ph.enter_context(nc.sbuf_tensor("ab_sl", [128, 16], F32)),
                rm=ph.enter_context(nc.sbuf_tensor("ab_rm", [128, 1], F32)),
                nrm=ph.enter_context(nc.sbuf_tensor("ab_nrm", [128, 1], F32)),
                ls=ph.enter_context(nc.sbuf_tensor("ab_ls", [128, 1], F32)),
                lr=ph.enter_context(nc.sbuf_tensor("ab_lr", [128, 1], F32)),
                dg=ph.enter_context(nc.sbuf_tensor("ab_dg", [128, 128], BF16)),
                av=ph.enter_context(nc.sbuf_tensor("ab_av", [128, 8192], BF16)),
            )
            bufs["ptb"] = [ph.enter_context(nc.psum_tensor(f"ab_ptb{i}", [128, 512], F32)) for i in range(2)]
            un = P.sem("ab_un")
            for b in range(B):
                t0 = b * S
                P.wait("sync", un, None)
                P.wait("sync", sSt, None)
                P.dma("sync", bufs["q"][:].rearrange("p (dc t) -> p dc t", t=S), qb_d[:, t0:t0 + S].rearrange("(dc p) t -> p dc t", p=128), sX)
                P.dma("sync", bufs["k"][:].rearrange("p (dc t) -> p dc t", t=S), kb_d[:, t0:t0 + S].rearrange("(dc p) t -> p dc t", p=128), sX)
                P.dma("sync", bufs["v"][:].rearrange("p (tt v) -> p tt v", v=512), vb_d[b * 16:(b + 1) * 16, :, :].rearrange("tt p v -> p tt v"), sX)
                for eng in ("tensor", "vector", "scalar"):
                    P.wait(eng, sX, None)
                P.wait("vector", sSt, None)
                for h in range(2):
                    for qt in range(16):
                        unit_b(b, h, qt, bufs, un)
                P.wait("sync", sDV, None)
                for sh in range(2):
                    shard = b * 2 + sh
                    for r2 in range(4):
                        P.dma("sync", a2a_in[shard * 512 + r2 * 128:shard * 512 + (r2 + 1) * 128, :],
                              bufs["av"][:, r2 * 2048 + sh * 1024: r2 * 2048 + (sh + 1) * 1024], sSt)
            P.barrier("attn_b")

    def unit_b(b, h, qt, bufs, un):
        nk = qt + 1
        ng = (nk + 3) // 4
        ms_sb, p_sb, pt_sb, slots = bufs["ms"], bufs["p"], bufs["pt"], bufs["sl"]
        sc_bank = pb[2 + (qt % 2)]
        av_bank = pb[4 + (qt % 2)]
        ptb = bufs["ptb"][qt % 2]
        P.wait("tensor", sDV, None); P.wait("tensor", sAC, None)
        P.wait("vector", sPE, None); P.wait("vector", sAC, None)
        P.wait("scalar", sPE, None); P.wait("scalar", sDV, None)
        for g in range(ng):
            k0 = g * 512
            w = min(512, nk * 128 - k0)
            if g >= 1:
                P.wait("tensor", sDV, None)
            for dc in range(2):
                dcg = h * 2 + dc
                P.op("tensor", lambda e, k2=k0, w2=w, d2=dcg, d3=dc, q2=qt: e.matmul(
                    sc_bank[:, 0:w2], bufs["q"][:, d2 * S + q2 * 128:d2 * S + (q2 + 1) * 128],
                    bufs["k"][:, d2 * S + k2:d2 * S + k2 + w2],
                    start=(d3 == 0), stop=(d3 == 1)), inc=(sPE if dc == 1 else None))
            P.wait("vector", sPE, None)
            P.op("vector", lambda e, k2=k0, w2=w: e.tensor_scalar(
                out=ms_sb[:, k2:k2 + w2], in0=sc_bank[:, 0:w2], scalar1=1.0 / 16.0, scalar2=None,
                op0=ALU.mult), inc=sDV)
            if g == ng - 1:
                kd = nk * 128 - 128
                P.op("vector", lambda e, k2=kd: e.tensor_tensor(
                    ms_sb[:, k2:k2 + 128], ms_sb[:, k2:k2 + 128], causal_add[:], ALU.add), inc=sDV)
            P.op("vector", lambda e, k2=k0, w2=w, g2=g: e.reduce_max(
                slots[:, g2:g2 + 1], ms_sb[:, k2:k2 + w2], axis=AX.X), inc=sDV)
        P.wait("vector", sDV, None)
        P.op("vector", lambda e, g2=ng: e.reduce_max(bufs["rm"][:], slots[:, 0:g2], axis=AX.X), inc=sDV)
        P.wait("vector", sDV, None)
        P.op("vector", lambda e: e.tensor_scalar(
            out=bufs["nrm"][:], in0=bufs["rm"][:], scalar1=-1.0, scalar2=None, op0=ALU.mult), inc=sDV)
        P.wait("scalar", sDV, None)
        for g in range(ng):
            k0 = g * 512
            w = min(512, nk * 128 - k0)
            P.op("scalar", lambda e, k2=k0, w2=w, g2=g: e.activation(
                p_sb[:, k2:k2 + w2], ms_sb[:, k2:k2 + w2], AF.Exp, bias=bufs["nrm"][:],
                accum_out=slots[:, 8 + g2:9 + g2]), inc=sAC)
        P.wait("vector", sAC, None)
        P.op("vector", lambda e, g2=ng: e.reduce_sum(bufs["ls"][:], slots[:, 8:8 + g2], axis=AX.X), inc=sDV)
        P.wait("vector", sDV, None)
        P.op("vector", lambda e: e.reciprocal(bufs["lr"][:], bufs["ls"][:]), inc=sDV)
        P.wait("vector", sDV, None)
        P.op("vector", lambda e: e.tensor_scalar(
            out=bufs["dg"][:], in0=ident_bf[:], scalar1=bufs["lr"][:], scalar2=None, op0=ALU.mult), inc=sDV)
        P.wait("tensor", sDV, None)
        for kc in range(nk):
            ps = kc % 4
            P.op("tensor", lambda e, k2=kc, p2=ps: e.matmul(
                ptb[:, p2 * 128:(p2 + 1) * 128], p_sb[:, k2 * 128:(k2 + 1) * 128], bufs["dg"][:],
                start=True, stop=True), inc=(sPE if (ps == 3 or kc == nk - 1) else None))
            if ps == 3 or kc == nk - 1:
                P.wait("scalar", sPE, None)
                wdt = (ps + 1) * 128
                P.op("scalar", lambda e, w2=wdt: e.copy(pt_sb[:, 0:w2], ptb[:, 0:w2]), inc=sAC)
                P.wait("tensor", sAC, None)
                base = kc - ps
                for j in range(ps + 1):
                    kcj = base + j
                    for vc in range(2):
                        vcg = h * 2 + vc
                        P.op("tensor", lambda e, k2=kcj, j2=j, v2=vcg, v3=vc: e.matmul(
                            av_bank[:, v3 * 128:(v3 + 1) * 128],
                            bufs["v"][:, k2 * 512 + v2 * 128:k2 * 512 + (v2 + 1) * 128],
                            pt_sb[:, j2 * 128:(j2 + 1) * 128],
                            start=(k2 == 0 and v3 == 0), stop=(k2 == nk - 1)), inc=(sPE if (kcj == nk - 1 and vc == 1) else None))
        P.wait("vector", sPE, None)
        for vc in range(2):
            r2 = h * 2 + vc
            P.op("vector", lambda e, r3=r2, q2=qt, v3=vc: e.tensor_copy(
                bufs["av"][:, r3 * 2048 + q2 * 128: r3 * 2048 + (q2 + 1) * 128],
                av_bank[:, v3 * 128:(v3 + 1) * 128]), inc=(sDV if vc == 1 else None))
        P.op("vector", lambda e: e.nop(), inc=un)

    def _ffn_block(xold, xnew, name):
        with ExitStack() as es:
            afm = es.enter_context(nc.sbuf_tensor("afm" + name, [128, NEC * T_OWN], BF16))
            ln_phase(xold, "s" + name, to_agin=False, afm=afm)
            ffn_phase(xold, xnew, name, afm)

    seq = [
        ("ln1", lambda: ln_phase(d["x_own"], "s1", to_agin=True)),
        ("ag1", allgather),
        ("proj_a", proj_a),
        ("attn_a", attn_a),
        ("a2a1", alltoall),
        ("wo_a", lambda: wo_phase("woaT", d["x_own"], xr[0], "a")),
        ("ffn1", lambda: _ffn_block(xr[0], xr[1], "f1")),
        ("ln3", lambda: ln_phase(xr[1], "s3", to_agin=True)),
        ("ag2", allgather),
        ("proj_b", proj_b),
        ("attn_b", attn_b),
        ("a2a2", alltoall),
        ("wo_b", lambda: wo_phase("wob", xr[1], xr[2], "b")),
        ("ffn2", lambda: _ffn_block(xr[2], y, "f2")),
    ]
    limit = len(seq) if _PHASE_LIMIT is None else _PHASE_LIMIT
    for nm, th in seq[:limit]:
        th()

    dumpable = dict(wdbg=wdbg, xlndbg=xlndbg, ag_in=ag_in, ag_out=ag_out, a2a_in=a2a_in, a2a_out=a2a_out,
                    xr0=xr[0], xr1=xr[1], xr2=xr[2], qa_d=qa_d, qha_d=qha_d,
                    ka_d=ka_d, kha_d=kha_d, va_d=va_d, qb_d=qb_d, kb_d=kb_d,
                    vb_d=vb_d, beff_d=beff_d)
    for nm in _DEBUG_DUMPS:
        ap = dumpable[nm]
        dt_ = ap.tensor.dtype
        od = nc.dram_tensor("dump_" + nm, list(ap.tensor.shape), dt_, kind="ExternalOutput").ap()
        P.dma("sync", od[...], ap[...], sSt)

    P.wait("sync", sSt, None)
    for s in ("wostpa0", "wostpa1", "wostpb0", "wostpb1", "ffstpf10", "ffstpf11", "ffstpf20", "ffstpf21"):
        if s in P.sems:
            P.wait("sync", P.sems[s], None)
    P.run()
    stack.close()
    return nc


def make_core_inputs(c, inputs, shared):
    ii = {}
    f32 = np.float32
    g = inputs
    x = np.asarray(g["L0_attn_a_hidden_states"], f32).reshape(T_ALL, E)
    ii["x_own"] = np.ascontiguousarray(x[c * T_OWN:(c + 1) * T_OWN])
    wq = np.asarray(g["L0_attn_a_Wq"], f32)
    wk = np.asarray(g["L0_attn_a_Wk"], f32)
    wv = np.asarray(g["L0_attn_a_Wv"], f32)
    ii["wqkva"] = np.ascontiguousarray(np.concatenate(
        [wq[c * 512:(c + 1) * 512, :].T, wk[c * 128:(c + 1) * 128, :].T, wv[c * 128:(c + 1) * 128, :].T], axis=1))
    ii["hp1"] = np.ascontiguousarray(np.asarray(g["L0_attn_a_hash_proj1"], f32)[4 * c:4 * c + 4].transpose(1, 0, 2).reshape(128, 512))
    ii["hb1"] = np.ascontiguousarray(np.asarray(g["L0_attn_a_hash_bias1"], f32)[4 * c:4 * c + 4].reshape(4, 128).T)
    ii["hp2"] = np.ascontiguousarray(np.asarray(g["L0_attn_a_hash_proj2"], f32)[4 * c:4 * c + 4].transpose(1, 0, 2).reshape(128, 512))
    ii["hb2"] = np.ascontiguousarray(np.asarray(g["L0_attn_a_hash_bias2"], f32)[4 * c:4 * c + 4].reshape(4, 128).T)
    ii["woaT"] = shared["woaT"]
    wqb = np.asarray(g["L0_attn_b_Wq"], f32)[:, c * 512:(c + 1) * 512]
    wkb = np.asarray(g["L0_attn_b_Wk"], f32)[:, c * 512:(c + 1) * 512]
    wvb = np.asarray(g["L0_attn_b_Wv"], f32)[:, c * 512:(c + 1) * 512]
    ii["wqkvb"] = np.ascontiguousarray(np.concatenate([wqb, wkb, wvb], axis=1))
    ii["wob"] = shared["wob"]
    ii["fker"] = shared["fker"]
    ii["fbias"] = shared["fbias"]
    ii["gate"] = np.ascontiguousarray(np.asarray(g["L0_ffn_gate"], f32)[c].reshape(1, 16))
    ii["gAT"] = shared["gAT"]
    ii["gBT"] = shared["gBT"]
    ii["asT"] = shared["asT"]
    ii["bsT"] = shared["bsT"]
    for k, v in CST.items():
        ii[k] = v
    return ii


def kernel(**inputs):
    f32 = np.float32
    la = np.asarray(inputs["L0_ffn_lora_A"], f32)
    lb = np.asarray(inputs["L0_ffn_lora_B"], f32)
    gAT = la.transpose(0, 2, 1).reshape(16, NEC, 128, RANK).transpose(2, 0, 1, 3).reshape(128, 16 * NEC * RANK)
    gBT = lb.transpose(0, 2, 1).reshape(16, 128, 1024).transpose(1, 0, 2).reshape(128, 16 * 1024)
    asv = np.asarray(inputs["L0_ffn_lora_A_shared"], f32)
    asT = asv.T.reshape(NEC, 128, RANK).transpose(1, 0, 2).reshape(128, NEC * RANK)
    shared = {
        "woaT": np.ascontiguousarray(np.asarray(inputs["L0_attn_a_Wo"], f32).T),
        "wob": np.ascontiguousarray(np.asarray(inputs["L0_attn_b_Wo"], f32)),
        "fker": np.ascontiguousarray(np.asarray(inputs["L0_ffn_kernel"], f32)),
        "fbias": np.ascontiguousarray(np.asarray(inputs["L0_ffn_bias"], f32).reshape(1, E)),
        "gAT": np.ascontiguousarray(gAT),
        "gBT": np.ascontiguousarray(gBT),
        "asT": np.ascontiguousarray(asT),
        "bsT": np.ascontiguousarray(np.asarray(inputs["L0_ffn_lora_B_shared"], f32).T),
    }
    nc = build_nc()
    in_maps = [make_core_inputs(c, inputs, shared) for c in range(N_CORES)]
    res = run_bass_kernel_spmd(nc, in_maps, core_ids=list(range(N_CORES)))
    out = np.concatenate([res.results[c]["y"] for c in range(N_CORES)], axis=0)
    return out.reshape(B, S, E)


# version 07 (final)
# speedup vs baseline: 1.2013x; 1.0286x over previous
import numpy as np
import ml_dtypes
from contextlib import ExitStack

import concourse.bass as bass
import concourse.mybir as mybir
from concourse.bass_utils import run_bass_kernel_spmd
from concourse.aot_env import allow_ici_all_to_all

F32 = mybir.dt.float32
BF16 = mybir.dt.bfloat16
FP8 = mybir.dt.float8e4
AF = mybir.ActivationFunctionType
ALU = mybir.AluOpType
AX = mybir.AxisListType

N_CORES = 8
B, S, E = 4, 2048, 4096
T_ALL = B * S
T_OWN = T_ALL // N_CORES
NTT_OWN = T_OWN // 128
NEC = E // 128
HA_D = 128
ROT = 64
NUM_REMAIN = 410
NEG = -3.0e8
LN_EPS = 1e-5
RANK = 32
BF = ml_dtypes.bfloat16
_PHASE_LIMIT = None
_DEBUG_DUMPS = []
_SERIALIZE_UNITS = False
_SER_LN = False
_SER_AX = False


def _ppf(p):
    p = np.asarray(p, np.float64)
    a = [-3.969683028665376e+01, 2.209460984245205e+02, -2.759285104469687e+02,
         1.383577518672690e+02, -3.066479806614716e+01, 2.506628277459239e+00]
    b = [-5.447609879822406e+01, 1.615858368580409e+02, -1.556989798598866e+02,
         6.680131188771972e+01, -1.328068155288572e+01]
    c = [-7.784894002430293e-03, -3.223964580411365e-01, -2.400758277161838e+00,
         -2.549732539343734e+00, 4.374664141464968e+00, 2.938163982698783e+00]
    d = [7.784695709041462e-03, 3.224671290700398e-01, 2.445134137142996e+00,
         3.754408661907416e+00]
    out = np.empty_like(p)
    lo = p < 0.02425
    hi = p > 1 - 0.02425
    mid = ~(lo | hi)
    q = np.sqrt(-2 * np.log(np.where(lo, p, 0.5)))
    out_lo = (((((c[0] * q + c[1]) * q + c[2]) * q + c[3]) * q + c[4]) * q + c[5]) / \
             ((((d[0] * q + d[1]) * q + d[2]) * q + d[3]) * q + 1)
    q = p - 0.5
    r = q * q
    out_mid = (((((a[0] * r + a[1]) * r + a[2]) * r + a[3]) * r + a[4]) * r + a[5]) * q / \
              (((((b[0] * r + b[1]) * r + b[2]) * r + b[3]) * r + b[4]) * r + 1)
    q = np.sqrt(-2 * np.log1p(-np.where(hi, p, 0.5)))
    out_hi = -(((((c[0] * q + c[1]) * q + c[2]) * q + c[3]) * q + c[4]) * q + c[5]) / \
             ((((d[0] * q + d[1]) * q + d[2]) * q + d[3]) * q + 1)
    out[lo] = out_lo[lo]
    out[mid] = out_mid[mid]
    out[hi] = out_hi[hi]
    return out


def make_constants():
    cst = {}
    cst["ident_bf"] = np.eye(128).astype(BF)
    tri = np.tril(np.ones((128, 128), np.float32))
    cst["causal01"] = tri
    cst["causal_add"] = np.where(tri > 0, 0.0, NEG).astype(np.float32)
    inv = 1.0 / 10000.0 ** (np.arange(0, HA_D, 2, dtype=np.float64) / HA_D)
    t = np.arange(S, dtype=np.float64)
    emb = np.concatenate([np.outer(t, inv)] * 2, -1)
    cst["cosA"] = np.ascontiguousarray(
        np.cos(emb).astype(np.float32).reshape(16, 128, HA_D).transpose(1, 0, 2).reshape(128, 16 * HA_D))
    cst["sinA"] = np.ascontiguousarray(
        np.sin(emb).astype(np.float32).reshape(16, 128, HA_D).transpose(1, 0, 2).reshape(128, 16 * HA_D))
    invb = 1.0 / 10000.0 ** (np.arange(0, ROT, 2, dtype=np.float64) / ROT)
    sinu = np.outer(t, invb)
    cos2 = np.repeat(np.cos(sinu), 2, axis=1).astype(np.float32)
    sin2 = np.repeat(np.sin(sinu), 2, axis=1).astype(np.float32)
    cst["cosB"] = np.ascontiguousarray(cos2.reshape(16, 128, ROT).transpose(1, 0, 2).reshape(128, 16 * ROT))
    cst["sinB"] = np.ascontiguousarray(sin2.reshape(16, 128, ROT).transpose(1, 0, 2).reshape(128, 16 * ROT))
    n = np.arange(S, dtype=np.float64) + 1.0
    z = np.full(S, -25.0)
    big = n > NUM_REMAIN
    z[big] = _ppf(1.0 - NUM_REMAIN / n[big])
    cst["ztab"] = np.ascontiguousarray(z.reshape(16, 128).T.astype(np.float32))
    cst["invn"] = np.ascontiguousarray((1.0 / n).reshape(16, 128).T.astype(np.float32))
    return cst


CST = make_constants()
ENGINES = ["tensor", "vector", "scalar", "gpsimd", "sync"]


class Sem:
    def __init__(self, handle):
        self.h = handle
        self.n = 0


class Prog:
    def __init__(self, nc, stack):
        self.nc = nc
        self.ops = {e: [] for e in ENGINES}
        self.stack = stack
        self.sems = {}

    def sem(self, name):
        if name not in self.sems:
            self.sems[name] = Sem(self.stack.enter_context(self.nc.semaphore(name)))
        return self.sems[name]

    def op(self, eng, fn, inc=None, inc_n=1):
        if inc is not None:
            inc.n += inc_n
            self.ops[eng].append(lambda e, f=fn, s=inc.h, k=inc_n: f(e).then_inc(s, k))
        else:
            self.ops[eng].append(fn)

    def dma(self, eng, dst, src, sem):
        self.op(eng, lambda e, a=dst, b=src: e.dma_start(out=a, in_=b), inc=sem, inc_n=16)

    def wait(self, eng, sem, val=None):
        v = sem.n if val is None else val
        self.ops[eng].append(lambda e, s=sem.h, vv=v: e.wait_ge(s, vv))

    def barrier(self, name, engines=ENGINES):
        bar = self.sem("bar")
        for eng in engines:
            self.op(eng, lambda e: e.nop(), inc=bar)
        tgt = bar.n
        for eng in engines:
            self.wait(eng, bar, tgt)

    def run(self):
        with self.nc.Block() as block:
            for eng in ENGINES:
                def make(eng_name):
                    def body(e):
                        for fn in self.ops[eng_name]:
                            fn(e)
                    return body
                getattr(block, eng)(make(eng))


def build_nc():
    nc = bass.Bass("TRN2", target_bir_lowering=False, debug=False)
    d = {}

    def di(name, shape, dtype=F32):
        d[name] = nc.dram_tensor(name, list(shape), dtype, kind="ExternalInput").ap()
        return d[name]

    di("x_own", (T_OWN, E))
    di("wqkva", (E, 768))
    di("hp1", (128, 4 * 128)); di("hb1", (128, 4))
    di("hp2", (128, 4 * 128)); di("hb2", (128, 4))
    di("woaT", (E, E))
    di("wqkvb", (E, 1536))
    di("wob", (E, E))
    di("fker", (E, E)); di("fbias", (1, E))
    di("gate", (1, 16))
    di("gAT", (128, 16 * NEC * RANK))
    di("gBT", (128, 16 * 1024))
    di("asT", (128, NEC * RANK))
    di("bsT", (RANK, E))
    di("ident_bf", (128, 128), BF16)
    di("causal01", (128, 128))
    di("causal_add", (128, 128))
    di("cosA", (128, 16 * 128)); di("sinA", (128, 16 * 128))
    di("cosB", (128, 16 * ROT)); di("sinB", (128, 16 * ROT))
    di("ztab", (128, 16)); di("invn", (128, 16))
    y = nc.dram_tensor("y", [T_OWN, E], F32, kind="ExternalOutput").ap()

    ag_in = nc.dram_tensor("ag_in", [E, T_OWN], BF16).ap()
    ag_out = nc.dram_tensor("ag_out", [N_CORES * E, T_OWN], BF16).ap()
    a2a_in = nc.dram_tensor("a2a_in", [N_CORES * 512, T_OWN], BF16).ap()
    a2a_out = nc.dram_tensor("a2a_out", [N_CORES * 512, T_OWN], BF16).ap()
    xr = [nc.dram_tensor(f"xr{i}", [T_OWN, E], F32).ap() for i in range(3)]
    qa_d = nc.dram_tensor("qa_d", [512, T_ALL], BF16).ap()
    qha_d = nc.dram_tensor("qha_d", [512, T_ALL], FP8).ap()
    ka_d = nc.dram_tensor("ka_d", [128, T_ALL], BF16).ap()
    kha_d = nc.dram_tensor("kha_d", [512, T_ALL], FP8).ap()
    va_d = nc.dram_tensor("va_d", [T_ALL // 128, 128, 128], BF16).ap()
    qb_d = nc.dram_tensor("qb_d", [512, T_ALL], BF16).ap()
    kb_d = nc.dram_tensor("kb_d", [512, T_ALL], BF16).ap()
    vb_d = nc.dram_tensor("vb_d", [T_ALL // 128, 128, 512], BF16).ap()
    beff_d = nc.dram_tensor("beff_d", [128, RANK * E // 128], F32).ap()
    xlndbg = nc.dram_tensor("xlndbg", [T_OWN, E], BF16).ap()
    wdbg = nc.dram_tensor("wdbg", [128, NEC * 256], BF16).ap()

    stack = ExitStack()
    P = Prog(nc, stack)
    e_ = stack.enter_context

    ident_bf = e_(nc.sbuf_tensor("c_ident", [128, 128], BF16))
    causal01 = e_(nc.sbuf_tensor("c_c01", [128, 128], F32))
    causal_add = e_(nc.sbuf_tensor("c_cadd", [128, 128], F32))
    cosA = e_(nc.sbuf_tensor("c_cosA", [128, 16 * 128], F32))
    sinA = e_(nc.sbuf_tensor("c_sinA", [128, 16 * 128], F32))
    cosB = e_(nc.sbuf_tensor("c_cosB", [128, 16 * ROT], F32))
    sinB = e_(nc.sbuf_tensor("c_sinB", [128, 16 * ROT], F32))
    ztab = e_(nc.sbuf_tensor("c_ztab", [128, 16], F32))
    invn = e_(nc.sbuf_tensor("c_invn", [128, 16], F32))
    ones_bf = e_(nc.sbuf_tensor("c_onesbf", [1, 128], BF16))
    ones_f = e_(nc.sbuf_tensor("c_onesf", [1, 128], F32))
    gate_row = e_(nc.sbuf_tensor("c_gater", [1, 16], F32))
    gbc = e_(nc.sbuf_tensor("c_gbc", [128, 16], F32))
    fbias_sb = e_(nc.sbuf_tensor("c_fbias", [1, E], BF16))
    hb1_sb = e_(nc.sbuf_tensor("c_hb1", [128, 4], F32))
    hb2_sb = e_(nc.sbuf_tensor("c_hb2", [128, 4], F32))
    hp1_sb = e_(nc.sbuf_tensor("c_hp1", [128, 4 * 128], BF16))
    hp2_sb = e_(nc.sbuf_tensor("c_hp2", [128, 4 * 128], BF16))
    abT_sb = e_(nc.sbuf_tensor("c_abT", [128, NEC * 64], BF16))
    bbT_sb = e_(nc.sbuf_tensor("c_bbT", [64, E], BF16))

    pb = [e_(nc.psum_tensor(f"pb{i}", [128, 512], F32)) for i in range(6)]

    sW = P.sem("sW"); sX = P.sem("sX"); sSt = P.sem("sSt")
    sWp = [P.sem("sW0"), P.sem("sW1")]
    sXp = [P.sem("sX0"), P.sem("sX1")]
    sPE = P.sem("sPE"); sDV = P.sem("sDV"); sAC = P.sem("sAC")
    sCC = P.sem("sCC"); sSc = P.sem("sSc")

    P.dma("sync", ident_bf[:], d["ident_bf"][:], sX)
    P.dma("sync", causal01[:], d["causal01"][:], sX)
    P.dma("sync", causal_add[:], d["causal_add"][:], sX)
    P.dma("sync", cosA[:], d["cosA"][:], sX)
    P.dma("sync", sinA[:], d["sinA"][:], sX)
    P.dma("sync", cosB[:], d["cosB"][:], sX)
    P.dma("sync", sinB[:], d["sinB"][:], sX)
    P.dma("sync", ztab[:], d["ztab"][:], sX)
    P.dma("sync", invn[:], d["invn"][:], sX)
    P.dma("sync", gate_row[:], d["gate"][:], sX)
    P.dma("sync", hb1_sb[:, :], d["hb1"][:], sX)
    P.dma("sync", hb2_sb[:, :], d["hb2"][:], sX)
    P.dma("gpsimd", fbias_sb[:], d["fbias"][:], sW)
    P.dma("gpsimd", hp1_sb[:], d["hp1"][:], sW)
    P.dma("gpsimd", hp2_sb[:], d["hp2"][:], sW)
    sMs = P.sem("sMs")
    P.op("gpsimd", lambda e: e.memset(ones_bf[:], 1.0), inc=sMs, inc_n=1)
    P.op("gpsimd", lambda e: e.memset(ones_f[:], 1.0), inc=sMs, inc_n=1)

    with ExitStack() as ph:
        lA = ph.enter_context(nc.sbuf_tensor("l_A", [128, 16 * 1024], BF16))
        acc = [ph.enter_context(nc.sbuf_tensor(f"l_acc{i}", [128, 1024], F32)) for i in range(2)]
        asb = ph.enter_context(nc.sbuf_tensor("l_as", [128, 1024], BF16))
        P.dma("gpsimd", lA[:], d["gAT"][:], sW)
        P.dma("gpsimd", asb[:], d["asT"][:], sW)
        P.wait("tensor", sX, None)
        P.wait("tensor", sW, None)
        P.wait("tensor", sMs, None)
        P.op("tensor", lambda e: e.matmul(pb[0][:, 0:16], ones_f[:, :], gate_row[:, :], start=True, stop=True), inc=sPE)
        P.wait("vector", sPE, None)
        P.op("vector", lambda e: e.tensor_copy(gbc[:], pb[0][:, 0:16]), inc=sDV)
        P.wait("vector", sW, None)
        P.op("vector", lambda e: e.tensor_scalar(out=acc[0][:], in0=lA[:, 0:1024], scalar1=gbc[:, 0:1], scalar2=None, op0=ALU.mult))
        for n in range(1, 16):
            P.op("vector", lambda e, n2=n: e.scalar_tensor_tensor(
                out=acc[0][:], in0=lA[:, n2 * 1024:(n2 + 1) * 1024], scalar=gbc[:, n2:n2 + 1],
                in1=acc[0][:], op0=ALU.mult, op1=ALU.add))
        P.op("vector", lambda e: e.tensor_copy(
            abT_sb[:].rearrange("p (c w) -> p c w", w=64)[:, :, 0:32],
            acc[0][:].rearrange("p (c r) -> p c r", r=32)))
        P.op("vector", lambda e: e.tensor_copy(
            abT_sb[:].rearrange("p (c w) -> p c w", w=64)[:, :, 32:64],
            asb[:].rearrange("p (c r) -> p c r", r=32)), inc=sDV)
        P.wait("gpsimd", sDV, None)
        P.dma("gpsimd", lA[:], d["gBT"][:], sW)
        P.wait("vector", sW, None)
        P.op("vector", lambda e: e.tensor_scalar(out=acc[1][:], in0=lA[:, 0:1024], scalar1=gbc[:, 0:1], scalar2=None, op0=ALU.mult))
        for n in range(1, 16):
            P.op("vector", lambda e, n2=n: e.scalar_tensor_tensor(
                out=acc[1][:], in0=lA[:, n2 * 1024:(n2 + 1) * 1024], scalar=gbc[:, n2:n2 + 1],
                in1=acc[1][:], op0=ALU.mult, op1=ALU.add), inc=(sDV if n == 15 else None))
        P.wait("sync", sDV, None)
        P.dma("sync", beff_d[:], acc[1][:], sSt)
        P.wait("gpsimd", sSt, None)
        P.dma("gpsimd", bbT_sb[0:RANK, :], beff_d.rearrange("(r s) w -> r (s w)", s=4), sW)
        P.dma("gpsimd", bbT_sb[RANK:2 * RANK, :], d["bsT"][:], sW)
        P.barrier("prologue")

    def wait_stores(eng):
        for nm, s in list(P.sems.items()):
            if nm.startswith(("wostp", "ffstp", "pb_st")) or nm == "sSt":
                P.wait(eng, s, s.n)

    def ln_phase(src_dram, name, to_agin, afm=None):
        with ExitStack() as ph:
            xf32 = [ph.enter_context(nc.sbuf_tensor(f"ln{name}_x{i}", [128, E], F32)) for i in range(2)]
            lnst = ph.enter_context(nc.sbuf_tensor("ln_st" + name, [128, 48], F32))
            lnjk = ph.enter_context(nc.sbuf_tensor("ln_jk" + name, [128, 512], BF16))
            lnagg = ph.enter_context(nc.sbuf_tensor("ln_agg" + name, [128, 2], F32))
            lnsl = ph.enter_context(nc.sbuf_tensor("ln_sl" + name, [128, 8], F32))
            lnr = ph.enter_context(nc.sbuf_tensor("ln_r" + name, [128, 1], F32))
            lntmp = ph.enter_context(nc.sbuf_tensor("ln_tmp" + name, [128, 1], F32))
            xln = ph.enter_context(nc.sbuf_tensor("ln_xb" + name, [128, E], BF16))
            tevac = [ph.enter_context(nc.sbuf_tensor(f"ln{name}_tv{i}", [128, 512], BF16)) for i in range(2)]
            pt8 = [ph.enter_context(nc.psum_tensor(f"ln{name}_pt{i}", [128, 1024], BF16)) for i in range(2)]
            lnu = P.sem("lnu_" + name)
            wait_stores("sync")
            for tt in range(NTT_OWN):
                if _SER_LN:
                    P.barrier("serln")
                buf = xf32[tt % 2]
                if tt >= 2:
                    P.wait("sync", lnu, tt - 1)
                P.dma("sync", buf[:], src_dram[tt * 128:(tt + 1) * 128, :], sXp[tt % 2])
                P.wait("vector", sXp[tt % 2], None)
                for c8 in range(8):
                    P.op("vector", lambda e, b2=buf, c=c8: e.bn_stats(
                        lnst[:, c * 6:(c + 1) * 6], b2[:, c * 512:(c + 1) * 512]))
                P.op("vector", lambda e: e.tensor_copy(lnjk[:, 0:256], xln[:, 0:256]))
                P.op("vector", lambda e: e.tensor_copy(lnjk[:, 256:512], xln[:, 256:512]), inc=sDV)
                P.wait("vector", sDV, None)
                P.op("vector", lambda e: e.bn_aggr(lnagg[:], lnst[:]))
                P.op("vector", lambda e: e.tensor_copy(lnjk[:, 0:256], xln[:, 1024:1280]))
                P.op("vector", lambda e: e.tensor_copy(lnjk[:, 256:512], xln[:, 1280:1536]), inc=sDV)
                P.wait("vector", sDV, None)
                P.op("vector", lambda e: e.tensor_copy(lnsl[:, 2:3], lnagg[:, 0:1]))
                P.op("vector", lambda e: e.tensor_scalar(
                    out=lnsl[:, 5:6], in0=lnagg[:, 1:2], scalar1=LN_EPS, scalar2=None, op0=ALU.add))
                P.op("vector", lambda e: e.tensor_copy(lnjk[:, 0:256], xln[:, 512:768]))
                P.op("vector", lambda e: e.tensor_copy(lnjk[:, 256:512], xln[:, 768:1024]), inc=sDV)
                P.wait("scalar", sDV, None)
                P.op("scalar", lambda e: e.activation(lntmp[:], lnsl[:, 5:6], AF.Ln), inc=sAC)
                P.wait("scalar", sAC, None)
                P.op("scalar", lambda e: e.activation(lnr[:], lntmp[:], AF.Exp, scale=-0.5))
                P.op("scalar", lambda e: e.activation(lnsl[:, 7:8], lnsl[:, 6:7], AF.Ln))
                P.op("scalar", lambda e: e.activation(lnsl[:, 7:8], lnsl[:, 6:7], AF.Ln), inc=sAC)
                P.wait("vector", sAC, None)
                P.wait("vector", sPE, None)
                P.op("vector", lambda e, b2=buf: e.tensor_scalar(
                    out=xln[:], in0=b2[:], scalar1=lnsl[:, 2:3], scalar2=lnr[:],
                    op0=ALU.subtract, op1=ALU.mult))
                P.op("vector", lambda e: e.tensor_copy(lnjk[:, 0:512], xln[:, 0:512]))
                P.op("vector", lambda e: e.tensor_copy(lnjk[:, 0:512], xln[:, 2048:2560]))
                P.op("vector", lambda e: e.tensor_copy(lnjk[:, 0:512], xln[:, E - 512:E]), inc=sDV)
                P.op("vector", lambda e: e.nop(), inc=lnu)
                P.wait("tensor", sDV, None)
                for g in range(4):
                    bank = pt8[g % 2]
                    P.wait("tensor", sAC, None)
                    for k in range(8):
                        ec = g * 8 + k
                        P.op("tensor", lambda e, bk=bank, c=ec, k2=k: e.transpose(
                            bk[:, k2 * 128:(k2 + 1) * 128], xln[:, c * 128:(c + 1) * 128], ident_bf[:]),
                            inc=(sPE if k == 7 else None))
                    P.wait("scalar", sPE, None)
                    if to_agin:
                        P.wait("scalar", sSt, None)
                        for half in range(2):
                            P.op("scalar", lambda e, bk=bank, h=half: e.copy(
                                tevac[h][:], bk[:, h * 512:(h + 1) * 512]), inc=sAC)
                        P.wait("sync", sAC, None)
                        for half in range(2):
                            ec0 = g * 8 + half * 4
                            P.dma("sync",
                                  ag_in[ec0 * 128:(ec0 + 4) * 128, tt * 128:(tt + 1) * 128].rearrange("(c p) t -> p c t", p=128),
                                  tevac[half][:].rearrange("p (c t) -> p c t", t=128), sSt)
                    else:
                        for half in range(2):
                            ec0 = g * 8 + half * 4
                            dst = afm[:].rearrange("p (c t) -> p c t", t=T_OWN)[:, ec0:ec0 + 4, tt * 128:(tt + 1) * 128]
                            P.op("scalar", lambda e, bk=bank, h=half, dd=dst: e.copy(
                                dd, bk[:, h * 512:(h + 1) * 512]), inc=sAC)
            P.barrier("ln_" + name)

    def allgather():
        P.wait("gpsimd", sSt, None)
        P.op("gpsimd", lambda e: e.collective_compute(
            "AllGather", ALU.bypass, replica_groups=[list(range(N_CORES))],
            ins=[ag_in.opt()], outs=[ag_out.opt()]), inc=sCC)
        for eng in ENGINES:
            P.wait(eng, sCC, None)

    def alltoall():
        P.wait("gpsimd", sSt, None)
        def go(e):
            with allow_ici_all_to_all():
                return e.collective_compute(
                    "AllToAll", ALU.bypass, replica_groups=[list(range(N_CORES))],
                    ins=[a2a_in.opt()], outs=[a2a_out.opt()])
        P.op("gpsimd", go, inc=sCC)
        for eng in ENGINES:
            P.wait(eng, sCC, None)

    def rope_a(sc, dst, psrc, pc, sbase, dbase):
        cA = cosA[:, pc * 128:(pc + 1) * 128]
        sA = sinA[:, pc * 128:(pc + 1) * 128]
        rt = sc["ropet"]
        P.op("vector", lambda e: e.tensor_tensor(rt[:, 0:128], psrc[:, sbase:sbase + 128], cA, ALU.mult))
        P.op("vector", lambda e: e.tensor_tensor(rt[:, 128:192], psrc[:, sbase + 64:sbase + 128], sA[:, 0:64], ALU.mult))
        P.op("vector", lambda e: e.tensor_tensor(rt[:, 192:256], psrc[:, sbase:sbase + 64], sA[:, 64:128], ALU.mult))
        P.op("vector", lambda e: e.tensor_sub(dst[:, dbase:dbase + 64], rt[:, 0:64], rt[:, 128:192]))
        P.op("vector", lambda e: e.tensor_add(dst[:, dbase + 64:dbase + 128], rt[:, 64:128], rt[:, 192:256]))

    def hash_head(sc, h, src_fm, dst):
        hsil, hy, hz = sc["hsil"], sc["hy"], sc["hz"]
        P.op("tensor", lambda e: e.matmul(pb[4][:], hp1_sb[:, h * 128:(h + 1) * 128], src_fm, start=True, stop=True), inc=sPE)
        P.wait("scalar", sPE, None)
        P.op("scalar", lambda e: e.activation(hsil[:], pb[4][:], AF.Silu, bias=hb1_sb[:, h:h + 1]), inc=sAC)
        P.wait("vector", sAC, None)
        P.op("vector", lambda e: e.tensor_tensor(hy[:], hsil[:], src_fm, ALU.add), inc=sDV)
        P.wait("tensor", sDV, None)
        P.op("tensor", lambda e: e.matmul(pb[5][:], hp2_sb[:, h * 128:(h + 1) * 128], hy[:], start=True, stop=True), inc=sPE)
        P.wait("vector", sPE, None)
        P.op("vector", lambda e: e.scalar_tensor_tensor(
            out=hz[:], in0=pb[5][:], scalar=hb2_sb[:, h:h + 1], in1=hy[:],
            op0=ALU.add, op1=ALU.add), inc=sDV)
        P.wait("scalar", sDV, None)
        P.op("scalar", lambda e: e.activation(dst, hz[:], AF.Sign), inc=sAC)

    def proj_a():
        with ExitStack() as ph:
            wsb = ph.enter_context(nc.sbuf_tensor("pa_w", [128, NEC * 768], BF16))
            xsl = [ph.enter_context(nc.sbuf_tensor(f"pa_x{i}", [128, NEC * 512], BF16)) for i in range(2)]
            sc = dict(
                qr=ph.enter_context(nc.sbuf_tensor("pa_qr", [128, 512], BF16)),
                kr=ph.enter_context(nc.sbuf_tensor("pa_kr", [128, 128], BF16)),
                ropet=ph.enter_context(nc.sbuf_tensor("pa_rt", [128, 256], F32)),
                qT=ph.enter_context(nc.sbuf_tensor("pa_qT", [128, 4 * 512], BF16)),
                kT=ph.enter_context(nc.sbuf_tensor("pa_kT", [128, 512], BF16)),
                v=ph.enter_context(nc.sbuf_tensor("pa_v", [128, 128], BF16)),
                hsil=ph.enter_context(nc.sbuf_tensor("pa_hs", [128, 512], F32)),
                hy=ph.enter_context(nc.sbuf_tensor("pa_hy", [128, 512], BF16)),
                hz=ph.enter_context(nc.sbuf_tensor("pa_hz", [128, 512], F32)),
                qh=ph.enter_context(nc.sbuf_tensor("pa_qh", [128, 4 * 512], FP8)),
                kh=ph.enter_context(nc.sbuf_tensor("pa_kh", [128, 4 * 512], FP8)),
            )
            pt8 = [ph.enter_context(nc.psum_tensor(f"pa_pt{i}", [128, 1024], BF16)) for i in range(2)]
            for ec in range(NEC):
                P.dma("gpsimd", wsb[:, ec * 768:(ec + 1) * 768], d["wqkva"][ec * 128:(ec + 1) * 128, :], sW)
            evt = P.sem("pa_evt")
            for hs in range(16):
                buf = xsl[hs % 2]
                r, half = hs // 2, hs % 2
                if hs >= 2:
                    P.wait("sync", sPE, None)
                for ec in range(NEC):
                    P.dma("sync", buf[:, ec * 512:(ec + 1) * 512],
                          ag_out[r * E + ec * 128:r * E + (ec + 1) * 128, half * 512:(half + 1) * 512], sXp[hs % 2])
                P.wait("tensor", sXp[hs % 2], None)
                P.wait("tensor", sW, None)
                for tt4 in range(4):
                    kglob = hs * 4 + tt4
                    pc = kglob % 16
                    bq = pb[tt4 % 2]
                    bkv = pb[2 + tt4 % 2]
                    if kglob >= 2:
                        P.wait("tensor", evt, kglob - 1)
                    for ec in range(NEC):
                        xt = buf[:, ec * 512 + tt4 * 128: ec * 512 + (tt4 + 1) * 128]
                        P.op("tensor", lambda e, bk=bq, x=xt, ec2=ec: e.matmul(
                            bk[:], x, wsb[:, ec2 * 768:ec2 * 768 + 512],
                            start=(ec2 == 0), stop=(ec2 == NEC - 1)))
                    for ec in range(NEC):
                        xt = buf[:, ec * 512 + tt4 * 128: ec * 512 + (tt4 + 1) * 128]
                        P.op("tensor", lambda e, bk=bkv, x=xt, ec2=ec: e.matmul(
                            bk[:, 0:256], x, wsb[:, ec2 * 768 + 512:(ec2 + 1) * 768],
                            start=(ec2 == 0), stop=(ec2 == NEC - 1)), inc=(sPE if ec == NEC - 1 else None))
                    P.wait("vector", sPE, None)
                    P.wait("vector", sSt, None)
                    for h in range(4):
                        rope_a(sc, sc["qr"], bq, pc, h * 128, h * 128)
                    rope_a(sc, sc["kr"], bkv, pc, 0, 0)
                    P.op("vector", lambda e, bk=bkv: e.tensor_copy(sc["v"][:], bk[:, 128:256]), inc=sDV)
                    P.wait("tensor", sDV, None)
                    bank = pt8[tt4 % 2]
                    for h in range(4):
                        P.op("tensor", lambda e, bk=bank, h2=h: e.transpose(
                            bk[:, h2 * 128:(h2 + 1) * 128], sc["qr"][:, h2 * 128:(h2 + 1) * 128], ident_bf[:]))
                    P.op("tensor", lambda e, bk=bank: e.transpose(bk[:, 512:640], sc["kr"][:, 0:128], ident_bf[:]), inc=sPE)
                    P.wait("scalar", sPE, None)
                    for h in range(4):
                        P.op("scalar", lambda e, bk=bank, h2=h, t4=tt4: e.copy(
                            sc["qT"][:, h2 * 512 + t4 * 128: h2 * 512 + (t4 + 1) * 128], bk[:, h2 * 128:(h2 + 1) * 128]))
                    P.op("scalar", lambda e, bk=bank, t4=tt4: e.copy(
                        sc["kT"][:, t4 * 128:(t4 + 1) * 128], bk[:, 512:640]), inc=sAC)
                    P.wait("sync", sDV, None)
                    P.dma("sync", va_d[kglob, :, :], sc["v"][:], sSt)
                    P.op("scalar", lambda e: e.nop(), inc=evt)
                P.wait("tensor", sAC, None)
                for h in range(4):
                    hash_head(sc, h, sc["qT"][:, h * 512:(h + 1) * 512], sc["qh"][:, h * 512:(h + 1) * 512])
                for h in range(4):
                    hash_head(sc, h, sc["kT"][:, 0:512], sc["kh"][:, h * 512:(h + 1) * 512])
                P.wait("sync", sAC, None)
                t0 = hs * 512
                for h in range(4):
                    P.dma("sync", qa_d[h * 128:(h + 1) * 128, t0:t0 + 512], sc["qT"][:, h * 512:(h + 1) * 512], sSt)
                    P.dma("sync", qha_d[h * 128:(h + 1) * 128, t0:t0 + 512], sc["qh"][:, h * 512:(h + 1) * 512], sSt)
                    P.dma("sync", kha_d[h * 128:(h + 1) * 128, t0:t0 + 512], sc["kh"][:, h * 512:(h + 1) * 512], sSt)
                P.dma("sync", ka_d[:, t0:t0 + 512], sc["kT"][:, 0:512], sSt)
            P.barrier("proj_a")

    def attn_a():
        with ExitStack() as ph:
            q_sb = ph.enter_context(nc.sbuf_tensor("aa_q", [128, 4 * S], BF16))
            qh_sb = ph.enter_context(nc.sbuf_tensor("aa_qh", [128, 4 * S], FP8))
            k_sb = ph.enter_context(nc.sbuf_tensor("aa_k", [128, S], BF16))
            kh_sb = ph.enter_context(nc.sbuf_tensor("aa_kh", [128, 4 * S], FP8))
            v_sb = ph.enter_context(nc.sbuf_tensor("aa_v", [128, S], BF16))
            sim2 = [ph.enter_context(nc.sbuf_tensor(f"aa_sim{i}", [128, S], BF16)) for i in range(2)]
            ms2 = [ph.enter_context(nc.sbuf_tensor(f"aa_ms{i}", [128, S], F32)) for i in range(2)]
            p2 = [ph.enter_context(nc.sbuf_tensor(f"aa_p{i}", [128, S], BF16)) for i in range(2)]
            pt_sb = ph.enter_context(nc.sbuf_tensor("aa_pt", [128, 1024], BF16))
            sl2 = [ph.enter_context(nc.sbuf_tensor(f"aa_sl{i}", [128, 24], F32)) for i in range(2)]
            th2 = [ph.enter_context(nc.sbuf_tensor(f"aa_th{i}", [128, 4], F32)) for i in range(2)]
            dg2 = [ph.enter_context(nc.sbuf_tensor(f"aa_dg{i}", [128, 128], BF16)) for i in range(2)]
            mk2 = [ph.enter_context(nc.sbuf_tensor(f"aa_mk{i}", [128, 512], BF16)) for i in range(2)]
            jk2 = [ph.enter_context(nc.sbuf_tensor(f"aa_jk{i}", [128, 512], BF16)) for i in range(2)]
            avst = ph.enter_context(nc.sbuf_tensor("aa_av", [128, 8192], BF16))
            ptf = [ph.enter_context(nc.psum_tensor(f"aa_ptb{i}", [128, 512], F32)) for i in range(2)]
            bufs = dict(q=q_sb, qh=qh_sb, k=k_sb, kh=kh_sb, v=v_sb, sim=sim2, ms=ms2,
                        p=p2, pt=pt_sb, sl=sl2, th=th2, dg=dg2, mk=mk2, jk=jk2,
                        av=avst, ptf=ptf, hstride=S, D=128, scale=float(1.0 / np.sqrt(128.0)))
            run_attn(bufs, kind="a")

    def attn_b():
        with ExitStack() as ph:
            q_sb = ph.enter_context(nc.sbuf_tensor("ab_q", [128, 4 * S], BF16))
            k_sb = ph.enter_context(nc.sbuf_tensor("ab_k", [128, 4 * S], BF16))
            v_sb = ph.enter_context(nc.sbuf_tensor("ab_v", [128, 4 * S], BF16))
            ms2 = [ph.enter_context(nc.sbuf_tensor(f"ab_ms{i}", [128, S], F32)) for i in range(2)]
            p2 = [ph.enter_context(nc.sbuf_tensor(f"ab_p{i}", [128, S], BF16)) for i in range(2)]
            pt_sb = ph.enter_context(nc.sbuf_tensor("ab_pt", [128, 1024], BF16))
            sl2 = [ph.enter_context(nc.sbuf_tensor(f"ab_sl{i}", [128, 24], F32)) for i in range(2)]
            dg2 = [ph.enter_context(nc.sbuf_tensor(f"ab_dg{i}", [128, 128], BF16)) for i in range(2)]
            avst = ph.enter_context(nc.sbuf_tensor("ab_av", [128, 8192], BF16))
            ptf = [ph.enter_context(nc.psum_tensor(f"ab_ptb{i}", [128, 512], F32)) for i in range(2)]
            bufs = dict(q=q_sb, k=k_sb, v=v_sb, ms=ms2, p=p2, pt=pt_sb, sl=sl2,
                        dg=dg2, av=avst, ptf=ptf, scale=float(1.0 / 16.0))
            run_attn(bufs, kind="b")

    def run_attn(bufs, kind):
        nheads = 4 if kind == "a" else 2
        pre = "aa" if kind == "a" else "ab"
        esim = P.sem(pre + "_esim"); emsk = P.sem(pre + "_emsk")
        eexp = P.sem(pre + "_eexp"); eptv = P.sem(pre + "_eptv")
        eav = P.sem(pre + "_eav"); esum = P.sem(pre + "_esum")
        ethe = P.sem(pre + "_ethe"); dnr = P.sem(pre + "_dnr")
        tve1 = P.sem(pre + "_tve1"); tac = P.sem(pre + "_tac")
        psim = P.sem(pre + "_psim"); psc = P.sem(pre + "_psc")
        pptg = P.sem(pre + "_pptg"); pavg = P.sem(pre + "_pavg")
        un = P.sem(pre + "_un")
        ctrs = dict(simg=0, scg=0, ptg=0, avg=0)
        rec = {}
        expCnt = {}; ptgCnt = {}; avgCnt = {}

        def front(u, b, h, qt):
            if _SERIALIZE_UNITS:
                for en in ("tensor", "vector", "scalar"):
                    for sm in (esim, emsk, eexp, eptv, eav, esum, ethe, dnr, tac,
                               psim, psc, pptg, pavg, sDV, sAC):
                        P.wait(en, sm, sm.n)
            p = u % 2
            nk = qt + 1
            ng = (nk + 3) // 4
            sim_sb = bufs["sim"][p] if kind == "a" else None
            ms_sb = bufs["ms"][p]
            slots = bufs["sl"][p]
            theta = bufs["th"][p] if kind == "a" else None
            mskt = bufs["mk"][p] if kind == "a" else None
            junk = bufs["jk"][p] if kind == "a" else None
            S_ = S
            if kind == "a":
                qcol = slice(h * S_ + qt * 128, h * S_ + (qt + 1) * 128)
                P.op("vector", lambda e, s2=slots: e.memset(s2[:], 0.0))
                for g in range(ng):
                    k0 = g * 512
                    w = min(512, nk * 128 - k0)
                    sg = ctrs["simg"]; ctrs["simg"] += 1
                    bank = pb[sg % 2]
                    if sg >= 2:
                        P.wait("tensor", esim, sg - 1)
                    P.op("tensor", lambda e, k2=k0, w2=w, bk=bank: e.matmul(
                        bk[:, 0:w2], bufs["qh"][:, qcol], bufs["kh"][:, h * S_ + k2: h * S_ + k2 + w2],
                        start=True, stop=True), inc=psim)
                    P.wait("vector", psim, psim.n)
                    if g < ng - 1:
                        P.op("vector", lambda e, k2=k0, w2=w, g2=g, bk=bank: e.tensor_scalar(
                            out=sim_sb[:, k2:k2 + w2], in0=bk[:, 0:w2], scalar1=1.0, scalar2=0.0,
                            op0=ALU.mult, op1=ALU.add, accum_out=slots[:, g2:g2 + 1]))
                        P.op("vector", lambda e, k2=k0, w2=w, g2=g: e.scalar_tensor_tensor(
                            out=junk[:, 0:w2], in0=sim_sb[:, k2:k2 + w2], scalar=1.0, in1=sim_sb[:, k2:k2 + w2],
                            op0=ALU.mult, op1=ALU.mult, accum_out=slots[:, 8 + g2:9 + g2]), inc=esim)
                    else:
                        nd = w - 128
                        if nd > 0:
                            P.op("vector", lambda e, k2=k0, w2=nd, g2=g, bk=bank: e.tensor_scalar(
                                out=sim_sb[:, k2:k2 + w2], in0=bk[:, 0:w2], scalar1=1.0, scalar2=0.0,
                                op0=ALU.mult, op1=ALU.add, accum_out=slots[:, g2:g2 + 1]))
                            P.op("vector", lambda e, k2=k0, w2=nd: e.scalar_tensor_tensor(
                                out=junk[:, 0:w2], in0=sim_sb[:, k2:k2 + w2], scalar=1.0, in1=sim_sb[:, k2:k2 + w2],
                                op0=ALU.mult, op1=ALU.mult, accum_out=slots[:, 13:14]))
                        P.op("vector", lambda e, k2=k0 + nd, nd2=nd, bk=bank: e.tensor_tensor(
                            sim_sb[:, k2:k2 + 128], bk[:, nd2:nd2 + 128], causal01[:], ALU.mult))
                        P.op("vector", lambda e, k2=k0 + nd: e.tensor_scalar(
                            out=junk[:, 0:128], in0=sim_sb[:, k2:k2 + 128], scalar1=1.0, scalar2=0.0,
                            op0=ALU.mult, op1=ALU.add, accum_out=slots[:, 4:5]))
                        P.op("vector", lambda e, k2=k0 + nd: e.scalar_tensor_tensor(
                            out=junk[:, 0:128], in0=sim_sb[:, k2:k2 + 128], scalar=1.0, in1=sim_sb[:, k2:k2 + 128],
                            op0=ALU.mult, op1=ALU.mult, accum_out=slots[:, 12:13]), inc=esim)
                P.op("vector", lambda e: e.nop(), inc=sDV)
                P.wait("vector", sDV, None)
                P.op("vector", lambda e, s2=slots: e.reduce_sum(s2[:, 14:15], s2[:, 0:6], axis=AX.X))
                P.op("vector", lambda e, s2=slots: e.reduce_sum(s2[:, 15:16], s2[:, 8:14], axis=AX.X), inc=sDV)
                P.wait("vector", sDV, None)
                P.op("vector", lambda e, s2=slots, p2=qt: e.tensor_scalar(
                    out=s2[:, 16:17], in0=s2[:, 14:15], scalar1=invn[:, p2:p2 + 1], scalar2=None, op0=ALU.mult))
                P.op("vector", lambda e, s2=slots, p2=qt: e.tensor_scalar(
                    out=s2[:, 17:18], in0=s2[:, 15:16], scalar1=invn[:, p2:p2 + 1], scalar2=None, op0=ALU.mult), inc=sDV)
                P.wait("vector", sDV, None)
                P.op("vector", lambda e, s2=slots: e.scalar_tensor_tensor(
                    out=s2[:, 18:19], in0=s2[:, 16:17], scalar=-1.0, in1=s2[:, 16:17], op0=ALU.mult, op1=ALU.mult), inc=sDV)
                P.wait("vector", sDV, None)
                P.op("vector", lambda e, s2=slots: e.tensor_tensor(s2[:, 19:20], s2[:, 17:18], s2[:, 18:19], ALU.add), inc=sDV)
                P.wait("vector", sDV, None)
                P.op("vector", lambda e, s2=slots: e.tensor_scalar(
                    out=s2[:, 19:20], in0=s2[:, 19:20], scalar1=1e-6, scalar2=None, op0=ALU.max), inc=tve1)
                P.wait("scalar", tve1, tve1.n)
                P.op("scalar", lambda e, s2=slots: e.activation(s2[:, 20:21], s2[:, 19:20], AF.Ln), inc=sAC)
                P.wait("scalar", sAC, None)
                P.op("scalar", lambda e, s2=slots: e.activation(s2[:, 21:22], s2[:, 20:21], AF.Exp, scale=0.5), inc=tac)
                P.wait("vector", tac, tac.n)
                P.op("vector", lambda e, s2=slots, p2=qt: e.tensor_scalar(
                    out=s2[:, 22:23], in0=s2[:, 21:22], scalar1=ztab[:, p2:p2 + 1], scalar2=None, op0=ALU.mult), inc=sDV)
                P.wait("vector", sDV, None)
                P.op("vector", lambda e, s2=slots, t2=theta: e.tensor_tensor(t2[:, 0:1], s2[:, 16:17], s2[:, 22:23], ALU.add), inc=sDV)
                P.wait("vector", sDV, None)
                P.op("vector", lambda e, t2=theta: e.tensor_scalar(
                    out=t2[:, 0:1], in0=t2[:, 0:1], scalar1=127.0, scalar2=None, op0=ALU.min), inc=ethe)
            for g in range(ng):
                k0 = g * 512
                w = min(512, nk * 128 - k0)
                sg = ctrs["scg"]; ctrs["scg"] += 1
                bank = pb[2 + sg % 2]
                if sg >= 2:
                    P.wait("tensor", emsk, sg - 1)
                if kind == "a":
                    P.op("tensor", lambda e, k2=k0, w2=w, bk=bank: e.matmul(
                        bk[:, 0:w2], bufs["q"][:, qcol], bufs["k"][:, k2:k2 + w2],
                        start=True, stop=True), inc=psc)
                else:
                    for dc in range(2):
                        dcg = h * 2 + dc
                        P.op("tensor", lambda e, k2=k0, w2=w, d2=dcg, d3=dc, q2=qt, bk=bank: e.matmul(
                            bk[:, 0:w2], bufs["q"][:, d2 * S + q2 * 128:d2 * S + (q2 + 1) * 128],
                            bufs["k"][:, d2 * S + k2:d2 * S + k2 + w2],
                            start=(d3 == 0), stop=(d3 == 1)), inc=(psc if dc == 1 else None))
                P.wait("vector", psc, psc.n)
                if u >= 2:
                    P.wait("vector", eexp, expCnt.get(u - 2, 0))
                if kind == "a":
                    P.op("vector", lambda e, k2=k0, w2=w, t2=theta: e.tensor_scalar(
                        out=mskt[:, 0:w2], in0=sim_sb[:, k2:k2 + w2], scalar1=t2[:, 0:1], scalar2=NEG,
                        op0=ALU.is_lt, op1=ALU.mult), inc=sDV)
                    P.wait("vector", sDV, None)
                    P.op("vector", lambda e, k2=k0, w2=w, bk=bank: e.scalar_tensor_tensor(
                        out=ms_sb[:, k2:k2 + w2], in0=bk[:, 0:w2], scalar=bufs["scale"],
                        in1=mskt[:, 0:w2], op0=ALU.mult, op1=ALU.add))
                else:
                    P.op("vector", lambda e, k2=k0, w2=w, bk=bank: e.tensor_scalar(
                        out=ms_sb[:, k2:k2 + w2], in0=bk[:, 0:w2], scalar1=bufs["scale"], scalar2=None,
                        op0=ALU.mult))
                if g == ng - 1:
                    kd = nk * 128 - 128
                    P.op("vector", lambda e, k2=kd: e.tensor_tensor(
                        ms_sb[:, k2:k2 + 128], ms_sb[:, k2:k2 + 128], causal_add[:], ALU.add), inc=sDV)
                    P.wait("vector", sDV, None)
                P.op("vector", lambda e, k2=k0, w2=w, g2=g, s2=slots: e.reduce_max(
                    s2[:, g2:g2 + 1], ms_sb[:, k2:k2 + w2], axis=AX.X), inc=emsk)
            P.wait("vector", emsk, emsk.n)
            P.op("vector", lambda e, g2=ng, s2=slots: e.reduce_max(s2[:, 6:7], s2[:, 0:g2], axis=AX.X), inc=sDV)
            P.wait("vector", sDV, None)
            P.op("vector", lambda e, s2=slots: e.tensor_scalar(
                out=s2[:, 7:8], in0=s2[:, 6:7], scalar1=-1.0, scalar2=None, op0=ALU.mult), inc=dnr)
            P.wait("scalar", dnr, dnr.n)
            for g in range(ng):
                k0 = g * 512
                w = min(512, nk * 128 - k0)
                if u >= 2:
                    P.wait("scalar", pptg, ptgCnt.get(u - 2, 0))
                P.op("scalar", lambda e, k2=k0, w2=w, g2=g, s2=slots, pp=bufs["p"][p]: e.activation(
                    pp[:, k2:k2 + w2], ms_sb[:, k2:k2 + w2], AF.Exp, bias=s2[:, 7:8],
                    accum_out=s2[:, 8 + g2:9 + g2]), inc=eexp)
            expCnt[u] = eexp.n

        def back(u, b, h, qt):
            if _SERIALIZE_UNITS:
                for en in ("tensor", "vector", "scalar"):
                    for sm in (esim, emsk, eexp, eptv, eav, esum, ethe, dnr, tac,
                               psim, psc, pptg, pavg, sDV, sAC):
                        P.wait(en, sm, sm.n)
            p = u % 2
            nk = qt + 1
            ng = (nk + 3) // 4
            slots = bufs["sl"][p]
            diagr = bufs["dg"][p]
            p_sb = bufs["p"][p]
            pt_sb = bufs["pt"]
            ptf = bufs["ptf"]
            P.wait("vector", eexp, expCnt[u])
            P.op("vector", lambda e, g2=ng, s2=slots: e.reduce_sum(s2[:, 16:17], s2[:, 8:8 + g2], axis=AX.X), inc=sDV)
            P.wait("vector", sDV, None)
            P.op("vector", lambda e, s2=slots: e.reciprocal(s2[:, 17:18], s2[:, 16:17]), inc=sDV)
            P.wait("vector", sDV, None)
            if u >= 2:
                P.wait("vector", pptg, ptgCnt[u - 2])
            P.op("vector", lambda e, s2=slots, dg=diagr: e.tensor_scalar(
                out=dg[:], in0=ident_bf[:], scalar1=s2[:, 17:18], scalar2=None, op0=ALU.mult), inc=esum)
            npt = (nk + 3) // 4
            av_bank = pb[4 + (u % 2)]
            ptg0 = ctrs["ptg"]
            first_av = True

            def emit_pt(gi):
                ptg = ctrs["ptg"]; ctrs["ptg"] += 1
                bank = ptf[ptg % 2]
                base = gi * 4
                cnt = min(4, nk - base)
                if gi == 0:
                    P.wait("tensor", esum, esum.n)
                if ptg >= 2:
                    P.wait("tensor", eptv, ptg - 1)
                for j in range(cnt):
                    kc = base + j
                    P.op("tensor", lambda e, k2=kc, j2=j, bk=bank, dg=diagr: e.matmul(
                        bk[:, j2 * 128:(j2 + 1) * 128], p_sb[:, k2 * 128:(k2 + 1) * 128], dg[:],
                        start=True, stop=True), inc=(pptg if j == cnt - 1 else None))
                P.wait("scalar", pptg, pptg.n)
                if ptg >= 2:
                    P.wait("scalar", pavg, ptg - 1)
                P.op("scalar", lambda e, bk=bank, w2=cnt * 128, sl2=(ptg % 2): e.copy(
                    pt_sb[:, sl2 * 512:sl2 * 512 + w2], bk[:, 0:w2]), inc=eptv)

            def emit_av(gi):
                nonlocal first_av
                ptg = ptg0 + gi
                base = gi * 4
                cnt = min(4, nk - base)
                P.wait("tensor", eptv, ptg + 1)
                if first_av:
                    P.wait("tensor", eav, max(0, u - 1))
                    first_av = False
                for j in range(cnt):
                    kc = base + j
                    if bufs.get("k") is not None and "sim" in bufs:
                        P.op("tensor", lambda e, k2=kc, j2=j, bk=av_bank, sl2=(ptg % 2): e.matmul(
                            bk[:, 0:128], bufs["v"][:, k2 * 128:(k2 + 1) * 128],
                            pt_sb[:, sl2 * 512 + j2 * 128:sl2 * 512 + (j2 + 1) * 128],
                            start=(k2 == 0), stop=(k2 == nk - 1)),
                            inc=(pavg if (j == cnt - 1) else None))
                    else:
                        for vc in range(2):
                            vcg = h * 2 + vc
                            P.op("tensor", lambda e, k2=kc, j2=j, v2=vcg, v3=vc, bk=av_bank, sl2=(ptg % 2): e.matmul(
                                bk[:, v3 * 128:(v3 + 1) * 128],
                                bufs["v"][:, k2 * 512 + v2 * 128:k2 * 512 + (v2 + 1) * 128],
                                pt_sb[:, sl2 * 512 + j2 * 128:sl2 * 512 + (j2 + 1) * 128],
                                start=(k2 == 0 and v3 == 0), stop=(k2 == nk - 1)),
                                inc=(pavg if (j == cnt - 1 and vc == 1) else None))

            emit_pt(0)
            if npt > 1:
                emit_pt(1)
            for gi in range(npt):
                if gi + 2 < npt:
                    pass
                emit_av(gi)
                if gi + 2 < npt:
                    emit_pt(gi + 2)
                elif gi + 2 == npt:
                    pass
            ptgCnt[u] = ctrs["ptg"]
            avgCnt[u] = pavg.n
            P.wait("vector", pavg, pavg.n)
            if kind == "a":
                P.op("vector", lambda e, h2=h, q2=qt, bk=av_bank: e.tensor_copy(
                    bufs["av"][:, h2 * 2048 + q2 * 128: h2 * 2048 + (q2 + 1) * 128], bk[:, 0:128]), inc=eav)
            else:
                for vc in range(2):
                    r2 = h * 2 + vc
                    P.op("vector", lambda e, r3=r2, q2=qt, v3=vc, bk=av_bank: e.tensor_copy(
                        bufs["av"][:, r3 * 2048 + q2 * 128: r3 * 2048 + (q2 + 1) * 128],
                        bk[:, v3 * 128:(v3 + 1) * 128]), inc=(eav if vc == 1 else None))
            P.op("vector", lambda e: e.nop(), inc=un)

        uidx = 0
        for b in range(B):
            t0 = b * S
            P.wait("sync", un, None)
            wait_stores("sync")
            if kind == "a":
                P.dma("sync", bufs["k"][:], ka_d[:, t0:t0 + S], sX)
                P.dma("sync", bufs["v"][:].rearrange("p (tt v) -> p tt v", v=128),
                      va_d[b * 16:(b + 1) * 16, :, :].rearrange("tt p v -> p tt v"), sX)
                for hh in range(4):
                    P.dma("sync", bufs["kh"][:, hh * S:(hh + 1) * S], kha_d[hh * 128:(hh + 1) * 128, t0:t0 + S], sX)
                    P.dma("sync", bufs["q"][:, hh * S:(hh + 1) * S], qa_d[hh * 128:(hh + 1) * 128, t0:t0 + S], sX)
                    P.dma("sync", bufs["qh"][:, hh * S:(hh + 1) * S], qha_d[hh * 128:(hh + 1) * 128, t0:t0 + S], sX)
            else:
                P.dma("sync", bufs["q"][:].rearrange("p (dc t) -> p dc t", t=S), qb_d[:, t0:t0 + S].rearrange("(dc p) t -> p dc t", p=128), sX)
                P.dma("sync", bufs["k"][:].rearrange("p (dc t) -> p dc t", t=S), kb_d[:, t0:t0 + S].rearrange("(dc p) t -> p dc t", p=128), sX)
                P.dma("sync", bufs["v"][:].rearrange("p (tt v) -> p tt v", v=512), vb_d[b * 16:(b + 1) * 16, :, :].rearrange("tt p v -> p tt v"), sX)
            for eng in ("tensor", "vector", "scalar"):
                P.wait(eng, sX, None)
            P.wait("vector", sSt, None)
            units = [(hh, qt) for hh in range(nheads) for qt in range(16)]
            prev = None
            for (hh, qt) in units:
                front(uidx, b, hh, qt)
                if prev is not None:
                    back(*prev)
                prev = (uidx, b, hh, qt)
                uidx += 1
            back(*prev)
            P.wait("sync", eav, eav.n)
            for sh in range(2):
                shard = b * 2 + sh
                if kind == "a":
                    for hh in range(4):
                        P.dma("sync", a2a_in[shard * 512 + hh * 128:shard * 512 + (hh + 1) * 128, :],
                              bufs["av"][:, hh * 2048 + sh * 1024: hh * 2048 + (sh + 1) * 1024], sSt)
                else:
                    for r2 in range(4):
                        P.dma("sync", a2a_in[shard * 512 + r2 * 128:shard * 512 + (r2 + 1) * 128, :],
                              bufs["av"][:, r2 * 2048 + sh * 1024: r2 * 2048 + (sh + 1) * 1024], sSt)
        P.barrier("attn_" + kind)

    def wo_phase(wname, xold_dram, xnew_dram, name):
        with ExitStack() as ph:
            afm = ph.enter_context(nc.sbuf_tensor("wo_afm" + name, [128, NEC * T_OWN], BF16))
            wsl = [ph.enter_context(nc.sbuf_tensor(f"wo{name}_w{i}", [128, NEC * 512], BF16)) for i in range(2)]
            rin = [ph.enter_context(nc.sbuf_tensor(f"wo{name}_ri{i}", [128, 512], F32)) for i in range(2)]
            rout = [ph.enter_context(nc.sbuf_tensor(f"wo{name}_ro{i}", [128, 512], F32)) for i in range(2)]
            addc = P.sem("woadd_" + name)
            wait_stores("scalar")
            sScp = [P.sem(f"wosc{name}{i}") for i in range(2)]
            sStp = [P.sem(f"wostp{name}{i}") for i in range(2)]
            addc0 = addc.n
            pe0 = sPE.n
            for ec in range(NEC):
                P.dma("sync", afm[:, ec * T_OWN:(ec + 1) * T_OWN], a2a_out[ec * 128:(ec + 1) * 128, :], sX)
            for ot in range(8):
                wbuf = wsl[ot % 2]
                if ot >= 2:
                    P.wait("gpsimd", addc, addc0 + (ot - 1) * 8)
                for ec in range(NEC):
                    P.dma("gpsimd", wbuf[:, ec * 512:(ec + 1) * 512],
                          d[wname][ec * 128:(ec + 1) * 128, ot * 512:(ot + 1) * 512], sWp[ot % 2])
                for tt in range(NTT_OWN):
                    kg = ot * NTT_OWN + tt
                    bank = pb[kg % 3]
                    if kg >= 2:
                        P.wait("scalar", addc, addc0 + kg - 1)
                    P.dma("scalar", rin[kg % 2][:], xold_dram[tt * 128:(tt + 1) * 128, ot * 512:(ot + 1) * 512], sScp[kg % 2])
                    if tt == 0:
                        P.wait("tensor", sWp[ot % 2], None)
                        if ot == 0:
                            P.wait("tensor", sX, None)
                    if kg >= 3:
                        P.wait("tensor", addc, addc0 + kg - 2)
                    for ec in range(NEC):
                        P.op("tensor", lambda e, bk=bank, ec2=ec, t2=tt, wb=wbuf: e.matmul(
                            bk[:], afm[:, ec2 * T_OWN + t2 * 128: ec2 * T_OWN + (t2 + 1) * 128],
                            wb[:, ec2 * 512:(ec2 + 1) * 512],
                            start=(ec2 == 0), stop=(ec2 == NEC - 1)), inc=(sPE if ec == NEC - 1 else None))
                    P.wait("vector", sScp[kg % 2], None)
                    P.wait("vector", sPE, pe0 + kg + 1)
                    if kg >= 2:
                        P.wait("vector", sStp[kg % 2], None)
                    P.op("vector", lambda e, bk=bank, k2=kg: e.tensor_tensor(
                        rout[k2 % 2][:], bk[:], rin[k2 % 2][:], ALU.add), inc=addc)
                    P.wait("sync", addc, addc0 + kg + 1)
                    P.dma("sync", xnew_dram[tt * 128:(tt + 1) * 128, ot * 512:(ot + 1) * 512], rout[kg % 2][:], sStp[kg % 2])
            P.barrier("wo_" + name)

    def ffn_phase(xold_dram, xnew_dram, name, afm):
        with ExitStack() as ph:
            wsl = [ph.enter_context(nc.sbuf_tensor(f"ff{name}_w{i}", [128, NEC * 512], BF16)) for i in range(2)]
            axT = ph.enter_context(nc.sbuf_tensor("ff_axT" + name, [64, NTT_OWN * 128], BF16))
            ax_tok = ph.enter_context(nc.sbuf_tensor("ff_ax" + name, [128, 64], BF16))
            rin = [ph.enter_context(nc.sbuf_tensor(f"ff{name}_ri{i}", [128, 512], F32)) for i in range(2)]
            rout = [ph.enter_context(nc.sbuf_tensor(f"ff{name}_ro{i}", [128, 512], F32)) for i in range(2)]
            axp = ph.enter_context(nc.psum_tensor("ff_axp" + name, [128, 128], BF16))
            ffa = P.sem("ffa_" + name)
            addc = P.sem("ffadd_" + name)
            wait_stores("scalar")
            sScp = [P.sem(f"ffsc{name}{i}") for i in range(2)]
            sStp = [P.sem(f"ffstp{name}{i}") for i in range(2)]
            addc0 = addc.n
            pe0 = sPE.n
            for tt in range(NTT_OWN):
                bank = pb[4]
                if _SER_AX:
                    P.barrier("serax")
                if tt >= 1:
                    P.wait("tensor", ffa, 2 * tt)
                for ec in range(NEC):
                    P.op("tensor", lambda e, ec2=ec, t2=tt, bk=bank: e.matmul(
                        bk[:, 0:64], afm[:, ec2 * T_OWN + t2 * 128: ec2 * T_OWN + (t2 + 1) * 128],
                        abT_sb[:, ec2 * 64:(ec2 + 1) * 64],
                        start=(ec2 == 0), stop=(ec2 == NEC - 1)), inc=(sPE if ec == NEC - 1 else None))
                P.wait("vector", sPE, None)
                P.op("vector", lambda e, bk=bank: e.tensor_copy(ax_tok[:, 0:64], bk[:, 0:64]), inc=ffa)
                P.wait("tensor", ffa, None)
                P.op("tensor", lambda e: e.transpose(axp[0:64, 0:128], ax_tok[:, 0:64], ident_bf[:]), inc=sPE)
                P.wait("vector", sPE, None)
                P.op("vector", lambda e, t2=tt: e.tensor_copy(
                    axT[0:64, t2 * 128:(t2 + 1) * 128], axp[0:64, 0:128]), inc=ffa)
            P.barrier("ffn_ax_" + name)
            pe0 = sPE.n
            for ot in range(8):
                wbuf = wsl[ot % 2]
                if ot >= 2:
                    P.wait("gpsimd", addc, addc0 + (ot - 1) * 8)
                for ec in range(NEC):
                    P.dma("gpsimd", wbuf[:, ec * 512:(ec + 1) * 512],
                          d["fker"][ec * 128:(ec + 1) * 128, ot * 512:(ot + 1) * 512], sWp[ot % 2])
                for tt in range(NTT_OWN):
                    kg = ot * NTT_OWN + tt
                    bank = pb[kg % 3]
                    if kg >= 2:
                        P.wait("scalar", addc, addc0 + kg - 1)
                    P.dma("scalar", rin[kg % 2][:], xold_dram[tt * 128:(tt + 1) * 128, ot * 512:(ot + 1) * 512], sScp[kg % 2])
                    if tt == 0:
                        P.wait("tensor", sWp[ot % 2], None)
                    if kg >= 3:
                        P.wait("tensor", addc, addc0 + kg - 2)
                    for ec in range(NEC):
                        P.op("tensor", lambda e, bk=bank, ec2=ec, t2=tt, wb=wbuf: e.matmul(
                            bk[:], afm[:, ec2 * T_OWN + t2 * 128: ec2 * T_OWN + (t2 + 1) * 128],
                            wb[:, ec2 * 512:(ec2 + 1) * 512],
                            start=(ec2 == 0), stop=False))
                    P.op("tensor", lambda e, bk=bank, o2=ot: e.matmul(
                        bk[:], ones_bf[:, :], fbias_sb[:, o2 * 512:(o2 + 1) * 512],
                        start=False, stop=False))
                    P.op("tensor", lambda e, bk=bank, o2=ot, t2=tt: e.matmul(
                        bk[:], axT[0:64, t2 * 128:(t2 + 1) * 128], bbT_sb[0:64, o2 * 512:(o2 + 1) * 512],
                        start=False, stop=True), inc=sPE)
                    P.wait("vector", sScp[kg % 2], None)
                    P.wait("vector", sPE, pe0 + kg + 1)
                    if kg >= 2:
                        P.wait("vector", sStp[kg % 2], None)
                    P.op("vector", lambda e, bk=bank, k2=kg: e.tensor_tensor(
                        rout[k2 % 2][:], bk[:], rin[k2 % 2][:], ALU.add), inc=addc)
                    P.wait("sync", addc, addc0 + kg + 1)
                    P.dma("sync", xnew_dram[tt * 128:(tt + 1) * 128, ot * 512:(ot + 1) * 512], rout[kg % 2][:], sStp[kg % 2])
            P.barrier("ffn_" + name)

    def proj_b():
        with ExitStack() as ph:
            wsb = ph.enter_context(nc.sbuf_tensor("pb_w", [128, NEC * 1536], BF16))
            xsl = [ph.enter_context(nc.sbuf_tensor(f"pb_x{i}", [128, NEC * 256], BF16)) for i in range(2)]
            sc = dict(
                qrot=[ph.enter_context(nc.sbuf_tensor(f"pb_qr{i}", [128, 512], BF16)) for i in range(2)],
                krot=[ph.enter_context(nc.sbuf_tensor(f"pb_kr{i}", [128, 512], BF16)) for i in range(2)],
                rtt=ph.enter_context(nc.sbuf_tensor("pb_rt", [128, 192], F32)),
                v=[ph.enter_context(nc.sbuf_tensor(f"pb_v{i}", [128, 512], BF16)) for i in range(2)],
                qT=ph.enter_context(nc.sbuf_tensor("pb_qT", [128, 512], BF16)),
                kT=ph.enter_context(nc.sbuf_tensor("pb_kT", [128, 512], BF16)),
            )
            pt8 = [ph.enter_context(nc.psum_tensor(f"pb_pt{i}", [128, 1024], BF16)) for i in range(2)]
            for ec in range(NEC):
                P.dma("gpsimd", wsb[:, ec * 1536:(ec + 1) * 1536], d["wqkvb"][ec * 128:(ec + 1) * 128, :], sW)
            evt = P.sem("pb_evt")
            tra = P.sem("pb_tra")
            cpy = P.sem("pb_cpy")
            stp = [P.sem(f"pb_st{i}") for i in range(2)]
            pend = []

            def front(tt):
                qs, tt2 = tt // 2, tt % 2
                buf = xsl[qs % 2]
                pc = tt % 16
                bq = pb[0 + tt % 2]
                bk_ = pb[2 + tt % 2]
                bv = pb[4 + tt % 2]
                if tt2 == 0:
                    r, quar = qs // 4, qs % 4
                    if qs >= 2:
                        P.wait("sync", sPE, None)
                    for ecg in range(4):
                        P.dma("sync",
                              buf[:].rearrange("p (c t) -> p c t", t=256)[:, ecg * 8:(ecg + 1) * 8, :],
                              ag_out[r * E + ecg * 1024:r * E + (ecg + 1) * 1024, quar * 256:(quar + 1) * 256].rearrange("(c p) t -> p c t", p=128),
                              sXp[qs % 2])
                    P.wait("tensor", sXp[qs % 2], None)
                    if qs == 0:
                        P.wait("tensor", sW, None)
                if tt >= 2:
                    P.wait("tensor", evt, tt - 1)
                for ec in range(NEC):
                    xt = buf[:, ec * 256 + tt2 * 128: ec * 256 + (tt2 + 1) * 128]
                    P.op("tensor", lambda e, bk2=bq, x=xt, ec2=ec: e.matmul(
                        bk2[:], x, wsb[:, ec2 * 1536:ec2 * 1536 + 512],
                        start=(ec2 == 0), stop=(ec2 == NEC - 1)))
                for ec in range(NEC):
                    xt = buf[:, ec * 256 + tt2 * 128: ec * 256 + (tt2 + 1) * 128]
                    P.op("tensor", lambda e, bk2=bk_, x=xt, ec2=ec: e.matmul(
                        bk2[:], x, wsb[:, ec2 * 1536 + 512:ec2 * 1536 + 1024],
                        start=(ec2 == 0), stop=(ec2 == NEC - 1)))
                for ec in range(NEC):
                    xt = buf[:, ec * 256 + tt2 * 128: ec * 256 + (tt2 + 1) * 128]
                    P.op("tensor", lambda e, bk2=bv, x=xt, ec2=ec: e.matmul(
                        bk2[:], x, wsb[:, ec2 * 1536 + 1024:(ec2 + 1) * 1536],
                        start=(ec2 == 0), stop=(ec2 == NEC - 1)), inc=(sPE if ec == NEC - 1 else None))
                P.wait("vector", sPE, None)
                if tt >= 2:
                    P.wait("vector", tra, tt - 1)
                rot_b(sc, sc["qrot"][tt % 2], bq, pc)
                rot_b(sc, sc["krot"][tt % 2], bk_, pc)
                P.op("vector", lambda e, bk2=bv, t2=tt: e.tensor_copy(sc["v"][t2 % 2][:], bk2[:]), inc=evt)

            def back(tt):
                bank = pt8[tt % 2]
                P.wait("tensor", evt, tt + 1)
                if tt >= 2:
                    P.wait("tensor", cpy, tt - 1)
                for dc in range(4):
                    P.op("tensor", lambda e, bk2=bank, d2=dc, t2=tt: e.transpose(
                        bk2[:, d2 * 128:(d2 + 1) * 128], sc["qrot"][t2 % 2][:, d2 * 128:(d2 + 1) * 128], ident_bf[:]))
                for dc in range(4):
                    P.op("tensor", lambda e, bk2=bank, d2=dc, t2=tt: e.transpose(
                        bk2[:, 512 + d2 * 128:512 + (d2 + 1) * 128], sc["krot"][t2 % 2][:, d2 * 128:(d2 + 1) * 128], ident_bf[:]),
                        inc=(tra if dc == 3 else None))
                P.wait("scalar", tra, tra.n)
                P.wait("scalar", stp[0], None)
                P.wait("scalar", stp[1], None)
                P.op("scalar", lambda e, bk2=bank: e.copy(sc["qT"][:], bk2[:, 0:512]))
                P.op("scalar", lambda e, bk2=bank: e.copy(sc["kT"][:], bk2[:, 512:1024]), inc=cpy)
                P.wait("sync", cpy, cpy.n)
                P.wait("sync", evt, tt + 1)
                t0 = tt * 128
                for dc in range(4):
                    P.dma("sync", qb_d[dc * 128:(dc + 1) * 128, t0:t0 + 128], sc["qT"][:, dc * 128:(dc + 1) * 128], stp[tt % 2])
                    P.dma("sync", kb_d[dc * 128:(dc + 1) * 128, t0:t0 + 128], sc["kT"][:, dc * 128:(dc + 1) * 128], stp[tt % 2])
                P.dma("sync", vb_d[tt, :, :], sc["v"][tt % 2][:], stp[tt % 2])

            prev = None
            for tt in range(64):
                front(tt)
                if prev is not None:
                    back(prev)
                prev = tt
            back(prev)
            P.barrier("proj_b")

    def rot_b(sc, dst, srcb, pc):
        cB = cosB[:, pc * ROT:(pc + 1) * ROT]
        sB = sinB[:, pc * ROT:(pc + 1) * ROT]
        rtt = sc["rtt"]
        for hh in range(2):
            base = hh * 256
            P.op("vector", lambda e, b2=base: e.tensor_copy(rtt[:, 0:63:2], srcb[:, b2 + 1:b2 + 64:2]))
            P.op("vector", lambda e, b2=base: e.tensor_copy(rtt[:, 1:64:2], srcb[:, b2:b2 + 63:2]))
            P.op("vector", lambda e, b2=base, c=cB: e.tensor_tensor(rtt[:, 64:128], srcb[:, b2:b2 + 64], c, ALU.mult))
            P.op("vector", lambda e, s2=sB: e.tensor_tensor(rtt[:, 128:192], rtt[:, 0:64], s2, ALU.mult))
            P.op("vector", lambda e, b2=base: e.tensor_add(dst[:, b2:b2 + 64], rtt[:, 64:128], rtt[:, 128:192]))
            P.op("vector", lambda e, b2=base: e.tensor_copy(dst[:, b2 + 64:b2 + 256], srcb[:, b2 + 64:b2 + 256]))

    def attn_b():
        with ExitStack() as ph:
            bufs = dict(
                q=ph.enter_context(nc.sbuf_tensor("ab_q", [128, 4 * S], BF16)),
                k=ph.enter_context(nc.sbuf_tensor("ab_k", [128, 4 * S], BF16)),
                v=ph.enter_context(nc.sbuf_tensor("ab_v", [128, 4 * S], BF16)),
                ms=ph.enter_context(nc.sbuf_tensor("ab_ms", [128, S], F32)),
                p=ph.enter_context(nc.sbuf_tensor("ab_p", [128, S], BF16)),
                pt=ph.enter_context(nc.sbuf_tensor("ab_pt", [128, 512], BF16)),
                sl=ph.enter_context(nc.sbuf_tensor("ab_sl", [128, 16], F32)),
                rm=ph.enter_context(nc.sbuf_tensor("ab_rm", [128, 1], F32)),
                nrm=ph.enter_context(nc.sbuf_tensor("ab_nrm", [128, 1], F32)),
                ls=ph.enter_context(nc.sbuf_tensor("ab_ls", [128, 1], F32)),
                lr=ph.enter_context(nc.sbuf_tensor("ab_lr", [128, 1], F32)),
                dg=ph.enter_context(nc.sbuf_tensor("ab_dg", [128, 128], BF16)),
                av=ph.enter_context(nc.sbuf_tensor("ab_av", [128, 8192], BF16)),
            )
            bufs["ptb"] = [ph.enter_context(nc.psum_tensor(f"ab_ptb{i}", [128, 512], F32)) for i in range(2)]
            un = P.sem("ab_un")
            for b in range(B):
                t0 = b * S
                P.wait("sync", un, None)
                P.wait("sync", sSt, None)
                P.dma("sync", bufs["q"][:].rearrange("p (dc t) -> p dc t", t=S), qb_d[:, t0:t0 + S].rearrange("(dc p) t -> p dc t", p=128), sX)
                P.dma("sync", bufs["k"][:].rearrange("p (dc t) -> p dc t", t=S), kb_d[:, t0:t0 + S].rearrange("(dc p) t -> p dc t", p=128), sX)
                P.dma("sync", bufs["v"][:].rearrange("p (tt v) -> p tt v", v=512), vb_d[b * 16:(b + 1) * 16, :, :].rearrange("tt p v -> p tt v"), sX)
                for eng in ("tensor", "vector", "scalar"):
                    P.wait(eng, sX, None)
                P.wait("vector", sSt, None)
                for h in range(2):
                    for qt in range(16):
                        unit_b(b, h, qt, bufs, un)
                P.wait("sync", sDV, None)
                for sh in range(2):
                    shard = b * 2 + sh
                    for r2 in range(4):
                        P.dma("sync", a2a_in[shard * 512 + r2 * 128:shard * 512 + (r2 + 1) * 128, :],
                              bufs["av"][:, r2 * 2048 + sh * 1024: r2 * 2048 + (sh + 1) * 1024], sSt)
            P.barrier("attn_b")

    def unit_b(b, h, qt, bufs, un):
        nk = qt + 1
        ng = (nk + 3) // 4
        ms_sb, p_sb, pt_sb, slots = bufs["ms"], bufs["p"], bufs["pt"], bufs["sl"]
        sc_bank = pb[2 + (qt % 2)]
        av_bank = pb[4 + (qt % 2)]
        ptb = bufs["ptb"][qt % 2]
        P.wait("tensor", sDV, None); P.wait("tensor", sAC, None)
        P.wait("vector", sPE, None); P.wait("vector", sAC, None)
        P.wait("scalar", sPE, None); P.wait("scalar", sDV, None)
        for g in range(ng):
            k0 = g * 512
            w = min(512, nk * 128 - k0)
            if g >= 1:
                P.wait("tensor", sDV, None)
            for dc in range(2):
                dcg = h * 2 + dc
                P.op("tensor", lambda e, k2=k0, w2=w, d2=dcg, d3=dc, q2=qt: e.matmul(
                    sc_bank[:, 0:w2], bufs["q"][:, d2 * S + q2 * 128:d2 * S + (q2 + 1) * 128],
                    bufs["k"][:, d2 * S + k2:d2 * S + k2 + w2],
                    start=(d3 == 0), stop=(d3 == 1)), inc=(sPE if dc == 1 else None))
            P.wait("vector", sPE, None)
            P.op("vector", lambda e, k2=k0, w2=w: e.tensor_scalar(
                out=ms_sb[:, k2:k2 + w2], in0=sc_bank[:, 0:w2], scalar1=1.0 / 16.0, scalar2=None,
                op0=ALU.mult), inc=sDV)
            if g == ng - 1:
                kd = nk * 128 - 128
                P.op("vector", lambda e, k2=kd: e.tensor_tensor(
                    ms_sb[:, k2:k2 + 128], ms_sb[:, k2:k2 + 128], causal_add[:], ALU.add), inc=sDV)
            P.op("vector", lambda e, k2=k0, w2=w, g2=g: e.reduce_max(
                slots[:, g2:g2 + 1], ms_sb[:, k2:k2 + w2], axis=AX.X), inc=sDV)
        P.wait("vector", sDV, None)
        P.op("vector", lambda e, g2=ng: e.reduce_max(bufs["rm"][:], slots[:, 0:g2], axis=AX.X), inc=sDV)
        P.wait("vector", sDV, None)
        P.op("vector", lambda e: e.tensor_scalar(
            out=bufs["nrm"][:], in0=bufs["rm"][:], scalar1=-1.0, scalar2=None, op0=ALU.mult), inc=sDV)
        P.wait("scalar", sDV, None)
        for g in range(ng):
            k0 = g * 512
            w = min(512, nk * 128 - k0)
            P.op("scalar", lambda e, k2=k0, w2=w, g2=g: e.activation(
                p_sb[:, k2:k2 + w2], ms_sb[:, k2:k2 + w2], AF.Exp, bias=bufs["nrm"][:],
                accum_out=slots[:, 8 + g2:9 + g2]), inc=sAC)
        P.wait("vector", sAC, None)
        P.op("vector", lambda e, g2=ng: e.reduce_sum(bufs["ls"][:], slots[:, 8:8 + g2], axis=AX.X), inc=sDV)
        P.wait("vector", sDV, None)
        P.op("vector", lambda e: e.reciprocal(bufs["lr"][:], bufs["ls"][:]), inc=sDV)
        P.wait("vector", sDV, None)
        P.op("vector", lambda e: e.tensor_scalar(
            out=bufs["dg"][:], in0=ident_bf[:], scalar1=bufs["lr"][:], scalar2=None, op0=ALU.mult), inc=sDV)
        P.wait("tensor", sDV, None)
        for kc in range(nk):
            ps = kc % 4
            P.op("tensor", lambda e, k2=kc, p2=ps: e.matmul(
                ptb[:, p2 * 128:(p2 + 1) * 128], p_sb[:, k2 * 128:(k2 + 1) * 128], bufs["dg"][:],
                start=True, stop=True), inc=(sPE if (ps == 3 or kc == nk - 1) else None))
            if ps == 3 or kc == nk - 1:
                P.wait("scalar", sPE, None)
                wdt = (ps + 1) * 128
                P.op("scalar", lambda e, w2=wdt: e.copy(pt_sb[:, 0:w2], ptb[:, 0:w2]), inc=sAC)
                P.wait("tensor", sAC, None)
                base = kc - ps
                for j in range(ps + 1):
                    kcj = base + j
                    for vc in range(2):
                        vcg = h * 2 + vc
                        P.op("tensor", lambda e, k2=kcj, j2=j, v2=vcg, v3=vc: e.matmul(
                            av_bank[:, v3 * 128:(v3 + 1) * 128],
                            bufs["v"][:, k2 * 512 + v2 * 128:k2 * 512 + (v2 + 1) * 128],
                            pt_sb[:, j2 * 128:(j2 + 1) * 128],
                            start=(k2 == 0 and v3 == 0), stop=(k2 == nk - 1)), inc=(sPE if (kcj == nk - 1 and vc == 1) else None))
        P.wait("vector", sPE, None)
        for vc in range(2):
            r2 = h * 2 + vc
            P.op("vector", lambda e, r3=r2, q2=qt, v3=vc: e.tensor_copy(
                bufs["av"][:, r3 * 2048 + q2 * 128: r3 * 2048 + (q2 + 1) * 128],
                av_bank[:, v3 * 128:(v3 + 1) * 128]), inc=(sDV if vc == 1 else None))
        P.op("vector", lambda e: e.nop(), inc=un)

    def _ffn_block(xold, xnew, name):
        with ExitStack() as es:
            afm = es.enter_context(nc.sbuf_tensor("afm" + name, [128, NEC * T_OWN], BF16))
            ln_phase(xold, "s" + name, to_agin=False, afm=afm)
            ffn_phase(xold, xnew, name, afm)

    seq = [
        ("ln1", lambda: ln_phase(d["x_own"], "s1", to_agin=True)),
        ("ag1", allgather),
        ("proj_a", proj_a),
        ("attn_a", attn_a),
        ("a2a1", alltoall),
        ("wo_a", lambda: wo_phase("woaT", d["x_own"], xr[0], "a")),
        ("ffn1", lambda: _ffn_block(xr[0], xr[1], "f1")),
        ("ln3", lambda: ln_phase(xr[1], "s3", to_agin=True)),
        ("ag2", allgather),
        ("proj_b", proj_b),
        ("attn_b", attn_b),
        ("a2a2", alltoall),
        ("wo_b", lambda: wo_phase("wob", xr[1], xr[2], "b")),
        ("ffn2", lambda: _ffn_block(xr[2], y, "f2")),
    ]
    limit = len(seq) if _PHASE_LIMIT is None else _PHASE_LIMIT
    for nm, th in seq[:limit]:
        th()

    dumpable = dict(wdbg=wdbg, xlndbg=xlndbg, ag_in=ag_in, ag_out=ag_out, a2a_in=a2a_in, a2a_out=a2a_out,
                    xr0=xr[0], xr1=xr[1], xr2=xr[2], qa_d=qa_d, qha_d=qha_d,
                    ka_d=ka_d, kha_d=kha_d, va_d=va_d, qb_d=qb_d, kb_d=kb_d,
                    vb_d=vb_d, beff_d=beff_d)
    for nm in _DEBUG_DUMPS:
        ap = dumpable[nm]
        dt_ = ap.tensor.dtype
        od = nc.dram_tensor("dump_" + nm, list(ap.tensor.shape), dt_, kind="ExternalOutput").ap()
        P.dma("sync", od[...], ap[...], sSt)

    P.wait("sync", sSt, None)
    for s in list(P.sems):
        if s.startswith(("wostp", "ffstp", "pb_st")):
            P.wait("sync", P.sems[s], None)
    P.run()
    stack.close()
    return nc


def make_core_inputs(c, inputs, shared):
    ii = {}
    f32 = np.float32
    g = inputs
    x = np.asarray(g["L0_attn_a_hidden_states"], f32).reshape(T_ALL, E)
    ii["x_own"] = np.ascontiguousarray(x[c * T_OWN:(c + 1) * T_OWN])
    wq = np.asarray(g["L0_attn_a_Wq"], f32)
    wk = np.asarray(g["L0_attn_a_Wk"], f32)
    wv = np.asarray(g["L0_attn_a_Wv"], f32)
    ii["wqkva"] = np.ascontiguousarray(np.concatenate(
        [wq[c * 512:(c + 1) * 512, :].T, wk[c * 128:(c + 1) * 128, :].T, wv[c * 128:(c + 1) * 128, :].T], axis=1))
    ii["hp1"] = np.ascontiguousarray(np.asarray(g["L0_attn_a_hash_proj1"], f32)[4 * c:4 * c + 4].transpose(1, 0, 2).reshape(128, 512))
    ii["hb1"] = np.ascontiguousarray(np.asarray(g["L0_attn_a_hash_bias1"], f32)[4 * c:4 * c + 4].reshape(4, 128).T)
    ii["hp2"] = np.ascontiguousarray(np.asarray(g["L0_attn_a_hash_proj2"], f32)[4 * c:4 * c + 4].transpose(1, 0, 2).reshape(128, 512))
    ii["hb2"] = np.ascontiguousarray(np.asarray(g["L0_attn_a_hash_bias2"], f32)[4 * c:4 * c + 4].reshape(4, 128).T)
    ii["woaT"] = shared["woaT"]
    wqb = np.asarray(g["L0_attn_b_Wq"], f32)[:, c * 512:(c + 1) * 512]
    wkb = np.asarray(g["L0_attn_b_Wk"], f32)[:, c * 512:(c + 1) * 512]
    wvb = np.asarray(g["L0_attn_b_Wv"], f32)[:, c * 512:(c + 1) * 512]
    ii["wqkvb"] = np.ascontiguousarray(np.concatenate([wqb, wkb, wvb], axis=1))
    ii["wob"] = shared["wob"]
    ii["fker"] = shared["fker"]
    ii["fbias"] = shared["fbias"]
    ii["gate"] = np.ascontiguousarray(np.asarray(g["L0_ffn_gate"], f32)[c].reshape(1, 16))
    ii["gAT"] = shared["gAT"]
    ii["gBT"] = shared["gBT"]
    ii["asT"] = shared["asT"]
    ii["bsT"] = shared["bsT"]
    for k, v in CST.items():
        ii[k] = v
    return ii


def kernel(**inputs):
    f32 = np.float32
    la = np.asarray(inputs["L0_ffn_lora_A"], f32)
    lb = np.asarray(inputs["L0_ffn_lora_B"], f32)
    gAT = la.transpose(0, 2, 1).reshape(16, NEC, 128, RANK).transpose(2, 0, 1, 3).reshape(128, 16 * NEC * RANK)
    gBT = lb.transpose(0, 2, 1).reshape(16, 128, 1024).transpose(1, 0, 2).reshape(128, 16 * 1024)
    asv = np.asarray(inputs["L0_ffn_lora_A_shared"], f32)
    asT = asv.T.reshape(NEC, 128, RANK).transpose(1, 0, 2).reshape(128, NEC * RANK)
    shared = {
        "woaT": np.ascontiguousarray(np.asarray(inputs["L0_attn_a_Wo"], f32).T),
        "wob": np.ascontiguousarray(np.asarray(inputs["L0_attn_b_Wo"], f32)),
        "fker": np.ascontiguousarray(np.asarray(inputs["L0_ffn_kernel"], f32)),
        "fbias": np.ascontiguousarray(np.asarray(inputs["L0_ffn_bias"], f32).reshape(1, E)),
        "gAT": np.ascontiguousarray(gAT),
        "gBT": np.ascontiguousarray(gBT),
        "asT": np.ascontiguousarray(asT),
        "bsT": np.ascontiguousarray(np.asarray(inputs["L0_ffn_lora_B_shared"], f32).T),
    }
    nc = build_nc()
    in_maps = [make_core_inputs(c, inputs, shared) for c in range(N_CORES)]
    res = run_bass_kernel_spmd(nc, in_maps, core_ids=list(range(N_CORES)))
    out = np.concatenate([res.results[c]["y"] for c in range(N_CORES)], axis=0)
    return out.reshape(B, S, E)
